# version 14
# baseline (speedup 1.0000x reference)
import numpy as np
import ml_dtypes
import contextlib
import concourse.bass as bass
import concourse.mybir as mybir
from concourse.bass import IndirectOffsetOnAxis
from concourse.bass_utils import run_bass_kernel_spmd

F32 = mybir.dt.float32
BF16 = mybir.dt.bfloat16
I32 = mybir.dt.int32
AF = mybir.ActivationFunctionType
ALU = mybir.AluOpType
AX = mybir.AxisListType

NC = 8
B = 2
N = 8192
D = 1024
NL = N // NC
TL = B * NL
NT = TL // 128
DI = 4096
KC = D // 128
ICN = DI // 128

_ENGINES = ("sync", "scalar", "vector", "tensor", "gpsimd")
_DMA_SEM = {"sync": "dS", "scalar": "dA", "gpsimd": "dG"}


class Sched:
    def __init__(self):
        self.count = {}
        self.ops = {e: [] for e in _ENGINES}

    def c(self, sem):
        return self.count.get(sem, 0)

    def emit(self, engine, fn, waits=(), inc=None, inc_amt=1):
        waits = [(s, v) for (s, v) in waits if v > 0]
        for s, v in waits:
            assert v <= self.c(s), f"wait {s}>={v} before inc (have {self.c(s)})"
        self.ops[engine].append((tuple(waits), fn, inc, inc_amt))
        if inc is not None:
            self.count[inc] = self.c(inc) + inc_amt
            return self.count[inc]
        return None

    def dma(self, queue, out, in_, waits=()):
        return self.emit(queue, lambda e, o=out, i=in_: e.dma_start(out=o, in_=i),
                         waits=waits, inc=_DMA_SEM[queue], inc_amt=16)

    def dmat(self, queue, out, in_, waits=()):
        return self.emit(queue, lambda e, o=out, i=in_: e.dma_start_transpose(o, i),
                         waits=waits, inc=_DMA_SEM[queue], inc_amt=16)

    def build(self, block, sems):
        for eng_name in _ENGINES:
            ops = self.ops[eng_name]

            def mk(ops):
                def run(eng):
                    last = {}
                    for waits, fn, inc, amt in ops:
                        for s, v in waits:
                            if last.get(s, 0) < v:
                                eng.wait_ge(sems[s], v)
                                last[s] = v
                        ins = fn(eng)
                        if inc is not None:
                            ins.then_inc(sems[inc], amt)
                return run
            getattr(block, eng_name)(mk(ops))


def build_program(dump=()):
    nc = bass.Bass("TRN2", target_bir_lowering=False)
    dt = lambda n, s, d, k="Internal": nc.dram_tensor(n, s, d, kind=k).ap()

    x_tm = dt("x_tm", [TL, D], F32, "ExternalInput")
    x_fmT = dt("x_fmT", [D, TL], F32, "ExternalInput")
    wqkv = dt("wqkv", [D, 3 * D], F32, "ExternalInput")
    wproj = dt("wproj", [D, D], F32, "ExternalInput")
    wgT = dt("wgT", [D, DI], F32, "ExternalInput")
    wuT = dt("wuT", [D, DI], F32, "ExternalInput")
    wdT = dt("wdT", [DI, D], F32, "ExternalInput")
    ident_d = dt("ident", [128, 128], BF16, "ExternalInput")
    id3_d = dt("id3", [64, 192], F32, "ExternalInput")
    pair1_d = dt("pair1", [128, 2], BF16, "ExternalInput")
    bcmask_d = dt("bcmask", [2, 128], F32, "ExternalInput")
    onesr_d = dt("onesr", [1, 128], F32, "ExternalInput")
    o1b_d = dt("o1b", [128, 1], BF16, "ExternalInput")
    o1f_d = dt("o1f", [128, 1], F32, "ExternalInput")
    gidx_d = dt("gidx", [64, 4], I32, "ExternalInput")
    out_d = dt("out", [TL, D], F32, "ExternalOutput")

    y1_dr = dt("y1_dr", [TL, D], F32)
    y1q_dr = dt("y1q_dr", [TL, D], BF16)
    y1qfm_dr = dt("y1qfm_dr", [D, TL], BF16)
    hq_dr = dt("hq_dr", [TL, DI], BF16)
    wgq_dr = dt("wgq_dr", [D, DI], BF16)
    wuq_dr = dt("wuq_dr", [D, DI], BF16)
    wdq_dr = dt("wdq_dr", [DI, D], BF16)
    k2_dr = dt("k2_dr", [2048, 64], BF16)
    ag_in = dt("ag_in", [16, D], BF16)
    ag_out = dt("ag_out", [128, D], BF16)
    ar_in = dt("ar_in", [2048, 132], F32)
    ar_out = dt("ar_out", [2048, 132], F32)
    w2_in = dt("w2_in", [256, 64], BF16)
    w2_out = dt("w2_out", [2048, 64], BF16)

    S = Sched()
    semnames = ["dS", "dA", "dG", "mm", "ev", "ac", "gp", "cc1", "cc2", "cc3"]

    with contextlib.ExitStack() as st:
        ent = st.enter_context
        sems = {n: ent(nc.semaphore(n)) for n in semnames}
        sb = lambda n, s, d: ent(nc.sbuf_tensor(n, s, d))

        S1 = sb("S1", [128, 16384], BF16)
        S2 = sb("S2", [128, 16384], BF16)
        S3 = sb("S3", [128, 16384], BF16)
        S4 = sb("S4", [128, 16384], BF16)
        scC = sb("scC", [128, 8192], BF16)
        scB = sb("scB", [128, 2048], BF16)
        scD = sb("scD", [128, 2048], BF16)
        scE = sb("scE", [128, 2048], BF16)
        CF1 = sb("CF1", [128, 4096], F32)
        CFD = sb("CFD", [128, 2048], F32)
        EI = sb("EI", [128, 512], I32)
        EI2 = sb("EI2", [128, 512], I32)
        bcs = sb("bcs", [128, 512], F32)
        zc = CF1[0:2, 0:512]
        rz = CF1[0:2, 512:1024]
        c_id = sb("c_id", [128, 128], BF16)
        c_id3 = sb("c_id3", [64, 192], F32)
        c_p1 = sb("c_p1", [128, 2], BF16)
        c_bc = sb("c_bc", [2, 128], F32)
        c_or = sb("c_or", [1, 128], F32)
        c_o1b = sb("c_o1b", [128, 1], BF16)
        c_o1f = sb("c_o1f", [128, 1], F32)
        gidx = sb("gidx_s", [64, 4], I32)
        xlpf = sb("xlpf", [128, 128], F32)
        xlp = sb("xlp", [128, 128], BF16)
        xlt = sb("xlt", [128, 128], BF16)
        xl_g = sb("xl_g", [128, 1024], BF16)
        xl_fm = sb("xl_fm", [128, 1024], BF16)
        ql_fm = sb("ql_fm", [128, 1024], BF16)
        kl_fm = sb("kl_fm", [128, 1024], BF16)
        bdq = [sb(f"bdq{i}", [128, 128], BF16) for i in range(2)]
        bdk = [sb(f"bdk{i}", [128, 128], BF16) for i in range(2)]
        bdw = [sb(f"bdw{i}", [128, 128], BF16) for i in range(2)]
        bsm = [sb(f"bsm{i}", [128, 128], BF16) for i in range(2)]
        k3v = [sb(f"k3v{i}", [128, 128], BF16) for i in range(2)]
        azb = [sb(f"azb{i}", [128, 132], F32) for i in range(2)]
        k2e = sb("k2e", [128, 64], F32)
        k2s = sb("k2s", [128, 64], BF16)
        nw = [{k: sb(f"nw_{k}{j}", [64, 64], BF16) for k in
               ("k2b", "k2t", "vk", "vt", "ptb", "tt")} for j in range(4)]
        dnc = [sb(f"dnc{j}", [64, 1], F32) for j in range(4)]
        colnames = ["sum", "negm", "ssq", "ssqb", "sd", "rstd", "rmax", "sa",
                    "dqg", "dqu", "dqd", "s2", "tmp", "tmp2",
                    "wsgs", "wsgd", "wsus", "wsud", "wsds", "wsdd",
                    "accg", "accu", "accd", "rowr"]
        cols = {n: sb("col_" + n, [128, 1], F32) for n in colnames}
        rsa_all = sb("rsa_all", [128, 16], F32)
        rsa2_all = sb("rsa2_all", [128, 16], F32)
        rsa3_all = sb("rsa3_all", [128, 16], F32)
        rs2d_all = sb("rs2d_all", [128, 16], F32)
        rs2_all = sb("rs2_all", [128, 16], F32)
        t11 = sb("t11", [1, 16], F32)

        pb = [ent(nc.psum_tensor(f"pb{i}", [128, 512], F32)) for i in range(6)]
        pA = ent(nc.psum_tensor("pA", [128, 132], F32))
        pT = ent(nc.psum_tensor("pT", [128, 128], BF16))

        blk = ent(nc.Block())

        def MM(ps_ap, lhsT, rhs, start, stop, waits=()):
            return S.emit("tensor",
                          lambda e, a=ps_ap, l=lhsT, r=rhs, s1=start, s2=stop:
                          e.matmul(a, l, r, start=s1, stop=s2),
                          waits=waits, inc="mm")

        def TRP(ps_ap, in_ap, id_ap, waits=()):
            return S.emit("tensor",
                          lambda e, a=ps_ap, i=in_ap, d=id_ap:
                          e.transpose(a, i, d), waits=waits, inc="mm")

        def EV(fn, waits=()):
            return S.emit("vector", fn, waits=waits, inc="ev")

        def ACT(fn, waits=()):
            return S.emit("scalar", fn, waits=waits, inc="ac")

        def GP(fn, waits=()):
            return S.emit("gpsimd", fn, waits=waits, inc="gp")

        def DRAIN(engine, sem):
            return S.emit(engine, lambda e: e.drain(), inc=sem)

        bank_free = {i: ("ev", 0) for i in list(range(6)) + ["pA", "pT"]}

        xfm = lambda k: S1[:, k * 2048:(k + 1) * 2048]
        Qf = lambda p: S2[:, p * 2048:(p + 1) * 2048]
        Kf = lambda p: S3[:, p * 2048:(p + 1) * 2048]
        Vt = lambda t: S4[:, t * 1024:(t + 1) * 1024]
        exp1 = lambda i: S1[:, i * 1024:(i + 1) * 1024]
        SVf = lambda p: S2[:, p * 2048:(p + 1) * 2048]
        wv16 = lambda k: S3[:, k * 1024:(k + 1) * 1024]

        S.dma("sync", c_id[:], ident_d)
        S.dma("sync", c_id3[:], id3_d)
        S.dma("sync", c_p1[:], pair1_d)
        S.dma("sync", c_bc[:], bcmask_d)
        S.dma("sync", c_or[:], onesr_d)
        S.dma("sync", c_o1b[:], o1b_d)
        S.dma("sync", c_o1f[:], o1f_d)
        d_consts = S.dma("sync", gidx[:], gidx_d)

        d_xfm = S.dma("gpsimd", S1[:].rearrange("p (c t) -> p c t", c=KC),
                     x_fmT.rearrange("(c p) t -> p c t", p=128))
        d_wv = S.dma("gpsimd", S3[:, 0:8192].rearrange("p (k q) -> p k q", k=KC),
                     wqkv[:, 2048:3072].rearrange("(k p) q -> p k q", p=128))

        for k in range(KC):
            for t in range(NT):
                ACT(lambda e, k=k, t=t: e.activation(
                    scB[:, 0:128], xfm(k)[:, t * 128:(t + 1) * 128], AF.Copy,
                    scale=1.0 / 128, accum_out=xlpf[:, k * 16 + t:k * 16 + t + 1]),
                    waits=[("dG", d_xfm)] if (k == 0 and t == 0)
                    else [("ac", S.c("ac"))])
        ac_xlp = S.c("ac")
        ev_xlp = EV(lambda e: e.tensor_copy(xlp[:], xlpf[:]),
                    waits=[("ac", ac_xlp)])
        w0, c0 = bank_free["pT"]
        m_t = TRP(pT[:], xlp[:], c_id[:],
                  waits=[("ev", ev_xlp), ("dS", d_consts), (w0, c0)])
        e_t = EV(lambda e: e.tensor_copy(xlt[:], pT[:]), waits=[("mm", m_t)])
        bank_free["pT"] = ("ev", e_t)
        d_ag = S.dma("sync", ag_in.rearrange("t (k p) -> k t p", p=128),
                     xlt[:], waits=[("ev", e_t)])
        S.emit("gpsimd",
               lambda e: e.collective_compute(
                   "AllGather", ALU.bypass, replica_groups=[list(range(NC))],
                   ins=[ag_in.opt()], outs=[ag_out.opt()]),
               waits=[("dS", d_ag)], inc="cc1")
        d_xlg = S.dma("sync", xl_g[:], ag_out, waits=[("cc1", 1)])

        for k in range(KC):
            w0, c0 = bank_free["pT"]
            m1 = TRP(pT[:], xl_g[:, k * 128:(k + 1) * 128], c_id[:],
                     waits=[("dS", d_xlg), (w0, c0)])
            e1 = EV(lambda e, k=k: e.tensor_copy(
                xl_fm[:, k * 128:(k + 1) * 128]
                .rearrange("p (b r s) -> p r b s", b=2, r=8, s=8),
                pT[:].rearrange("p (r b s) -> p r b s", r=8, b=2, s=8)),
                waits=[("mm", m1)])
            bank_free["pT"] = ("ev", e1)
        ev_xlfm = S.c("ev")

        def pass1_gen(wT, accname, pieces, buf_tensor, bufw, first_ev_wait,
                      out):
            prev_ev = {}
            acc = cols[accname]
            for i, (r0, cc, wdt) in enumerate(pieces):
                half = i % 2
                buf = buf_tensor[:, half * bufw:half * bufw + wdt]
                waits = []
                if i >= 2:
                    waits.append(("ev", prev_ev[i - 2]))
                elif first_ev_wait[0]:
                    waits.append(("ev", first_ev_wait[0]))
                d1 = S.dma("sync", buf, wT[r0:r0 + 128, cc:cc + wdt], waits=waits)
                gr = EV(lambda e, b=buf: e.tensor_reduce(
                    cols["tmp2"][:], b, AX.X, ALU.add, True),
                    waits=[("dS", d1)])
                if i == 0:
                    g2 = EV(lambda e, a=acc: e.tensor_copy(a[:], cols["tmp2"][:]),
                            waits=[("ev", gr)])
                else:
                    g2 = EV(lambda e, a=acc: e.tensor_add(a[:], a[:],
                                                          cols["tmp2"][:]),
                            waits=[("ev", gr)])
                prev_ev[i] = g2
                out[0] = g2
                yield

        p_gu = [(r * 128, ch * 2048, 2048) for r in range(8) for ch in range(2)]
        p_d = [(r * 128, 0, 1024) for r in range(32)]


        def ws_finalize(accname, numel, scol, dcol, slot, evwait):
            w5, c5 = bank_free[5]
            m1 = MM(pb[5][0:1, 0:1], cols[accname][:], c_o1f[:],
                    start=True, stop=True, waits=[("ev", evwait), (w5, c5)])
            e1 = EV(lambda e, s=slot: e.tensor_scalar(
                t11[0:1, s * 2:s * 2 + 1], pb[5][0:1, 0:1],
                1.0 / numel, 1e-5, ALU.mult, ALU.max), waits=[("mm", m1)])
            e2 = EV(lambda e, s=slot: e.reciprocal(
                t11[0:1, s * 2 + 1:s * 2 + 2], t11[0:1, s * 2:s * 2 + 1]),
                waits=[("ev", e1)])
            m2 = MM(pb[5][:, 0:1], c_or[:], t11[0:1, slot * 2:slot * 2 + 1],
                    start=True, stop=True, waits=[("ev", e2)])
            m3 = MM(pb[5][:, 1:2], c_or[:], t11[0:1, slot * 2 + 1:slot * 2 + 2],
                    start=True, stop=True)
            EV(lambda e, d=dcol: e.tensor_copy(cols[d][:], pb[5][:, 0:1]),
               waits=[("mm", m3)])
            e4 = EV(lambda e, s=scol: e.tensor_copy(cols[s][:], pb[5][:, 1:2]))
            bank_free[5] = ("ev", e4)
            return e4


        def pass2_gen(wT, scol, q_dr, pieces, wswait, buf_tensor, bufw,
                      first_gp_wait, out, first_ev_wait=0):
            prev = {}
            for i, (r0, cc, wdt) in enumerate(pieces):
                half = i % 2
                buf = buf_tensor[:, half * bufw:half * bufw + wdt]
                waits = [("ev", wswait[0])]
                if i >= 2:
                    waits.append(("ev", prev[i - 2][0]))
                elif first_gp_wait[0]:
                    waits.append(("ev", first_gp_wait[0]))
                if i < 2 and first_ev_wait:
                    waits.append(("ev", first_ev_wait))
                d1 = S.dma("sync", buf, wT[r0:r0 + 128, cc:cc + wdt], waits=waits)
                glast = None
                for q in range(wdt // 512):
                    qw = [("dS", d1)]
                    if i >= 1 and q == 0:
                        qw.append(("dS", prev[i - 1][1]))
                    g1 = EV(lambda e, b=buf, q=q, s=scol: e.tensor_scalar(
                        EI2[:], b[:, q * 512:(q + 1) * 512], cols[s][:], None,
                        ALU.mult), waits=qw)
                    glast = EV(lambda e, q=q: e.tensor_scalar(
                        scD[:, q * 512:(q + 1) * 512], EI2[:], 1.0, -1.0,
                        ALU.min, ALU.max), waits=[("ev", g1)])
                glast = DRAIN("vector", "ev")
                d2 = S.dma("sync", q_dr[r0:r0 + 128, cc:cc + wdt],
                           scD[:, 0:wdt], waits=[("ev", glast)])
                prev[i] = (glast, d2)
                out[0] = glast
                out[1] = d2
                yield

        p1g_out = [0]
        p1u_out = [0]
        p1d_out = [0]
        wsg_box = [0]
        wsu_box = [0]
        wsd_box = [0]
        p2g_out = [0, 0]
        p2u_out = [0, 0]
        p2d_out = [0, 0]
        p_g2 = [(r * 128, ch * 1024, 1024) for r in range(8) for ch in range(4)]

        def wq1_fn():
            yield from pass1_gen(wgT, "accg", p_gu, CF1, 2048, [0], p1g_out)
            yield from pass1_gen(wuT, "accu", p_gu, CF1, 2048, p1g_out, p1u_out)
            yield from pass1_gen(wdT, "accd", p_d, CF1, 2048, p1u_out, p1d_out)
            wsg_box[0] = ws_finalize("accg", D * DI, "wsgs", "wsgd", 0,
                                     p1g_out[0])
            wsu_box[0] = ws_finalize("accu", D * DI, "wsus", "wsud", 1,
                                     p1u_out[0])
            wsd_box[0] = ws_finalize("accd", D * DI, "wsds", "wsdd", 2,
                                     p1d_out[0])
            yield

        wq1 = wq1_fn()

        def wq1_pump(n):
            for _ in range(n):
                next(wq1, None)

        def wq2_fn():
            yield from pass2_gen(wgT, "wsgs", wgq_dr, p_g2, wsg_box, CFD,
                                 1024, [0], p2g_out)
            yield from pass2_gen(wuT, "wsus", wuq_dr, p_g2, wsu_box, CFD,
                                 1024, [0], p2u_out)

        wq2 = [None]

        def wq2_pump(n):
            if not wsd_box[0]:
                return
            if wq2[0] is None:
                wq2[0] = wq2_fn()
            for _ in range(n):
                next(wq2[0], None)

        for t in range(NT):
            for g2 in range(2):
                bank = 4 + (t * 2 + g2) % 2
                w0, c0 = bank_free[bank]
                for k in range(KC):
                    mmid = MM(pb[bank][:, 0:512], xfm(k)[:, t * 128:(t + 1) * 128],
                              wv16(k)[:, g2 * 512:(g2 + 1) * 512],
                              start=(k == 0), stop=(k == KC - 1),
                              waits=([("dG", max(d_wv, d_xfm)), (w0, c0)]
                                     if k == 0 else ()))
                ev1 = EV(lambda e, t=t, g2=g2, b=bank: e.tensor_copy(
                    Vt(t)[:, g2 * 512:(g2 + 1) * 512], pb[b][:, 0:512]),
                    waits=[("mm", mmid)])
                bank_free[bank] = ("ev", ev1)
                wq1_pump(1)
        mm_v_end = S.c("mm")
        ev_v_end = S.c("ev")

        mm_qlkl = {}
        kt_d = dt("dbg_k2trace", [128, 128], F32, "ExternalOutput") \
            if "k2trace" in dump else None
        d_trc = [0]
        k2_iter = [0]
        mm_bdq2 = {0: 0, 1: 0}
        d_k2_prev = 0
        ev_k2_prev = 0
        last_round_mm = mm_v_end
        for rnd in range(2):
            dwq = S.dma("gpsimd", scC[:, 0:4096].rearrange("p (k q) -> p k q", k=KC),
                        wqkv[:, rnd * 512:rnd * 512 + 512]
                        .rearrange("(k p) q -> p k q", p=128),
                        waits=[("mm", last_round_mm)])
            dwk = S.dma("gpsimd", scC[:, 4096:8192].rearrange("p (k q) -> p k q", k=KC),
                        wqkv[:, 1024 + rnd * 512:1024 + rnd * 512 + 512]
                        .rearrange("(k p) q -> p k q", p=128))
            for half in range(2):
                dwait = dwq if half == 0 else dwk
                for j in range(4):
                    p = rnd * 4 + j
                    wsl = scC[:, half * 4096 + j * 1024:half * 4096 + (j + 1) * 1024]
                    for g in range(4):
                        bank = g
                        w0, c0 = bank_free[bank]
                        for k in range(KC):
                            kw = []
                            if k == 0:
                                kw = [("dG", dwait), (w0, c0)]
                            mmid = MM(pb[bank][:, 0:512],
                                      wsl[:, k * 128:(k + 1) * 128],
                                      xfm(k)[:, g * 512:(g + 1) * 512],
                                      start=(k == 0), stop=(k == KC - 1),
                                      waits=kw)
                        dst = Qf(p) if half == 0 else Kf(p)
                        scale = 0.125 if half == 0 else 1.0
                        a1 = ACT(lambda e, d=dst, g=g, b=bank, s=scale:
                                 e.activation(d[:, g * 512:(g + 1) * 512],
                                              pb[b][:, 0:512], AF.Copy, scale=s),
                                 waits=[("mm", mmid)])
                        bank_free[bank] = ("ac", a1)
                    wpa, cpa = bank_free["pA"]
                    for k in range(KC):
                        mmid = MM(pA[:, 0:128], wsl[:, k * 128:(k + 1) * 128],
                                  xl_fm[:, k * 128:(k + 1) * 128],
                                  start=(k == 0), stop=(k == KC - 1),
                                  waits=([("ev", ev_xlfm), (wpa, cpa)]
                                         if k == 0 else ()))
                    dst2 = ql_fm if half == 0 else kl_fm
                    scale = 0.125 if half == 0 else 1.0
                    e1 = EV(lambda e, d=dst2, p=p, s=scale: e.tensor_scalar(
                        d[:, p * 128:(p + 1) * 128], pA[:, 0:128], s, None,
                        ALU.mult), waits=[("mm", mmid)])
                    bank_free["pA"] = ("ev", e1)
                    mm_qlkl[(half, p)] = e1
                    wq1_pump(2)
                last_round_mm = S.c("mm")

            for j in range(4):
                p = rnd * 4 + j
                for b in range(2):
                    idx = b * 8 + p
                    jb = k2_iter[0] % 2
                    k2_iter[0] += 1
                    wql = mm_qlkl[(0, p)]
                    wkl = mm_qlkl[(1, p)]
                    evm = EV(lambda e, jb=jb: e.memset(bdq[jb][:], 0.0),
                             waits=[("ev", max(wql, wkl)),
                                    ("mm", mm_bdq2[jb])])
                    EV(lambda e, jb=jb, p=p, b=b: e.tensor_copy(
                        bdq[jb][0:64, 0:64],
                        ql_fm[0:64, p * 128 + b * 64:p * 128 + b * 64 + 64]))
                    e2 = EV(lambda e, jb=jb, p=p, b=b: e.tensor_copy(
                        bdq[jb][64:128, 64:128],
                        ql_fm[64:128, p * 128 + b * 64:p * 128 + b * 64 + 64]))
                    bank = 4
                    w0, c0 = bank_free[bank]
                    m1 = MM(pb[bank][:, 0:64], bdq[jb][:],
                            kl_fm[:, p * 128 + b * 64:p * 128 + b * 64 + 64],
                            start=True, stop=True, waits=[("ev", e2), (w0, c0)])
                    a1 = ACT(lambda e, b2=bank: e.activation(
                        k2e[:], pb[b2][:, 0:64], AF.Exp,
                        accum_out=cols["rowr"][:]),
                        waits=[("mm", m1), ("ev", ev_k2_prev)])
                    bank_free[bank] = ("ac", a1)
                    er = EV(lambda e: e.reciprocal(cols["tmp"][:],
                                                   cols["rowr"][:]),
                            waits=[("ac", a1), ("dG", d_k2_prev)])
                    e4 = EV(lambda e: e.tensor_scalar(
                        k2s[:], k2e[:], cols["tmp"][:], None, ALU.mult),
                        waits=[("ev", er)])
                    if "k2trace" in dump:
                        EV(lambda e: e.tensor_copy(
                            azb[0][:, 0:1], cols["rowr"][:]),
                            waits=[("dA", d_trc[0])])
                        EV(lambda e: e.tensor_copy(
                            azb[0][:, 1:2], cols["tmp"][:]))
                        EV(lambda e: e.tensor_copy(azb[0][:, 2:4], k2e[:, 0:2]))
                        e4 = EV(lambda e: e.tensor_copy(
                            azb[0][:, 4:6], k2s[:, 0:2]))
                        d_trc[0] = S.dma("scalar",
                                         kt_d[:, idx * 8:idx * 8 + 8],
                                         azb[0][:, 0:8], waits=[("ev", e4)])
                    e4d = DRAIN("vector", "ev")
                    ev_k2_prev = e4d
                    mm_bdq2[jb] = S.c("mm")
                    d_k2_prev = S.dma("gpsimd", k2_dr[idx * 128:(idx + 1) * 128, :],
                                      k2s[:], waits=[("ev", e4d)])
        d_k2st_all = d_k2_prev
        mm_qkv_end = S.c("mm")
        ac_qkv_end = S.c("ac")

        d_gath = []
        for j in range(4):
            d_gath.append(S.emit(
                "gpsimd",
                lambda e, j=j: e.indirect_dma_start(
                    nw[j]["k2b"][:], None, k2_dr,
                    IndirectOffsetOnAxis(gidx[:, j:j + 1], 0)),
                waits=[("dG", d_k2st_all)], inc="dG", inc_amt=16))


        d_wpj = S.dma("gpsimd", scC[:].rearrange("p (k q) -> p k q", k=KC),
                      wproj.rearrange("(k p) q -> p k q", p=128),
                      waits=[("mm", mm_qkv_end)])


        newton_done_ev = [None]

        def newton_gen():
            for j in range(4):
                w0, c0 = bank_free["pT"]
                m1 = TRP(pT[0:64, 0:64], nw[j]["k2b"][:], c_id[0:64, 0:64],
                         waits=[("dG", d_gath[j]), (w0, c0)])
                e1 = EV(lambda e, j=j: e.tensor_copy(nw[j]["k2t"][:],
                                                     pT[0:64, 0:64]),
                        waits=[("mm", m1)])
                bank_free["pT"] = ("ev", e1)
                yield
            for j in range(4):
                bankid = 4 if j < 2 else 5
                co = 256 + (j % 2) * 128
                w0, c0 = bank_free[bankid]
                m1 = MM(pb[bankid][0:1, co:co + 64], c_o1b[0:64, :],
                        nw[j]["k2b"][:], start=True, stop=True,
                        waits=[("dG", d_gath[j]), (w0, c0)])
                e1 = EV(lambda e, b=bankid, co=co, j=j: e.reduce_max(
                    t11[0:1, 8 + j:9 + j], pb[b][0:1, co:co + 64], AX.X),
                    waits=[("mm", m1)])
                e2 = EV(lambda e, j=j: e.reciprocal(
                    t11[0:1, 12 + j:13 + j], t11[0:1, 8 + j:9 + j]),
                    waits=[("ev", e1)])
                m2 = MM(pb[bankid][0:64, co + 64:co + 65], c_or[0:1, 0:64],
                        t11[0:1, 12 + j:13 + j], start=True, stop=True,
                        waits=[("ev", e2)])
                e3d = EV(lambda e, j=j, b=bankid, co=co: e.tensor_copy(
                    dnc[j][:], pb[b][0:64, co + 64:co + 65]),
                    waits=[("mm", m2)])
                EV(lambda e, j=j: e.tensor_scalar(
                    nw[j]["vk"][:], nw[j]["k2t"][:], dnc[j][:], None, ALU.mult),
                    waits=[("ev", e3d)])
                e5 = EV(lambda e, j=j: e.tensor_scalar(
                    nw[j]["vt"][:], nw[j]["k2b"][:], dnc[j][:], None, ALU.mult))
                bank_free[bankid] = ("ev", e5)
                yield
            for it in range(6):
                for pair in range(2):
                    bankid = 4 + pair
                    w0, c0 = bank_free[bankid]
                    mm1 = 0
                    for jj in range(2):
                        j = pair * 2 + jj
                        co = jj * 128
                        MM(pb[bankid][0:64, co:co + 64], nw[j]["k2t"][:],
                           nw[j]["vk"][:], start=True, stop=True,
                           waits=[(w0, c0)] if jj == 0 else ())
                        mm1 = MM(pb[bankid][0:64, co + 64:co + 128],
                                 nw[j]["vk"][:], nw[j]["k2t"][:],
                                 start=True, stop=True)
                    ee = 0
                    for jj in range(2):
                        j = pair * 2 + jj
                        co = jj * 128
                        EV(lambda e, j=j, b=bankid, co=co: e.scalar_tensor_tensor(
                            nw[j]["tt"][:], pb[b][0:64, co:co + 64], -1.0,
                            c_id3[:, 0:64], ALU.mult, ALU.add),
                            waits=[("mm", mm1)])
                        ee = EV(lambda e, j=j, b=bankid, co=co: e.tensor_copy(
                            nw[j]["ptb"][:], pb[b][0:64, co + 64:co + 128]))
                    yield
                    mm2 = 0
                    for jj in range(2):
                        j = pair * 2 + jj
                        co = jj * 128
                        mm2 = MM(pb[bankid][0:64, co:co + 64], nw[j]["ptb"][:],
                                 nw[j]["tt"][:], start=True, stop=True,
                                 waits=[("ev", ee)] if jj == 0 else ())
                    for jj in range(2):
                        j = pair * 2 + jj
                        co = jj * 128
                        ee = EV(lambda e, j=j, b=bankid, co=co:
                                e.scalar_tensor_tensor(
                                    nw[j]["tt"][:], pb[b][0:64, co:co + 64],
                                    -1.0, c_id3[:, 64:128], ALU.mult, ALU.add),
                                waits=[("mm", mm2)])
                    yield
                    for jj in range(2):
                        j = pair * 2 + jj
                        co = jj * 128
                        mm2 = MM(pb[bankid][0:64, co:co + 64], nw[j]["ptb"][:],
                                 nw[j]["tt"][:], start=True, stop=True,
                                 waits=[("ev", ee)] if jj == 0 else ())
                    for jj in range(2):
                        j = pair * 2 + jj
                        co = jj * 128
                        ee = EV(lambda e, j=j, b=bankid, co=co:
                                e.scalar_tensor_tensor(
                                    nw[j]["tt"][:], pb[b][0:64, co:co + 64],
                                    -1.0, c_id3[:, 128:192], ALU.mult, ALU.add),
                                waits=[("mm", mm2)])
                    yield
                    for jj in range(2):
                        j = pair * 2 + jj
                        co = jj * 128
                        MM(pb[bankid][0:64, co:co + 64], nw[j]["vt"][:],
                           nw[j]["tt"][:], start=True, stop=True,
                           waits=[("ev", ee)] if jj == 0 else ())
                        mm2 = MM(pb[bankid][0:64, co + 64:co + 128],
                                 nw[j]["tt"][:], nw[j]["vt"][:],
                                 start=True, stop=True)
                    for jj in range(2):
                        j = pair * 2 + jj
                        co = jj * 128
                        EV(lambda e, j=j, b=bankid, co=co: e.tensor_scalar(
                            nw[j]["vk"][:], pb[b][0:64, co:co + 64], 0.25,
                            None, ALU.mult), waits=[("mm", mm2)])
                        ee = EV(lambda e, j=j, b=bankid, co=co: e.tensor_scalar(
                            nw[j]["vt"][:], pb[b][0:64, co + 64:co + 128],
                            0.25, None, ALU.mult))
                    bank_free[bankid] = ("ev", ee)
                    yield
            newton_done_ev[0] = S.c("ev")

        ngen = newton_gen()

        def pump(n):
            for _ in range(n):
                if newton_done_ev[0] is not None:
                    return
                next(ngen, None)

        def pump_all():
            while newton_done_ev[0] is None:
                if next(ngen, "X") == "X":
                    break

        mm_bdk = {0: 0, 1: 0}
        for idx in range(16):
            b, p = divmod(idx, 8)
            jb = idx % 2
            evm4 = EV(lambda e, jb=jb: e.memset(bdk[jb][:], 0.0),
                      waits=[("mm", mm_bdk[jb]), ("ev", mm_qlkl[(1, p)])])
            EV(lambda e, jb=jb, p=p, b=b: e.tensor_copy(
                bdk[jb][0:64, 0:64],
                kl_fm[0:64, p * 128 + b * 64:p * 128 + b * 64 + 64]))
            e2 = EV(lambda e, jb=jb, p=p, b=b: e.tensor_copy(
                bdk[jb][64:128, 64:128],
                kl_fm[64:128, p * 128 + b * 64:p * 128 + b * 64 + 64]))
            for g in range(2):
                bank = (idx * 2 + g) % 2
                w0, c0 = bank_free[bank]
                m1 = MM(pb[bank][:, 0:512], bdk[jb][:],
                        Qf(p)[:, b * 1024 + g * 512:b * 1024 + (g + 1) * 512],
                        start=True, stop=True,
                        waits=[("ev", e2), (w0, c0), ("ac", ac_qkv_end)])
                a1 = ACT(lambda e, idx=idx, g=g, b2=bank: e.activation(
                    exp1(idx)[:, g * 512:(g + 1) * 512], pb[b2][:, 0:512],
                    AF.Exp), waits=[("mm", m1), ("mm", mm_qkv_end)])
                bank_free[bank] = ("ac", a1)
            mm_bdk[jb] = S.c("mm")
            pump(2)
            wq1_pump(4)
        ac_exp1_end = S.c("ac")

        mm_bdq5 = {0: 0, 1: 0}
        mm_a = {0: 0, 1: 0}
        d_az = 0
        d_az_prev = {0: 0, 1: 0}
        for idx in range(16):
            b, p = divmod(idx, 8)
            jb = idx % 2
            evm5 = EV(lambda e, jb=jb: e.memset(bdq[jb][:], 0.0),
                      waits=[("mm", mm_bdq5[jb]), ("ev", mm_qlkl[(0, p)])])
            EV(lambda e, jb=jb, p=p, b=b: e.tensor_copy(
                bdq[jb][0:64, 0:64],
                ql_fm[0:64, p * 128 + b * 64:p * 128 + b * 64 + 64]))
            e2 = EV(lambda e, jb=jb, p=p, b=b: e.tensor_copy(
                bdq[jb][64:128, 64:128],
                ql_fm[64:128, p * 128 + b * 64:p * 128 + b * 64 + 64]))
            pa_key = "pA" if idx % 2 == 0 else 2
            pa_ap = pA if idx % 2 == 0 else pb[2]
            wA, cA = bank_free[pa_key]
            m2 = 0
            for quad in range(2):
                bank = quad % 2
                half = quad % 2
                w0, c0 = bank_free[bank]
                m1 = 0
                for q in range(4):
                    t8 = quad * 4 + q
                    kw = []
                    if q == 0:
                        kw = [("ev", e2), (w0, c0), ("ac", ac_qkv_end)]
                    m1 = MM(pb[bank][:, q * 128:(q + 1) * 128],
                            Kf(p)[:, b * 1024 + t8 * 128:b * 1024 + (t8 + 1) * 128],
                            bdq[jb][:], start=True, stop=True, waits=kw)
                a1 = ACT(lambda e, b2=bank, h=half: e.activation(
                    scB[:, h * 1024:h * 1024 + 512], pb[b2][:, 0:512], AF.Exp),
                    waits=[("mm", m1), ("mm", mm_a[half])])
                bank_free[bank] = ("ac", a1)
                for q in range(4):
                    t8 = quad * 4 + q
                    tglob = b * 8 + t8
                    kw = [("ac", a1)]
                    if t8 == 0:
                        kw += [(wA, cA), ("ev", ev_v_end)]
                    MM(pa_ap[:, 0:128],
                       scB[:, half * 1024 + q * 128:half * 1024 + (q + 1) * 128],
                       Vt(tglob)[:, p * 128:(p + 1) * 128],
                       start=(t8 == 0), stop=(t8 == 7), waits=kw)
                    m2 = MM(pa_ap[:, 128:129],
                            scB[:, half * 1024 + q * 128:half * 1024 + (q + 1) * 128],
                            c_o1b[:], start=False, stop=(t8 == 7))
                mm_a[half] = S.c("mm")
            mm_bdq5[jb] = S.c("mm")
            eaz = EV(lambda e, i=idx % 2, a=pa_ap: e.tensor_copy(
                azb[i][:, 0:129], a[:, 0:129]),
                     waits=[("mm", m2), ("dA", max(d_az_prev[idx % 2],
                                                   d_trc[0]))])
            bank_free[pa_key] = ("ev", eaz)
            d_az = S.dma("scalar", ar_in[idx * 128:(idx + 1) * 128, 0:129],
                         azb[idx % 2][:, 0:129], waits=[("ev", eaz)])
            d_az_prev[idx % 2] = d_az
            pump(3)
            wq1_pump(4)
            wq2_pump(2)
        pump_all()
        wq1_pump(10 ** 6)
        mm_p5_end = S.c("mm")

        d_w2 = 0
        for j in range(4):
            d_w2 = S.dma("scalar", w2_in[j * 64:(j + 1) * 64, :], nw[j]["vt"][:],
                         waits=[("ev", newton_done_ev[0])])
        S.emit("gpsimd",
               lambda e: e.collective_compute(
                   "AllReduce", ALU.add, replica_groups=[list(range(NC))],
                   ins=[ar_in.opt()], outs=[ar_out.opt()]),
               waits=[("dA", d_az)], inc="cc2")
        S.emit("gpsimd",
               lambda e: e.collective_compute(
                   "AllGather", ALU.bypass, replica_groups=[list(range(NC))],
                   ins=[w2_in.opt()], outs=[w2_out.opt()]),
               waits=[("dA", d_w2)], inc="cc3")

        p_u2 = [(r * 128, ch * 1024, 1024) for r in range(8) for ch in range(4)]


        ev_azread = {0: 0, 1: 0}
        mm_bdwread = {0: 0, 1: 0}
        ev_sv_final = 0
        mm_rz_prev = 0
        for idx in range(16):
            b, p = divmod(idx, 8)
            i2 = idx % 2
            r0 = (idx // 2) * 256 + (idx % 2) * 128
            d1 = S.dma("scalar", azb[i2][:, 0:132],
                       ar_out[idx * 128:(idx + 1) * 128, :],
                       waits=[("cc2", 1), ("ev", ev_azread[i2])])
            d2a = S.dma("scalar", bdw[i2][0:64, 0:64], w2_out[r0:r0 + 64, :],
                        waits=[("cc3", 1), ("mm", mm_bdwread[i2])])
            d2b = S.dma("scalar", bdw[i2][64:128, 64:128],
                        w2_out[r0 + 64:r0 + 128, :])
            if idx < 2:
                EV(lambda e, i2=i2: e.memset(bdw[i2][0:64, 64:128], 0.0))
                EV(lambda e, i2=i2: e.memset(bdw[i2][64:128, 0:64], 0.0))
                EV(lambda e, i2=i2: e.memset(bsm[i2][:], 0.0))
            e1 = EV(lambda e, i2=i2: e.reciprocal(cols["tmp"][:],
                                                  azb[i2][:, 128:129]),
                    waits=[("dA", d1)])
            e1b = EV(lambda e, i2=i2: e.tensor_scalar(
                k3v[i2][:], azb[i2][:, 0:128], cols["tmp"][:], None, ALU.mult),
                waits=[("ev", e1)])
            ev_azread[i2] = e1b
            bank = 2
            w0, c0 = bank_free[bank]
            m1 = MM(pb[bank][:, 0:128], bdw[i2][:], k3v[i2][:],
                    start=True, stop=True,
                    waits=[("ev", e1b), ("dA", d2b), (w0, c0),
                           ("mm", mm_bdwread[i2])])
            e3 = EV(lambda e, b2=bank, i2=i2: e.tensor_copy(
                bsm[i2][0:64, 0:64], pb[b2][0:64, 0:64]), waits=[("mm", m1)])
            e4 = EV(lambda e, b2=bank, i2=i2: e.tensor_copy(
                bsm[i2][64:128, 64:128], pb[b2][64:128, 64:128]))
            bank_free[bank] = ("ev", e4)
            for g in range(2):
                bank = 3
                w0, c0 = bank_free[bank]
                m2b = MM(pb[bank][0:2, 0:512], c_p1[:],
                         exp1(idx)[:, g * 512:(g + 1) * 512],
                         start=True, stop=True,
                         waits=[("ac", ac_exp1_end), (w0, c0)])
                e5 = EV(lambda e, b2=bank: e.tensor_copy(zc[:],
                                                         pb[b2][0:2, 0:512]),
                        waits=[("mm", m2b)] +
                              ([("ev", p1d_out[0])] if idx == 0 and g == 0 else []))
                e6 = EV(lambda e: e.reciprocal(rz[:], zc[:]),
                        waits=[("mm", mm_rz_prev), ("ev", e5)])
                w4, c4 = bank_free[4]
                m3 = MM(pb[4][:, 0:512], c_bc[:], rz[:],
                        start=True, stop=True, waits=[("ev", e6), (w4, c4)])
                mm_rz_prev = m3
                e7 = EV(lambda e: e.tensor_copy(bcs[:], pb[4][:, 0:512]),
                        waits=[("mm", m3)])
                bank_free[4] = ("ev", e7)
                bank_free[bank] = ("ev", e5)
                bank = (idx * 2 + g) % 2
                w0, c0 = bank_free[bank]
                m4 = MM(pb[bank][:, 0:512], bsm[i2][:],
                        exp1(idx)[:, g * 512:(g + 1) * 512],
                        start=True, stop=True, waits=[("ev", e4), (w0, c0)])
                e8 = EV(lambda e, p=p, b=b, g=g, b2=bank: e.tensor_tensor(
                    SVf(p)[:, b * 1024 + g * 512:b * 1024 + (g + 1) * 512],
                    pb[b2][:, 0:512], bcs[:], ALU.mult),
                    waits=[("mm", m4), ("ev", e7)])
                bank_free[bank] = ("ev", e8)
            mm_bdwread[i2] = S.c("mm")
            ev_sv_final = S.c("ev")
            wq2_pump(4)
        mm_g_end = S.c("mm")

        d_wgq1 = S.dma("sync", S3[:].rearrange("p (k i) -> p k i", k=4),
                       wgq_dr[0:512, :].rearrange("(k p) i -> p k i", p=128),
                       waits=[("dS", p2g_out[1]), ("mm", mm_p5_end)])
        S.dma("sync", S4[:].rearrange("p (k i) -> p k i", k=4),
              wgq_dr[512:1024, :].rearrange("(k p) i -> p k i", p=128))
        S.dma("sync", S1[:].rearrange("p (k i) -> p k i", k=4),
              wuq_dr[0:512, :].rearrange("(k p) i -> p k i", p=128),
              waits=[("dS", p2u_out[1]), ("mm", mm_g_end)])

        def wgq(k):
            if k < 4:
                return S3[:, k * 4096:(k + 1) * 4096]
            return S4[:, (k - 4) * 4096:(k - 3) * 4096]

        def wuq(k):
            if k < 4:
                return S1[:, k * 4096:(k + 1) * 4096]
            return S2[:, (k - 4) * 4096:(k - 3) * 4096]

        d_xres = {}
        ev_xread = {0: 0, 1: 0}
        ev_v1_free = 0
        ev_y1_free = 0
        d_y1_prev = 0
        ac_sq_prev = 0
        gp_ei_reader = 0
        d_y1q_prev = 0
        d_y1q_all = 0
        for t in range(NT):
            bankA = (t % 2) * 2
            bankB = bankA + 1
            xw = [("ev", ev_xread[t % 2])]
            if t == 0:
                xw.append(("ev", p1d_out[0]))
                xw.append(("mm", S.c("mm")))
            d_xres[t] = S.dma("scalar",
                              CF1[:, (t % 2) * 1024:(t % 2) * 1024 + 1024],
                              x_tm[t * 128:(t + 1) * 128, :], waits=xw)
            e1 = 0
            for og in range(2):
                bank = bankA if og == 0 else bankB
                w0, c0 = bank_free[bank]
                for k in range(KC):
                    kw = []
                    if k == 0:
                        kw = [("dG", d_wpj), (w0, c0), ("ev", ev_sv_final)]
                    mmid = MM(pb[bank][:, 0:512],
                              SVf(k)[:, t * 128:(t + 1) * 128],
                              scC[:, k * 1024 + og * 512:k * 1024 + (og + 1) * 512],
                              start=(k == 0), stop=(k == KC - 1), waits=kw)
                e1 = EV(lambda e, t=t, og=og, b2=bank: e.tensor_add(
                    CF1[:, 2048 + og * 512:2048 + (og + 1) * 512],
                    pb[b2][:, 0:512],
                    CF1[:, (t % 2) * 1024 + og * 512:
                        (t % 2) * 1024 + (og + 1) * 512]),
                    waits=[("mm", mmid), ("dA", d_xres[t]), ("ev", ev_v1_free)])
                bank_free[bank] = ("ev", e1)
            ev_xread[t % 2] = e1
            es = EV(lambda e: e.reduce_sum(cols["sum"][:], CF1[:, 2048:3072],
                                           AX.X))
            e2 = EV(lambda e: e.tensor_scalar(
                cols["negm"][:], cols["sum"][:], -1.0 / D, None, ALU.mult),
                waits=[("ev", es)])
            w5, c5 = bank_free[5]
            a1 = ACT(lambda e: e.activation(
                pb[5][:, 0:512], CF1[:, 2048:2560], AF.Square,
                bias=cols["negm"][:], accum_out=cols["ssq"][:]),
                waits=[("ev", e2), (w5, c5), ("ac", ac_sq_prev)])
            a2 = ACT(lambda e: e.activation(
                pb[5][:, 0:512], CF1[:, 2560:3072], AF.Square,
                bias=cols["negm"][:], accum_out=cols["ssqb"][:]))
            bank_free[5] = ("ac", a2)
            ac_sq_prev = a2
            e3 = EV(lambda e: e.tensor_add(cols["ssq"][:], cols["ssq"][:],
                                           cols["ssqb"][:]),
                    waits=[("ac", a2)])
            e3b = EV(lambda e: e.tensor_scalar(
                cols["ssq"][:], cols["ssq"][:], 1.0 / D, 1e-5,
                ALU.mult, ALU.add), waits=[("ev", e3)])
            a3 = ACT(lambda e: e.activation(cols["sd"][:], cols["ssq"][:],
                                            AF.Sqrt),
                     waits=[("ev", e3b)])
            e4 = EV(lambda e: e.reciprocal(cols["rstd"][:], cols["sd"][:]),
                    waits=[("ac", a3)])
            e5 = EV(lambda e: e.tensor_scalar(
                CF1[:, 3072:4096], CF1[:, 2048:3072], cols["negm"][:],
                cols["rstd"][:], ALU.add, ALU.mult),
                waits=[("ev", e4), ("dA", d_y1q_prev)])
            ev_v1_free = e5
            d_y1_prev = S.dma("scalar", y1_dr[t * 128:(t + 1) * 128, :],
                              CF1[:, 3072:4096], waits=[("ev", e5)])
            e6 = EV(lambda e: e.reduce_max(cols["rmax"][:], CF1[:, 3072:4096],
                                           AX.X, apply_absolute_value=True))
            e7 = EV(lambda e, t=t: e.tensor_scalar(
                rsa_all[:, t:t + 1], cols["rmax"][:], 1e-5, 1.0 / 127,
                ALU.max, ALU.mult), waits=[("ev", e6)])
            e8 = EV(lambda e, t=t: e.reciprocal(cols["sa"][:],
                                                rsa_all[:, t:t + 1]),
                    waits=[("ev", e7)])
            for hh in range(2):
                EV(lambda e, hh=hh: e.tensor_scalar(
                    EI[:], CF1[:, 3072 + hh * 512:3072 + (hh + 1) * 512],
                    cols["sa"][:], None, ALU.mult),
                    waits=[("ev", e8)])
                EV(lambda e, hh=hh: e.tensor_copy(
                    scB[:, hh * 512:(hh + 1) * 512], EI[:]),
                    waits=[("dA", d_y1q_prev)])
            evd = DRAIN("vector", "ev")
            ev_y1_free = S.c("ev")
            d_y1q_prev = S.dma("scalar", y1q_dr[t * 128:(t + 1) * 128, :],
                               scB[:, 0:1024], waits=[("ev", evd)])
            d_y1q_all = d_y1q_prev
            wq2_pump(4)
        wq2_pump(10 ** 6)
        mm_proj_end = S.c("mm")

        d_st_prev = {0: 0, 1: 0}
        d_y1f_prev = 0
        for cch in range(KC):
            buf = scE[:] if cch % 2 == 0 else scC[:, 0:2048]
            wv_ = [("dA", d_y1q_all)]
            if cch % 2 == 1 and cch == 1:
                wv_.append(("mm", mm_proj_end))
            if d_st_prev[cch % 2]:
                wv_.append(("dS", d_st_prev[cch % 2]))
            dtr = S.dmat("sync", buf, y1q_dr[:, cch * 128:(cch + 1) * 128],
                         waits=wv_)
            d_st_prev[cch % 2] = S.dma(
                "sync", y1qfm_dr[cch * 128:(cch + 1) * 128, :],
                buf, waits=[("dS", dtr)])
        d_y1qfm_all = d_st_prev[1]

        wq3 = [None]

        def wq3_pump(n):
            if wq3[0] is None:
                wq3[0] = pass2_gen(wdT, "wsds", wdq_dr, p_d, wsd_box, CFD,
                                   1024, p2u_out, p2d_out)
            for _ in range(n):
                next(wq3[0], None)


        d_wq_all = S.dma("sync", S2[:].rearrange("p (k i) -> p k i", k=4),
                         wuq_dr[512:1024, :].rearrange("(k p) i -> p k i",
                                                       p=128),
                         waits=[("mm", mm_proj_end)])

        ev_dqg = EV(lambda e: e.tensor_scalar(
            rsa2_all[:], rsa_all[:], cols["wsgd"][:], None, ALU.mult))
        ev_dqu = EV(lambda e: e.tensor_scalar(
            rsa3_all[:], rsa_all[:], cols["wsud"][:], None, ALU.mult))

        f1t_d = dt("dbg_f1trace", [128, 4096], F32, "ExternalOutput") \
            if "f1trace" in dump else None
        mm_f1_tile = {0: 0, 1: 0}
        ac_f1_prev = [0]
        eh_slot = {0: 0, 1: 0}
        gp_hq_prev = 0
        gp_ei_last = [0]
        d_hq_last = [0]
        d_hq_all = 0
        ev_h_last = 0
        for t in range(NT):
            d_y1f = S.dma("scalar", scE[:, (t % 2) * 1024:(t % 2) * 1024 + 1024]
                          .rearrange("p (k q) -> p k q", k=KC),
                          y1qfm_dr[:, t * 128:(t + 1) * 128]
                          .rearrange("(k p) q -> p k q", p=128),
                          waits=([("dS", d_y1qfm_all)] if t == 0 else []) +
                                [("mm", mm_f1_tile[t % 2])])

            def stat(k, t=t):
                return scE[:, (t % 2) * 1024 + k * 128:
                           (t % 2) * 1024 + (k + 1) * 128]


            for ig in range(8):
                bg = (ig % 2) * 2
                bu = bg + 1
                gslot = ig % 2
                wg0, cg0 = bank_free[bg]
                mg = 0
                for k in range(KC):
                    kw = []
                    if k == 0:
                        kw = [("dA", d_y1f), ("dS", d_wq_all), (wg0, cg0)]
                    mg = MM(pb[bg][:, 0:512], stat(k),
                            wgq(k)[:, ig * 512:(ig + 1) * 512],
                            start=(k == 0), stop=(k == KC - 1), waits=kw)
                wu0, cu0 = bank_free[bu]
                mu = 0
                for k in range(KC):
                    kw = [(wu0, cu0)] if k == 0 else []
                    mu = MM(pb[bu][:, 0:512], stat(k),
                            wuq(k)[:, ig * 512:(ig + 1) * 512],
                            start=(k == 0), stop=(k == KC - 1), waits=kw)
                a1 = ACT(lambda e, b2=bg, gs=gslot, t=t: e.activation(
                    scB[:, gs * 512:(gs + 1) * 512], pb[b2][:, 0:512],
                    AF.Silu, scale=rsa2_all[:, t:t + 1]),
                    waits=[("ev", ev_dqg), ("mm", mg), ("ev", eh_slot[gslot]),
                           ("dA", d_y1q_all if t == 0 else 0),
                           ("dS", d_y1qfm_all if t == 0 else 0)])
                bank_free[bg] = ("ac", a1)
                eh = EV(lambda e, b2=bu, ig=ig, gs=gslot, t=t:
                        e.scalar_tensor_tensor(
                            CF1[:, ig * 512:(ig + 1) * 512], pb[b2][:, 0:512],
                            rsa3_all[:, t:t + 1],
                            scB[:, gs * 512:(gs + 1) * 512], ALU.mult,
                            ALU.mult),
                        waits=[("ac", a1), ("ev", ev_dqu), ("mm", mu)])
                bank_free[bu] = ("ev", eh)
                eh_slot[gslot] = eh
            if "f1trace" in dump and t == 0:
                ehd0 = DRAIN("vector", "ev")
                d_f1t = S.dma("sync", f1t_d, CF1[:, 0:4096],
                              waits=[("ev", ehd0)])
                e_g = EV(lambda e: e.tensor_copy(azb[0][:, 0:1],
                                                 cols["dqg"][:]),
                         waits=[("dS", d_f1t)])
            ehd = DRAIN("vector", "ev")
            g1 = EV(lambda e: e.tensor_reduce(
                cols["tmp2"][:], CF1[:], AX.X, ALU.max, True),
                waits=[("ev", ehd)])
            e10 = EV(lambda e, t=t: e.tensor_scalar(
                rs2_all[:, t:t + 1], cols["tmp2"][:], 1e-5, 1.0 / 127,
                ALU.max, ALU.mult), waits=[("ev", g1)])
            e11 = EV(lambda e, t=t: e.reciprocal(cols["s2"][:],
                                                 rs2_all[:, t:t + 1]),
                     waits=[("ev", e10)])
            for q in range(8):
                EV(lambda e, q=q: e.tensor_scalar(
                    EI[:], CF1[:, q * 512:(q + 1) * 512], cols["s2"][:], None,
                    ALU.mult), waits=[("ev", e11)])
                EV(lambda e, q=q: e.tensor_copy(
                    scB[:, 1024 + (q % 2) * 512:1024 + (q % 2 + 1) * 512],
                    EI[:]),
                    waits=[("dS", d_hq_last[0])])
                if q % 2 == 1:
                    evd2 = DRAIN("vector", "ev")
                    d_hq = S.dma("sync",
                                 hq_dr[t * 128:(t + 1) * 128,
                                       (q - 1) * 512:(q + 1) * 512],
                                 scB[:, 1024:2048], waits=[("ev", evd2)])
                    d_hq_last[0] = d_hq
                    d_hq_all = d_hq
            ev_h_last = S.c("ev")
            mm_f1_tile[t % 2] = S.c("mm")
            ac_f1_prev[0] = S.c("ac")
            wq3_pump(2)
        wq3_pump(10 ** 6)
        mm_f1_end = S.c("mm")
        ac_f1_end = S.c("ac")

        ev_dqd = EV(lambda e: e.tensor_scalar(
            rs2d_all[:], rs2_all[:], cols["wsdd"][:], None, ALU.mult))
        d_wdc = []
        for ic in range(32):
            dst = (S2[:, ic * 1024:(ic + 1) * 1024] if ic < 16 else
                   S1[:, (ic - 16) * 1024:(ic - 15) * 1024])
            kw = ([("dS", p2d_out[1]), ("mm", mm_f1_end)] if ic == 0 else [])
            d_wdc.append(S.dma("gpsimd", dst,
                               wdq_dr[ic * 128:(ic + 1) * 128, :], waits=kw))

        def wdq(ic):
            if ic < 16:
                return S2[:, ic * 1024:(ic + 1) * 1024]
            return S1[:, (ic - 16) * 1024:(ic - 15) * 1024]

        ev_v2_free = 0
        d_out_prev = 0
        ac_sq2_prev = 0
        mm_tb_read = {0: mm_f1_end, 1: mm_f1_end}
        ev_y1r_free = {0: 0, 1: 0}
        for tb in range(4):
            slab = S3 if tb % 2 == 0 else S4
            d_hqf = 0
            for ic in range(ICN):
                kw = []
                if ic == 0:
                    kw = [("dS", d_hq_all), ("mm", mm_tb_read[tb % 2])]
                d_hqf = S.dmat("sync", slab[:, ic * 512:(ic + 1) * 512],
                               hq_dr[tb * 512:(tb + 1) * 512,
                                     ic * 128:(ic + 1) * 128], waits=kw)
            for t4 in range(4):
                t = tb * 4 + t4
                yw = [("ev", ev_y1r_free[t4 % 2])]
                if tb == 0 and t4 == 0:
                    yw += [("ev", ev_h_last)]
                d_y1r = S.dma("scalar",
                              CF1[:, (t4 % 2) * 1024:(t4 % 2) * 1024 + 1024],
                              y1_dr[t * 128:(t + 1) * 128, :], waits=yw)

                e1 = 0
                for og in range(2):
                    bank = t4 % 2 + og * 2
                    w0, c0 = bank_free[bank]
                    mmid = 0
                    for ic in range(ICN):
                        kw = []
                        if ic == 0:
                            kw = [("dS", d_hqf), (w0, c0)]
                        if tb == 0 and t4 == 0 and og == 0:
                            kw.append(("dG", d_wdc[ic]))
                        mmid = MM(pb[bank][:, 0:512],
                                  slab[:, ic * 512 + t4 * 128:
                                       ic * 512 + (t4 + 1) * 128],
                                  wdq(ic)[:, og * 512:(og + 1) * 512],
                                  start=(ic == 0), stop=(ic == ICN - 1),
                                  waits=kw)
                    e1 = EV(lambda e, og=og, t4=t4, b2=bank, t=t:
                            e.scalar_tensor_tensor(
                                CF1[:, 2048 + og * 512:2048 + (og + 1) * 512],
                                pb[b2][:, 0:512], rs2d_all[:, t:t + 1],
                                CF1[:, (t4 % 2) * 1024 + og * 512:
                                    (t4 % 2) * 1024 + (og + 1) * 512],
                                ALU.mult, ALU.add),
                            waits=[("mm", mmid), ("ev", ev_dqd), ("dA", d_y1r),
                                   ("ev", ev_v2_free)])
                    bank_free[bank] = ("ev", e1)
                ev_y1r_free[t4 % 2] = e1
                es = EV(lambda e: e.reduce_sum(cols["sum"][:],
                                               CF1[:, 2048:3072], AX.X))
                e2 = EV(lambda e: e.tensor_scalar(
                    cols["negm"][:], cols["sum"][:], -1.0 / D, None, ALU.mult),
                    waits=[("ev", es)])
                w5, c5 = bank_free[5]
                aw = [("ev", e2), (w5, c5), ("ac", ac_sq2_prev)]
                if tb == 0 and t4 == 0:
                    aw.append(("ac", ac_f1_end))
                a1 = ACT(lambda e: e.activation(
                    pb[5][:, 0:512], CF1[:, 2048:2560], AF.Square,
                    bias=cols["negm"][:], accum_out=cols["ssq"][:]), waits=aw)
                a2 = ACT(lambda e: e.activation(
                    pb[5][:, 0:512], CF1[:, 2560:3072], AF.Square,
                    bias=cols["negm"][:], accum_out=cols["ssqb"][:]))
                bank_free[5] = ("ac", a2)
                ac_sq2_prev = a2
                e3 = EV(lambda e: e.tensor_add(cols["ssq"][:], cols["ssq"][:],
                                               cols["ssqb"][:]),
                        waits=[("ac", a2)])
                e3b = EV(lambda e: e.tensor_scalar(
                    cols["ssq"][:], cols["ssq"][:], 1.0 / D, 1e-5,
                    ALU.mult, ALU.add), waits=[("ev", e3)])
                a3 = ACT(lambda e: e.activation(cols["sd"][:], cols["ssq"][:],
                                                AF.Sqrt),
                         waits=[("ev", e3b)])
                e4r = EV(lambda e: e.reciprocal(cols["rstd"][:],
                                                cols["sd"][:]),
                         waits=[("ac", a3)])
                e5 = EV(lambda e: e.tensor_scalar(
                    CF1[:, 3072:4096], CF1[:, 2048:3072], cols["negm"][:],
                    cols["rstd"][:], ALU.add, ALU.mult),
                    waits=[("ev", e4r), ("dA", d_out_prev)])
                ev_v2_free = e5
                d_out_prev = S.dma("scalar", out_d[t * 128:(t + 1) * 128, :],
                                   CF1[:, 3072:4096], waits=[("ev", e5)])
            mm_tb_read[tb % 2] = S.c("mm")

        if "k2probe" in dump:
            kp = dt("dbg_k2probe", [128, 132], F32, "ExternalOutput")
            wfin = [("dA", S.c("dA")), ("dS", S.c("dS")), ("dG", S.c("dG")),
                    ("ev", S.c("ev")), ("ac", S.c("ac")), ("gp", S.c("gp")),
                    ("mm", S.c("mm"))]
            e_kp = EV(lambda e: e.tensor_copy(CF1[:, 0:1], cols["rowr"][:]),
                      waits=wfin)
            EV(lambda e: e.tensor_copy(CF1[:, 1:2], cols["tmp"][:]))
            EV(lambda e: e.tensor_copy(CF1[:, 2:66], k2e[:]))
            EV(lambda e: e.tensor_copy(CF1[:, 66:130], k2s[:]))
            EV(lambda e: e.tensor_copy(CF1[:, 130:131], cols["wsgs"][:]))
            e_kp2 = EV(lambda e: e.tensor_copy(CF1[:, 131:132], cols["wsgd"][:]))
            S.dma("sync", kp[:, 0:132], CF1[:, 0:132], waits=[("ev", e_kp2)])

        dumpable = {"y1_dr": y1_dr, "y1q_dr": y1q_dr, "y1qfm_dr": y1qfm_dr,
                    "hq_dr": hq_dr, "wgq_dr": wgq_dr, "wuq_dr": wuq_dr,
                    "wdq_dr": wdq_dr, "k2_dr": k2_dr, "ag_out": ag_out,
                    "ar_in": ar_in, "ar_out": ar_out, "w2_out": w2_out}
        shapes = {"y1_dr": ([TL, D], F32), "y1q_dr": ([TL, D], BF16),
                  "y1qfm_dr": ([D, TL], BF16), "hq_dr": ([TL, DI], BF16),
                  "wgq_dr": ([D, DI], BF16), "wuq_dr": ([D, DI], BF16),
                  "wdq_dr": ([DI, D], BF16), "k2_dr": ([2048, 64], BF16),
                  "ag_out": ([128, D], BF16), "ar_in": ([2048, 132], F32),
                  "ar_out": ([2048, 132], F32), "w2_out": ([2048, 64], BF16)}
        for nmd in dump:
            if nmd in ("k2probe", "k2trace", "f1trace"):
                continue
            sh, dty = shapes[nmd]
            od = dt("dbg_" + nmd, sh, dty, "ExternalOutput")
            S.dma("sync", od, dumpable[nmd],
                  waits=[("dA", S.c("dA")), ("dS", S.c("dS")),
                         ("dG", S.c("dG")), ("ev", S.c("ev")),
                         ("ac", S.c("ac")), ("gp", S.c("gp")),
                         ("mm", S.c("mm")), ("cc2", 1), ("cc3", 1)])

        S.emit("sync", lambda e: e.nop(),
               waits=[("dS", S.c("dS")), ("dA", S.c("dA")), ("dG", S.c("dG")),
                      ("ev", S.c("ev")), ("ac", S.c("ac")), ("gp", S.c("gp")),
                      ("mm", S.c("mm"))])

        S.build(blk, sems)
    return nc


_NC_CACHE = [None]


def _consts():
    c = {}
    c["ident"] = np.eye(128).astype(ml_dtypes.bfloat16)
    id3 = np.zeros((64, 192), np.float32)
    id3[:, 0:64] = 7.0 * np.eye(64)
    id3[:, 64:128] = 15.0 * np.eye(64)
    id3[:, 128:192] = 13.0 * np.eye(64)
    c["id3"] = id3
    p1 = np.zeros((128, 2), np.float32)
    p1[0:64, 0] = 1.0
    p1[64:128, 1] = 1.0
    c["pair1"] = p1.astype(ml_dtypes.bfloat16)
    bcm = np.zeros((2, 128), np.float32)
    bcm[0, 0:64] = 1.0
    bcm[1, 64:128] = 1.0
    c["bcmask"] = bcm
    c["onesr"] = np.ones((1, 128), np.float32)
    c["o1b"] = np.ones((128, 1), np.float32).astype(ml_dtypes.bfloat16)
    c["o1f"] = np.ones((128, 1), np.float32)
    return c


def kernel(L0_attn_x, L0_attn_w_qkv, L0_attn_w_proj,
           L0_ffn_w_gate, L0_ffn_w_up, L0_ffn_w_down, _dump=(), _raw=False):
    x = np.ascontiguousarray(np.asarray(L0_attn_x, np.float32))
    wqkv = np.ascontiguousarray(np.asarray(L0_attn_w_qkv, np.float32))
    wpj = np.ascontiguousarray(np.asarray(L0_attn_w_proj, np.float32))
    wgT = np.ascontiguousarray(np.asarray(L0_ffn_w_gate, np.float32).T)
    wuT = np.ascontiguousarray(np.asarray(L0_ffn_w_up, np.float32).T)
    wdT = np.ascontiguousarray(np.asarray(L0_ffn_w_down, np.float32).T)

    if _dump:
        nc = build_program(dump=_dump)
    else:
        if _NC_CACHE[0] is None:
            _NC_CACHE[0] = build_program()
        nc = _NC_CACHE[0]

    cs = _consts()
    in_maps = []
    for c in range(NC):
        xs = np.ascontiguousarray(x[:, c * NL:(c + 1) * NL, :]).reshape(TL, D)
        gi = np.zeros((64, 4), np.int32)
        for j in range(4):
            jj, h = divmod(j, 2)
            idx = 2 * c + jj
            gi[:, j] = idx * 128 + h * 64 + np.arange(64)
        m = {
            "x_tm": xs,
            "x_fmT": np.ascontiguousarray(xs.T),
            "wqkv": wqkv, "wproj": wpj,
            "wgT": wgT, "wuT": wuT, "wdT": wdT,
            "gidx": gi,
        }
        m.update(cs)
        in_maps.append(m)

    res = run_bass_kernel_spmd(nc, in_maps, core_ids=list(range(NC)))
    if _raw:
        return res
    full = np.empty((B, N, D), np.float32)
    for c in range(NC):
        r = np.asarray(res.results[c]["out"], np.float32)
        full[:, c * NL:(c + 1) * NL, :] = r.reshape(B, NL, D)
    return full


# version 15
# speedup vs baseline: 1.0325x; 1.0325x over previous
import numpy as np
import ml_dtypes
import contextlib
import concourse.bass as bass
import concourse.mybir as mybir
from concourse.bass import IndirectOffsetOnAxis
from concourse.bass_utils import run_bass_kernel_spmd

F32 = mybir.dt.float32
BF16 = mybir.dt.bfloat16
I32 = mybir.dt.int32
AF = mybir.ActivationFunctionType
ALU = mybir.AluOpType
AX = mybir.AxisListType

NC = 8
B = 2
N = 8192
D = 1024
NL = N // NC
TL = B * NL
NT = TL // 128
DI = 4096
KC = D // 128
ICN = DI // 128

_ENGINES = ("sync", "scalar", "vector", "tensor", "gpsimd")
_DMA_SEM = {"sync": "dS", "scalar": "dA", "gpsimd": "dG"}


class Sched:
    def __init__(self):
        self.count = {}
        self.ops = {e: [] for e in _ENGINES}

    def c(self, sem):
        return self.count.get(sem, 0)

    def emit(self, engine, fn, waits=(), inc=None, inc_amt=1):
        waits = [(s, v) for (s, v) in waits if v > 0]
        for s, v in waits:
            assert v <= self.c(s), f"wait {s}>={v} before inc (have {self.c(s)})"
        self.ops[engine].append((tuple(waits), fn, inc, inc_amt))
        if inc is not None:
            self.count[inc] = self.c(inc) + inc_amt
            return self.count[inc]
        return None

    def dma(self, queue, out, in_, waits=()):
        return self.emit(queue, lambda e, o=out, i=in_: e.dma_start(out=o, in_=i),
                         waits=waits, inc=_DMA_SEM[queue], inc_amt=16)

    def dmat(self, queue, out, in_, waits=()):
        return self.emit(queue, lambda e, o=out, i=in_: e.dma_start_transpose(o, i),
                         waits=waits, inc=_DMA_SEM[queue], inc_amt=16)

    def build(self, block, sems):
        for eng_name in _ENGINES:
            ops = self.ops[eng_name]

            def mk(ops):
                def run(eng):
                    last = {}
                    for waits, fn, inc, amt in ops:
                        for s, v in waits:
                            if last.get(s, 0) < v:
                                eng.wait_ge(sems[s], v)
                                last[s] = v
                        ins = fn(eng)
                        if inc is not None:
                            ins.then_inc(sems[inc], amt)
                return run
            getattr(block, eng_name)(mk(ops))


def build_program(dump=()):
    nc = bass.Bass("TRN2", target_bir_lowering=False)
    dt = lambda n, s, d, k="Internal": nc.dram_tensor(n, s, d, kind=k).ap()

    x_tm = dt("x_tm", [TL, D], F32, "ExternalInput")
    x_fmT = dt("x_fmT", [D, TL], F32, "ExternalInput")
    wqkv = dt("wqkv", [D, 3 * D], F32, "ExternalInput")
    wproj = dt("wproj", [D, D], F32, "ExternalInput")
    wgT = dt("wgT", [D, DI], F32, "ExternalInput")
    wuT = dt("wuT", [D, DI], F32, "ExternalInput")
    wdT = dt("wdT", [DI, D], F32, "ExternalInput")
    ident_d = dt("ident", [128, 128], BF16, "ExternalInput")
    id3_d = dt("id3", [64, 192], F32, "ExternalInput")
    pair1_d = dt("pair1", [128, 2], BF16, "ExternalInput")
    bcmask_d = dt("bcmask", [2, 128], F32, "ExternalInput")
    onesr_d = dt("onesr", [1, 128], F32, "ExternalInput")
    o1b_d = dt("o1b", [128, 1], BF16, "ExternalInput")
    o1f_d = dt("o1f", [128, 1], F32, "ExternalInput")
    gidx_d = dt("gidx", [64, 4], I32, "ExternalInput")
    out_d = dt("out", [TL, D], F32, "ExternalOutput")

    y1_dr = dt("y1_dr", [TL, D], F32)
    y1q_dr = dt("y1q_dr", [TL, D], BF16)
    y1qfm_dr = dt("y1qfm_dr", [D, TL], BF16)
    hq_dr = dt("hq_dr", [TL, DI], BF16)
    wgq_dr = dt("wgq_dr", [D, DI], BF16)
    wuq_dr = dt("wuq_dr", [D, DI], BF16)
    wdq_dr = dt("wdq_dr", [DI, D], BF16)
    k2_dr = dt("k2_dr", [2048, 64], BF16)
    ag_in = dt("ag_in", [16, D], BF16)
    ag_out = dt("ag_out", [128, D], BF16)
    ar_in = dt("ar_in", [2048, 132], F32)
    ar_out = dt("ar_out", [2048, 132], F32)
    w2_in = dt("w2_in", [256, 64], BF16)
    w2_out = dt("w2_out", [2048, 64], BF16)

    S = Sched()
    semnames = ["dS", "dA", "dG", "mm", "ev", "ac", "gp", "cc1", "cc2", "cc3"]

    with contextlib.ExitStack() as st:
        ent = st.enter_context
        sems = {n: ent(nc.semaphore(n)) for n in semnames}
        sb = lambda n, s, d: ent(nc.sbuf_tensor(n, s, d))

        S1 = sb("S1", [128, 16384], BF16)
        S2 = sb("S2", [128, 16384], BF16)
        S3 = sb("S3", [128, 16384], BF16)
        S4 = sb("S4", [128, 16384], BF16)
        scC = sb("scC", [128, 8192], BF16)
        scB = sb("scB", [128, 2048], BF16)
        scD = sb("scD", [128, 2048], BF16)
        scE = sb("scE", [128, 2048], BF16)
        CF1 = sb("CF1", [128, 4096], F32)
        CFD = sb("CFD", [128, 2048], F32)
        EI = sb("EI", [128, 512], I32)
        EI2 = sb("EI2", [128, 512], I32)
        bcs = sb("bcs", [128, 512], F32)
        zc = CF1[0:2, 0:512]
        rz = CF1[0:2, 512:1024]
        c_id = sb("c_id", [128, 128], BF16)
        c_id3 = sb("c_id3", [64, 192], F32)
        c_p1 = sb("c_p1", [128, 2], BF16)
        c_bc = sb("c_bc", [2, 128], F32)
        c_or = sb("c_or", [1, 128], F32)
        c_o1b = sb("c_o1b", [128, 1], BF16)
        c_o1f = sb("c_o1f", [128, 1], F32)
        gidx = sb("gidx_s", [64, 4], I32)
        xlpf = sb("xlpf", [128, 128], F32)
        xlp = sb("xlp", [128, 128], BF16)
        xlt = sb("xlt", [128, 128], BF16)
        xl_g = sb("xl_g", [128, 1024], BF16)
        xl_fm = sb("xl_fm", [128, 1024], BF16)
        ql_fm = sb("ql_fm", [128, 1024], BF16)
        kl_fm = sb("kl_fm", [128, 1024], BF16)
        bdq = [sb(f"bdq{i}", [128, 128], BF16) for i in range(2)]
        bdk = [sb(f"bdk{i}", [128, 128], BF16) for i in range(2)]
        bdw = [sb(f"bdw{i}", [128, 128], BF16) for i in range(2)]
        bsm = [sb(f"bsm{i}", [128, 128], BF16) for i in range(2)]
        k3v = [sb(f"k3v{i}", [128, 128], BF16) for i in range(2)]
        azb = [sb(f"azb{i}", [128, 132], F32) for i in range(2)]
        k2e = sb("k2e", [128, 64], F32)
        k2s = sb("k2s", [128, 64], BF16)
        nw = [{k: sb(f"nw_{k}{j}", [64, 64], BF16) for k in
               ("k2b", "k2t", "vk", "vt", "ptb", "tt")} for j in range(4)]
        dnc = [sb(f"dnc{j}", [64, 1], F32) for j in range(4)]
        colnames = ["sum", "negm", "ssq", "ssqb", "sd", "rstd", "rmax", "sa",
                    "dqg", "dqu", "dqd", "s2", "tmp", "tmp2",
                    "wsgs", "wsgd", "wsus", "wsud", "wsds", "wsdd",
                    "accg", "accu", "accd", "rowr"]
        cols = {n: sb("col_" + n, [128, 1], F32) for n in colnames}
        rsa_all = sb("rsa_all", [128, 16], F32)
        rsa2_all = sb("rsa2_all", [128, 16], F32)
        rsa3_all = sb("rsa3_all", [128, 16], F32)
        rs2d_all = sb("rs2d_all", [128, 16], F32)
        rs2_all = sb("rs2_all", [128, 16], F32)
        t11 = sb("t11", [1, 16], F32)

        pb = [ent(nc.psum_tensor(f"pb{i}", [128, 512], F32)) for i in range(6)]
        pA = ent(nc.psum_tensor("pA", [128, 132], F32))
        pT = ent(nc.psum_tensor("pT", [128, 128], BF16))

        blk = ent(nc.Block())

        def MM(ps_ap, lhsT, rhs, start, stop, waits=()):
            return S.emit("tensor",
                          lambda e, a=ps_ap, l=lhsT, r=rhs, s1=start, s2=stop:
                          e.matmul(a, l, r, start=s1, stop=s2),
                          waits=waits, inc="mm")

        def TRP(ps_ap, in_ap, id_ap, waits=()):
            return S.emit("tensor",
                          lambda e, a=ps_ap, i=in_ap, d=id_ap:
                          e.transpose(a, i, d), waits=waits, inc="mm")

        def EV(fn, waits=()):
            return S.emit("vector", fn, waits=waits, inc="ev")

        def ACT(fn, waits=()):
            return S.emit("scalar", fn, waits=waits, inc="ac")

        def GP(fn, waits=()):
            return S.emit("gpsimd", fn, waits=waits, inc="gp")

        def DRAIN(engine, sem):
            return S.emit(engine, lambda e: e.drain(), inc=sem)

        bank_free = {i: ("ev", 0) for i in list(range(6)) + ["pA", "pT"]}

        xfm = lambda k: S1[:, k * 2048:(k + 1) * 2048]
        Qf = lambda p: S2[:, p * 2048:(p + 1) * 2048]
        Kf = lambda p: S3[:, p * 2048:(p + 1) * 2048]
        Vt = lambda t: S4[:, t * 1024:(t + 1) * 1024]
        exp1 = lambda i: S1[:, i * 1024:(i + 1) * 1024]
        SVf = lambda p: S2[:, p * 2048:(p + 1) * 2048]
        wv16 = lambda k: S3[:, k * 1024:(k + 1) * 1024]

        S.dma("sync", c_id[:], ident_d)
        S.dma("sync", c_id3[:], id3_d)
        S.dma("sync", c_p1[:], pair1_d)
        S.dma("sync", c_bc[:], bcmask_d)
        S.dma("sync", c_or[:], onesr_d)
        S.dma("sync", c_o1b[:], o1b_d)
        S.dma("sync", c_o1f[:], o1f_d)
        d_consts = S.dma("sync", gidx[:], gidx_d)

        d_xfm = S.dma("gpsimd", S1[:].rearrange("p (c t) -> p c t", c=KC),
                     x_fmT.rearrange("(c p) t -> p c t", p=128))
        d_wv = S.dma("gpsimd", S3[:, 0:8192].rearrange("p (k q) -> p k q", k=KC),
                     wqkv[:, 2048:3072].rearrange("(k p) q -> p k q", p=128))

        for k in range(KC):
            for t in range(NT):
                ACT(lambda e, k=k, t=t: e.activation(
                    scB[:, 0:128], xfm(k)[:, t * 128:(t + 1) * 128], AF.Copy,
                    scale=1.0 / 128, accum_out=xlpf[:, k * 16 + t:k * 16 + t + 1]),
                    waits=[("dG", d_xfm)] if (k == 0 and t == 0)
                    else [("ac", S.c("ac"))])
        ac_xlp = S.c("ac")
        ev_xlp = EV(lambda e: e.tensor_copy(xlp[:], xlpf[:]),
                    waits=[("ac", ac_xlp)])
        w0, c0 = bank_free["pT"]
        m_t = TRP(pT[:], xlp[:], c_id[:],
                  waits=[("ev", ev_xlp), ("dS", d_consts), (w0, c0)])
        e_t = EV(lambda e: e.tensor_copy(xlt[:], pT[:]), waits=[("mm", m_t)])
        bank_free["pT"] = ("ev", e_t)
        d_ag = S.dma("sync", ag_in.rearrange("t (k p) -> k t p", p=128),
                     xlt[:], waits=[("ev", e_t)])
        S.emit("gpsimd",
               lambda e: e.collective_compute(
                   "AllGather", ALU.bypass, replica_groups=[list(range(NC))],
                   ins=[ag_in.opt()], outs=[ag_out.opt()]),
               waits=[("dS", d_ag)], inc="cc1")
        d_xlg = S.dma("sync", xl_g[:], ag_out, waits=[("cc1", 1)])

        for k in range(KC):
            w0, c0 = bank_free["pT"]
            m1 = TRP(pT[:], xl_g[:, k * 128:(k + 1) * 128], c_id[:],
                     waits=[("dS", d_xlg), (w0, c0)])
            e1 = EV(lambda e, k=k: e.tensor_copy(
                xl_fm[:, k * 128:(k + 1) * 128]
                .rearrange("p (b r s) -> p r b s", b=2, r=8, s=8),
                pT[:].rearrange("p (r b s) -> p r b s", r=8, b=2, s=8)),
                waits=[("mm", m1)])
            bank_free["pT"] = ("ev", e1)
        ev_xlfm = S.c("ev")

        def pass1_gen(wT, accname, pieces, buf_tensor, bufw, first_ev_wait,
                      out):
            prev_ev = {}
            acc = cols[accname]
            for i, (r0, cc, wdt) in enumerate(pieces):
                half = i % 2
                buf = buf_tensor[:, half * bufw:half * bufw + wdt]
                waits = []
                if i >= 2:
                    waits.append(("ev", prev_ev[i - 2]))
                elif first_ev_wait[0]:
                    waits.append(("ev", first_ev_wait[0]))
                d1 = S.dma("sync", buf, wT[r0:r0 + 128, cc:cc + wdt], waits=waits)
                gr = EV(lambda e, b=buf: e.tensor_reduce(
                    cols["tmp2"][:], b, AX.X, ALU.add, True),
                    waits=[("dS", d1)])
                if i == 0:
                    g2 = EV(lambda e, a=acc: e.tensor_copy(a[:], cols["tmp2"][:]),
                            waits=[("ev", gr)])
                else:
                    g2 = EV(lambda e, a=acc: e.tensor_add(a[:], a[:],
                                                          cols["tmp2"][:]),
                            waits=[("ev", gr)])
                prev_ev[i] = g2
                out[0] = g2
                yield

        p_gu = [(r * 128, ch * 2048, 2048) for r in range(8) for ch in range(2)]
        p_d = [(r * 128, 0, 1024) for r in range(32)]


        def ws_finalize(accname, numel, scol, dcol, slot, evwait):
            w5, c5 = bank_free[5]
            m1 = MM(pb[5][0:1, 0:1], cols[accname][:], c_o1f[:],
                    start=True, stop=True, waits=[("ev", evwait), (w5, c5)])
            e1 = EV(lambda e, s=slot: e.tensor_scalar(
                t11[0:1, s * 2:s * 2 + 1], pb[5][0:1, 0:1],
                1.0 / numel, 1e-5, ALU.mult, ALU.max), waits=[("mm", m1)])
            e2 = EV(lambda e, s=slot: e.reciprocal(
                t11[0:1, s * 2 + 1:s * 2 + 2], t11[0:1, s * 2:s * 2 + 1]),
                waits=[("ev", e1)])
            m2 = MM(pb[5][:, 0:1], c_or[:], t11[0:1, slot * 2:slot * 2 + 1],
                    start=True, stop=True, waits=[("ev", e2)])
            m3 = MM(pb[5][:, 1:2], c_or[:], t11[0:1, slot * 2 + 1:slot * 2 + 2],
                    start=True, stop=True)
            EV(lambda e, d=dcol: e.tensor_copy(cols[d][:], pb[5][:, 0:1]),
               waits=[("mm", m3)])
            e4 = EV(lambda e, s=scol: e.tensor_copy(cols[s][:], pb[5][:, 1:2]))
            bank_free[5] = ("ev", e4)
            return e4


        def pass2_gen(wT, scol, q_dr, pieces, wswait, buf_tensor, bufw,
                      first_gp_wait, out, first_ev_wait=0):
            prev = {}
            for i, (r0, cc, wdt) in enumerate(pieces):
                half = i % 2
                buf = buf_tensor[:, half * bufw:half * bufw + wdt]
                waits = [("ev", wswait[0])]
                if i >= 2:
                    waits.append(("ev", prev[i - 2][0]))
                elif first_gp_wait[0]:
                    waits.append(("ev", first_gp_wait[0]))
                if i < 2 and first_ev_wait:
                    waits.append(("ev", first_ev_wait))
                d1 = S.dma("sync", buf, wT[r0:r0 + 128, cc:cc + wdt], waits=waits)
                glast = None
                for q in range(wdt // 512):
                    qw = [("dS", d1)]
                    if i >= 1 and q == 0:
                        qw.append(("dS", prev[i - 1][1]))
                    g1 = EV(lambda e, b=buf, q=q, s=scol: e.tensor_scalar(
                        EI2[:], b[:, q * 512:(q + 1) * 512], cols[s][:], None,
                        ALU.mult), waits=qw)
                    glast = EV(lambda e, q=q: e.tensor_scalar(
                        scD[:, q * 512:(q + 1) * 512], EI2[:], 1.0, -1.0,
                        ALU.min, ALU.max), waits=[("ev", g1)])
                glast = DRAIN("vector", "ev")
                d2 = S.dma("sync", q_dr[r0:r0 + 128, cc:cc + wdt],
                           scD[:, 0:wdt], waits=[("ev", glast)])
                prev[i] = (glast, d2)
                out[0] = glast
                out[1] = d2
                yield

        p1g_out = [0]
        p1u_out = [0]
        p1d_out = [0]
        wsg_box = [0]
        wsu_box = [0]
        wsd_box = [0]
        p2g_out = [0, 0]
        p2u_out = [0, 0]
        p2d_out = [0, 0]
        p_g2 = [(r * 128, ch * 1024, 1024) for r in range(8) for ch in range(4)]

        def wq1_fn():
            yield from pass1_gen(wgT, "accg", p_gu, CF1, 2048, [0], p1g_out)
            yield from pass1_gen(wuT, "accu", p_gu, CF1, 2048, p1g_out, p1u_out)
            yield from pass1_gen(wdT, "accd", p_d, CF1, 2048, p1u_out, p1d_out)
            wsg_box[0] = ws_finalize("accg", D * DI, "wsgs", "wsgd", 0,
                                     p1g_out[0])
            wsu_box[0] = ws_finalize("accu", D * DI, "wsus", "wsud", 1,
                                     p1u_out[0])
            wsd_box[0] = ws_finalize("accd", D * DI, "wsds", "wsdd", 2,
                                     p1d_out[0])
            yield

        wq1 = wq1_fn()

        def wq1_pump(n):
            for _ in range(n):
                next(wq1, None)

        def wq2_fn():
            yield from pass2_gen(wgT, "wsgs", wgq_dr, p_g2, wsg_box, CFD,
                                 1024, [0], p2g_out)
            yield from pass2_gen(wuT, "wsus", wuq_dr, p_g2, wsu_box, CFD,
                                 1024, [0], p2u_out)

        wq2 = [None]

        def wq2_pump(n):
            if not wsd_box[0]:
                return
            if wq2[0] is None:
                wq2[0] = wq2_fn()
            for _ in range(n):
                next(wq2[0], None)

        for t in range(NT):
            for g2 in range(2):
                bank = 4 + (t * 2 + g2) % 2
                w0, c0 = bank_free[bank]
                for k in range(KC):
                    mmid = MM(pb[bank][:, 0:512], xfm(k)[:, t * 128:(t + 1) * 128],
                              wv16(k)[:, g2 * 512:(g2 + 1) * 512],
                              start=(k == 0), stop=(k == KC - 1),
                              waits=([("dG", max(d_wv, d_xfm)), (w0, c0)]
                                     if k == 0 else ()))
                ev1 = EV(lambda e, t=t, g2=g2, b=bank: e.tensor_copy(
                    Vt(t)[:, g2 * 512:(g2 + 1) * 512], pb[b][:, 0:512]),
                    waits=[("mm", mmid)])
                bank_free[bank] = ("ev", ev1)
                wq1_pump(1)
        mm_v_end = S.c("mm")
        ev_v_end = S.c("ev")

        mm_qlkl = {}
        kt_d = dt("dbg_k2trace", [128, 128], F32, "ExternalOutput") \
            if "k2trace" in dump else None
        d_trc = [0]
        k2_iter = [0]
        mm_bdq2 = {0: 0, 1: 0}
        d_k2_prev = 0
        ev_k2_prev = 0
        last_round_mm = mm_v_end
        for rnd in range(2):
            dwq = S.dma("gpsimd", scC[:, 0:4096].rearrange("p (k q) -> p k q", k=KC),
                        wqkv[:, rnd * 512:rnd * 512 + 512]
                        .rearrange("(k p) q -> p k q", p=128),
                        waits=[("mm", last_round_mm)])
            dwk = S.dma("gpsimd", scC[:, 4096:8192].rearrange("p (k q) -> p k q", k=KC),
                        wqkv[:, 1024 + rnd * 512:1024 + rnd * 512 + 512]
                        .rearrange("(k p) q -> p k q", p=128))
            for half in range(2):
                dwait = dwq if half == 0 else dwk
                for j in range(4):
                    p = rnd * 4 + j
                    wsl = scC[:, half * 4096 + j * 1024:half * 4096 + (j + 1) * 1024]
                    for g in range(4):
                        bank = g
                        w0, c0 = bank_free[bank]
                        for k in range(KC):
                            kw = []
                            if k == 0:
                                kw = [("dG", dwait), (w0, c0)]
                            mmid = MM(pb[bank][:, 0:512],
                                      wsl[:, k * 128:(k + 1) * 128],
                                      xfm(k)[:, g * 512:(g + 1) * 512],
                                      start=(k == 0), stop=(k == KC - 1),
                                      waits=kw)
                        dst = Qf(p) if half == 0 else Kf(p)
                        scale = 0.125 if half == 0 else 1.0
                        a1 = ACT(lambda e, d=dst, g=g, b=bank, s=scale:
                                 e.activation(d[:, g * 512:(g + 1) * 512],
                                              pb[b][:, 0:512], AF.Copy, scale=s),
                                 waits=[("mm", mmid)])
                        bank_free[bank] = ("ac", a1)
                    wpa, cpa = bank_free["pA"]
                    for k in range(KC):
                        mmid = MM(pA[:, 0:128], wsl[:, k * 128:(k + 1) * 128],
                                  xl_fm[:, k * 128:(k + 1) * 128],
                                  start=(k == 0), stop=(k == KC - 1),
                                  waits=([("ev", ev_xlfm), (wpa, cpa)]
                                         if k == 0 else ()))
                    dst2 = ql_fm if half == 0 else kl_fm
                    scale = 0.125 if half == 0 else 1.0
                    e1 = EV(lambda e, d=dst2, p=p, s=scale: e.tensor_scalar(
                        d[:, p * 128:(p + 1) * 128], pA[:, 0:128], s, None,
                        ALU.mult), waits=[("mm", mmid)])
                    bank_free["pA"] = ("ev", e1)
                    mm_qlkl[(half, p)] = e1
                    wq1_pump(2)
                last_round_mm = S.c("mm")

            for j in range(4):
                p = rnd * 4 + j
                for b in range(2):
                    idx = b * 8 + p
                    jb = k2_iter[0] % 2
                    k2_iter[0] += 1
                    wql = mm_qlkl[(0, p)]
                    wkl = mm_qlkl[(1, p)]
                    evm = EV(lambda e, jb=jb: e.memset(bdq[jb][:], 0.0),
                             waits=[("ev", max(wql, wkl)),
                                    ("mm", mm_bdq2[jb])])
                    EV(lambda e, jb=jb, p=p, b=b: e.tensor_copy(
                        bdq[jb][0:64, 0:64],
                        ql_fm[0:64, p * 128 + b * 64:p * 128 + b * 64 + 64]))
                    e2 = EV(lambda e, jb=jb, p=p, b=b: e.tensor_copy(
                        bdq[jb][64:128, 64:128],
                        ql_fm[64:128, p * 128 + b * 64:p * 128 + b * 64 + 64]))
                    bank = 4
                    w0, c0 = bank_free[bank]
                    m1 = MM(pb[bank][:, 0:64], bdq[jb][:],
                            kl_fm[:, p * 128 + b * 64:p * 128 + b * 64 + 64],
                            start=True, stop=True, waits=[("ev", e2), (w0, c0)])
                    a1 = ACT(lambda e, b2=bank: e.activation(
                        k2e[:], pb[b2][:, 0:64], AF.Exp,
                        accum_out=cols["rowr"][:]),
                        waits=[("mm", m1), ("ev", ev_k2_prev)])
                    bank_free[bank] = ("ac", a1)
                    er = EV(lambda e: e.reciprocal(cols["tmp"][:],
                                                   cols["rowr"][:]),
                            waits=[("ac", a1), ("dG", d_k2_prev)])
                    e4 = EV(lambda e: e.tensor_scalar(
                        k2s[:], k2e[:], cols["tmp"][:], None, ALU.mult),
                        waits=[("ev", er)])
                    if "k2trace" in dump:
                        EV(lambda e: e.tensor_copy(
                            azb[0][:, 0:1], cols["rowr"][:]),
                            waits=[("dA", d_trc[0])])
                        EV(lambda e: e.tensor_copy(
                            azb[0][:, 1:2], cols["tmp"][:]))
                        EV(lambda e: e.tensor_copy(azb[0][:, 2:4], k2e[:, 0:2]))
                        e4 = EV(lambda e: e.tensor_copy(
                            azb[0][:, 4:6], k2s[:, 0:2]))
                        d_trc[0] = S.dma("scalar",
                                         kt_d[:, idx * 8:idx * 8 + 8],
                                         azb[0][:, 0:8], waits=[("ev", e4)])
                    e4d = DRAIN("vector", "ev")
                    ev_k2_prev = e4d
                    mm_bdq2[jb] = S.c("mm")
                    d_k2_prev = S.dma("gpsimd", k2_dr[idx * 128:(idx + 1) * 128, :],
                                      k2s[:], waits=[("ev", e4d)])
        d_k2st_all = d_k2_prev
        mm_qkv_end = S.c("mm")
        ac_qkv_end = S.c("ac")

        d_gath = []
        for j in range(4):
            d_gath.append(S.emit(
                "gpsimd",
                lambda e, j=j: e.indirect_dma_start(
                    nw[j]["k2b"][:], None, k2_dr,
                    IndirectOffsetOnAxis(gidx[:, j:j + 1], 0)),
                waits=[("dG", d_k2st_all)], inc="dG", inc_amt=16))


        d_wpj = S.dma("gpsimd", scC[:].rearrange("p (k q) -> p k q", k=KC),
                      wproj.rearrange("(k p) q -> p k q", p=128),
                      waits=[("mm", mm_qkv_end)])


        newton_done_ev = [None]

        def newton_gen():
            for j in range(4):
                w0, c0 = bank_free["pT"]
                m1 = TRP(pT[0:64, 0:64], nw[j]["k2b"][:], c_id[0:64, 0:64],
                         waits=[("dG", d_gath[j]), (w0, c0)])
                e1 = EV(lambda e, j=j: e.tensor_copy(nw[j]["k2t"][:],
                                                     pT[0:64, 0:64]),
                        waits=[("mm", m1)])
                bank_free["pT"] = ("ev", e1)
                yield
            for j in range(4):
                bankid = 4 if j < 2 else 5
                co = 256 + (j % 2) * 128
                w0, c0 = bank_free[bankid]
                m1 = MM(pb[bankid][0:1, co:co + 64], c_o1b[0:64, :],
                        nw[j]["k2b"][:], start=True, stop=True,
                        waits=[("dG", d_gath[j]), (w0, c0)])
                e1 = EV(lambda e, b=bankid, co=co, j=j: e.reduce_max(
                    t11[0:1, 8 + j:9 + j], pb[b][0:1, co:co + 64], AX.X),
                    waits=[("mm", m1)])
                e2 = EV(lambda e, j=j: e.reciprocal(
                    t11[0:1, 12 + j:13 + j], t11[0:1, 8 + j:9 + j]),
                    waits=[("ev", e1)])
                m2 = MM(pb[bankid][0:64, co + 64:co + 65], c_or[0:1, 0:64],
                        t11[0:1, 12 + j:13 + j], start=True, stop=True,
                        waits=[("ev", e2)])
                e3d = EV(lambda e, j=j, b=bankid, co=co: e.tensor_copy(
                    dnc[j][:], pb[b][0:64, co + 64:co + 65]),
                    waits=[("mm", m2)])
                EV(lambda e, j=j: e.tensor_scalar(
                    nw[j]["vk"][:], nw[j]["k2t"][:], dnc[j][:], None, ALU.mult),
                    waits=[("ev", e3d)])
                e5 = EV(lambda e, j=j: e.tensor_scalar(
                    nw[j]["vt"][:], nw[j]["k2b"][:], dnc[j][:], None, ALU.mult))
                bank_free[bankid] = ("ev", e5)
                yield
            for it in range(6):
                for pair in range(2):
                    bankid = 4 + pair
                    w0, c0 = bank_free[bankid]
                    mm1 = 0
                    for jj in range(2):
                        j = pair * 2 + jj
                        co = jj * 128
                        MM(pb[bankid][0:64, co:co + 64], nw[j]["k2t"][:],
                           nw[j]["vk"][:], start=True, stop=True,
                           waits=[(w0, c0)] if jj == 0 else ())
                        mm1 = MM(pb[bankid][0:64, co + 64:co + 128],
                                 nw[j]["vk"][:], nw[j]["k2t"][:],
                                 start=True, stop=True)
                    ee = 0
                    for jj in range(2):
                        j = pair * 2 + jj
                        co = jj * 128
                        EV(lambda e, j=j, b=bankid, co=co: e.scalar_tensor_tensor(
                            nw[j]["tt"][:], pb[b][0:64, co:co + 64], -1.0,
                            c_id3[:, 0:64], ALU.mult, ALU.add),
                            waits=[("mm", mm1)])
                        ee = EV(lambda e, j=j, b=bankid, co=co: e.tensor_copy(
                            nw[j]["ptb"][:], pb[b][0:64, co + 64:co + 128]))
                    yield
                    mm2 = 0
                    for jj in range(2):
                        j = pair * 2 + jj
                        co = jj * 128
                        mm2 = MM(pb[bankid][0:64, co:co + 64], nw[j]["ptb"][:],
                                 nw[j]["tt"][:], start=True, stop=True,
                                 waits=[("ev", ee)] if jj == 0 else ())
                    for jj in range(2):
                        j = pair * 2 + jj
                        co = jj * 128
                        ee = EV(lambda e, j=j, b=bankid, co=co:
                                e.scalar_tensor_tensor(
                                    nw[j]["tt"][:], pb[b][0:64, co:co + 64],
                                    -1.0, c_id3[:, 64:128], ALU.mult, ALU.add),
                                waits=[("mm", mm2)])
                    yield
                    for jj in range(2):
                        j = pair * 2 + jj
                        co = jj * 128
                        mm2 = MM(pb[bankid][0:64, co:co + 64], nw[j]["ptb"][:],
                                 nw[j]["tt"][:], start=True, stop=True,
                                 waits=[("ev", ee)] if jj == 0 else ())
                    for jj in range(2):
                        j = pair * 2 + jj
                        co = jj * 128
                        ee = EV(lambda e, j=j, b=bankid, co=co:
                                e.scalar_tensor_tensor(
                                    nw[j]["tt"][:], pb[b][0:64, co:co + 64],
                                    -1.0, c_id3[:, 128:192], ALU.mult, ALU.add),
                                waits=[("mm", mm2)])
                    yield
                    for jj in range(2):
                        j = pair * 2 + jj
                        co = jj * 128
                        MM(pb[bankid][0:64, co:co + 64], nw[j]["vt"][:],
                           nw[j]["tt"][:], start=True, stop=True,
                           waits=[("ev", ee)] if jj == 0 else ())
                        mm2 = MM(pb[bankid][0:64, co + 64:co + 128],
                                 nw[j]["tt"][:], nw[j]["vt"][:],
                                 start=True, stop=True)
                    for jj in range(2):
                        j = pair * 2 + jj
                        co = jj * 128
                        EV(lambda e, j=j, b=bankid, co=co: e.tensor_scalar(
                            nw[j]["vk"][:], pb[b][0:64, co:co + 64], 0.25,
                            None, ALU.mult), waits=[("mm", mm2)])
                        ee = EV(lambda e, j=j, b=bankid, co=co: e.tensor_scalar(
                            nw[j]["vt"][:], pb[b][0:64, co + 64:co + 128],
                            0.25, None, ALU.mult))
                    bank_free[bankid] = ("ev", ee)
                    yield
            newton_done_ev[0] = S.c("ev")

        ngen = newton_gen()

        def pump(n):
            for _ in range(n):
                if newton_done_ev[0] is not None:
                    return
                next(ngen, None)

        def pump_all():
            while newton_done_ev[0] is None:
                if next(ngen, "X") == "X":
                    break

        mm_bdk = {0: 0, 1: 0}
        for idx in range(16):
            b, p = divmod(idx, 8)
            jb = idx % 2
            evm4 = EV(lambda e, jb=jb: e.memset(bdk[jb][:], 0.0),
                      waits=[("mm", mm_bdk[jb]), ("ev", mm_qlkl[(1, p)])])
            EV(lambda e, jb=jb, p=p, b=b: e.tensor_copy(
                bdk[jb][0:64, 0:64],
                kl_fm[0:64, p * 128 + b * 64:p * 128 + b * 64 + 64]))
            e2 = EV(lambda e, jb=jb, p=p, b=b: e.tensor_copy(
                bdk[jb][64:128, 64:128],
                kl_fm[64:128, p * 128 + b * 64:p * 128 + b * 64 + 64]))
            for g in range(2):
                bank = (idx * 2 + g) % 2
                w0, c0 = bank_free[bank]
                m1 = MM(pb[bank][:, 0:512], bdk[jb][:],
                        Qf(p)[:, b * 1024 + g * 512:b * 1024 + (g + 1) * 512],
                        start=True, stop=True,
                        waits=[("ev", e2), (w0, c0), ("ac", ac_qkv_end)])
                a1 = ACT(lambda e, idx=idx, g=g, b2=bank: e.activation(
                    exp1(idx)[:, g * 512:(g + 1) * 512], pb[b2][:, 0:512],
                    AF.Exp), waits=[("mm", m1), ("mm", mm_qkv_end)])
                bank_free[bank] = ("ac", a1)
            mm_bdk[jb] = S.c("mm")
            pump(2)
            wq1_pump(4)
        ac_exp1_end = S.c("ac")

        mm_bdq5 = {0: 0, 1: 0}
        mm_a = {0: 0, 1: 0}
        d_az = 0
        d_az_prev = {0: 0, 1: 0}
        for idx in range(16):
            b, p = divmod(idx, 8)
            jb = idx % 2
            evm5 = EV(lambda e, jb=jb: e.memset(bdq[jb][:], 0.0),
                      waits=[("mm", mm_bdq5[jb]), ("ev", mm_qlkl[(0, p)])])
            EV(lambda e, jb=jb, p=p, b=b: e.tensor_copy(
                bdq[jb][0:64, 0:64],
                ql_fm[0:64, p * 128 + b * 64:p * 128 + b * 64 + 64]))
            e2 = EV(lambda e, jb=jb, p=p, b=b: e.tensor_copy(
                bdq[jb][64:128, 64:128],
                ql_fm[64:128, p * 128 + b * 64:p * 128 + b * 64 + 64]))
            pa_key = "pA" if idx % 2 == 0 else 2
            pa_ap = pA if idx % 2 == 0 else pb[2]
            wA, cA = bank_free[pa_key]
            m2 = 0
            for quad in range(2):
                bank = quad % 2
                half = quad % 2
                w0, c0 = bank_free[bank]
                m1 = 0
                for q in range(4):
                    t8 = quad * 4 + q
                    kw = []
                    if q == 0:
                        kw = [("ev", e2), (w0, c0), ("ac", ac_qkv_end)]
                    m1 = MM(pb[bank][:, q * 128:(q + 1) * 128],
                            Kf(p)[:, b * 1024 + t8 * 128:b * 1024 + (t8 + 1) * 128],
                            bdq[jb][:], start=True, stop=True, waits=kw)
                a1 = ACT(lambda e, b2=bank, h=half: e.activation(
                    scB[:, h * 1024:h * 1024 + 512], pb[b2][:, 0:512], AF.Exp),
                    waits=[("mm", m1), ("mm", mm_a[half])])
                bank_free[bank] = ("ac", a1)
                for q in range(4):
                    t8 = quad * 4 + q
                    tglob = b * 8 + t8
                    kw = [("ac", a1)]
                    if t8 == 0:
                        kw += [(wA, cA), ("ev", ev_v_end)]
                    MM(pa_ap[:, 0:128],
                       scB[:, half * 1024 + q * 128:half * 1024 + (q + 1) * 128],
                       Vt(tglob)[:, p * 128:(p + 1) * 128],
                       start=(t8 == 0), stop=(t8 == 7), waits=kw)
                    m2 = MM(pa_ap[:, 128:129],
                            scB[:, half * 1024 + q * 128:half * 1024 + (q + 1) * 128],
                            c_o1b[:], start=False, stop=(t8 == 7))
                mm_a[half] = S.c("mm")
            mm_bdq5[jb] = S.c("mm")
            eaz = EV(lambda e, i=idx % 2, a=pa_ap: e.tensor_copy(
                azb[i][:, 0:129], a[:, 0:129]),
                     waits=[("mm", m2), ("dA", max(d_az_prev[idx % 2],
                                                   d_trc[0]))])
            bank_free[pa_key] = ("ev", eaz)
            d_az = S.dma("scalar", ar_in[idx * 128:(idx + 1) * 128, 0:129],
                         azb[idx % 2][:, 0:129], waits=[("ev", eaz)])
            d_az_prev[idx % 2] = d_az
            pump(3)
            wq1_pump(4)
            wq2_pump(2)
        pump_all()
        wq1_pump(10 ** 6)
        mm_p5_end = S.c("mm")

        d_w2 = 0
        for j in range(4):
            d_w2 = S.dma("scalar", w2_in[j * 64:(j + 1) * 64, :], nw[j]["vt"][:],
                         waits=[("ev", newton_done_ev[0])])
        S.emit("gpsimd",
               lambda e: e.collective_compute(
                   "AllReduce", ALU.add, replica_groups=[list(range(NC))],
                   ins=[ar_in.opt()], outs=[ar_out.opt()]),
               waits=[("dA", d_az)], inc="cc2")
        S.emit("gpsimd",
               lambda e: e.collective_compute(
                   "AllGather", ALU.bypass, replica_groups=[list(range(NC))],
                   ins=[w2_in.opt()], outs=[w2_out.opt()]),
               waits=[("dA", d_w2)], inc="cc3")

        p_u2 = [(r * 128, ch * 1024, 1024) for r in range(8) for ch in range(4)]


        ev_azread = {0: 0, 1: 0}
        mm_bdwread = {0: 0, 1: 0}
        ev_sv_final = 0
        mm_rz_prev = 0
        for idx in range(16):
            b, p = divmod(idx, 8)
            i2 = idx % 2
            r0 = (idx // 2) * 256 + (idx % 2) * 128
            d1 = S.dma("scalar", azb[i2][:, 0:132],
                       ar_out[idx * 128:(idx + 1) * 128, :],
                       waits=[("cc2", 1), ("ev", ev_azread[i2])])
            d2a = S.dma("scalar", bdw[i2][0:64, 0:64], w2_out[r0:r0 + 64, :],
                        waits=[("cc3", 1), ("mm", mm_bdwread[i2])])
            d2b = S.dma("scalar", bdw[i2][64:128, 64:128],
                        w2_out[r0 + 64:r0 + 128, :])
            if idx < 2:
                EV(lambda e, i2=i2: e.memset(bdw[i2][0:64, 64:128], 0.0))
                EV(lambda e, i2=i2: e.memset(bdw[i2][64:128, 0:64], 0.0))
                EV(lambda e, i2=i2: e.memset(bsm[i2][:], 0.0))
            e1 = EV(lambda e, i2=i2: e.reciprocal(cols["tmp"][:],
                                                  azb[i2][:, 128:129]),
                    waits=[("dA", d1)])
            e1b = EV(lambda e, i2=i2: e.tensor_scalar(
                k3v[i2][:], azb[i2][:, 0:128], cols["tmp"][:], None, ALU.mult),
                waits=[("ev", e1)])
            ev_azread[i2] = e1b
            bank = 2
            w0, c0 = bank_free[bank]
            m1 = MM(pb[bank][:, 0:128], bdw[i2][:], k3v[i2][:],
                    start=True, stop=True,
                    waits=[("ev", e1b), ("dA", d2b), (w0, c0),
                           ("mm", mm_bdwread[i2])])
            e3 = EV(lambda e, b2=bank, i2=i2: e.tensor_copy(
                bsm[i2][0:64, 0:64], pb[b2][0:64, 0:64]), waits=[("mm", m1)])
            e4 = EV(lambda e, b2=bank, i2=i2: e.tensor_copy(
                bsm[i2][64:128, 64:128], pb[b2][64:128, 64:128]))
            bank_free[bank] = ("ev", e4)
            for g in range(2):
                bank = 3
                w0, c0 = bank_free[bank]
                m2b = MM(pb[bank][0:2, 0:512], c_p1[:],
                         exp1(idx)[:, g * 512:(g + 1) * 512],
                         start=True, stop=True,
                         waits=[("ac", ac_exp1_end), (w0, c0)])
                e5 = EV(lambda e, b2=bank: e.tensor_copy(zc[:],
                                                         pb[b2][0:2, 0:512]),
                        waits=[("mm", m2b)] +
                              ([("ev", p1d_out[0])] if idx == 0 and g == 0 else []))
                e6 = EV(lambda e: e.reciprocal(rz[:], zc[:]),
                        waits=[("mm", mm_rz_prev), ("ev", e5)])
                w4, c4 = bank_free[4]
                m3 = MM(pb[4][:, 0:512], c_bc[:], rz[:],
                        start=True, stop=True, waits=[("ev", e6), (w4, c4)])
                mm_rz_prev = m3
                e7 = EV(lambda e: e.tensor_copy(bcs[:], pb[4][:, 0:512]),
                        waits=[("mm", m3)])
                bank_free[4] = ("ev", e7)
                bank_free[bank] = ("ev", e5)
                bank = (idx * 2 + g) % 2
                w0, c0 = bank_free[bank]
                m4 = MM(pb[bank][:, 0:512], bsm[i2][:],
                        exp1(idx)[:, g * 512:(g + 1) * 512],
                        start=True, stop=True, waits=[("ev", e4), (w0, c0)])
                e8 = EV(lambda e, p=p, b=b, g=g, b2=bank: e.tensor_tensor(
                    SVf(p)[:, b * 1024 + g * 512:b * 1024 + (g + 1) * 512],
                    pb[b2][:, 0:512], bcs[:], ALU.mult),
                    waits=[("mm", m4), ("ev", e7)])
                bank_free[bank] = ("ev", e8)
            mm_bdwread[i2] = S.c("mm")
            ev_sv_final = S.c("ev")
            wq2_pump(4)
        mm_g_end = S.c("mm")

        d_wgq1 = S.dma("sync", S3[:].rearrange("p (k i) -> p k i", k=4),
                       wgq_dr[0:512, :].rearrange("(k p) i -> p k i", p=128),
                       waits=[("dS", p2g_out[1]), ("mm", mm_p5_end)])
        S.dma("sync", S4[:].rearrange("p (k i) -> p k i", k=4),
              wgq_dr[512:1024, :].rearrange("(k p) i -> p k i", p=128))
        S.dma("sync", S1[:].rearrange("p (k i) -> p k i", k=4),
              wuq_dr[0:512, :].rearrange("(k p) i -> p k i", p=128),
              waits=[("dS", p2u_out[1]), ("mm", mm_g_end)])

        def wgq(k):
            if k < 4:
                return S3[:, k * 4096:(k + 1) * 4096]
            return S4[:, (k - 4) * 4096:(k - 3) * 4096]

        def wuq(k):
            if k < 4:
                return S1[:, k * 4096:(k + 1) * 4096]
            return S2[:, (k - 4) * 4096:(k - 3) * 4096]

        d_xres = {}
        ev_xread = {0: 0, 1: 0}
        ev_v1_free = 0
        ev_y1_free = 0
        d_y1_prev = 0
        ac_sq_prev = 0
        gp_ei_reader = 0
        d_y1q_prev = 0
        d_y1q_all = 0
        for t in range(NT):
            bankA = (t % 2) * 2
            bankB = bankA + 1
            xw = [("ev", ev_xread[t % 2])]
            if t == 0:
                xw.append(("ev", p1d_out[0]))
                xw.append(("mm", S.c("mm")))
            d_xres[t] = S.dma("scalar",
                              CF1[:, (t % 2) * 1024:(t % 2) * 1024 + 1024],
                              x_tm[t * 128:(t + 1) * 128, :], waits=xw)
            e1 = 0
            for og in range(2):
                bank = bankA if og == 0 else bankB
                w0, c0 = bank_free[bank]
                for k in range(KC):
                    kw = []
                    if k == 0:
                        kw = [("dG", d_wpj), (w0, c0), ("ev", ev_sv_final)]
                    mmid = MM(pb[bank][:, 0:512],
                              SVf(k)[:, t * 128:(t + 1) * 128],
                              scC[:, k * 1024 + og * 512:k * 1024 + (og + 1) * 512],
                              start=(k == 0), stop=(k == KC - 1), waits=kw)
                e1 = EV(lambda e, t=t, og=og, b2=bank: e.tensor_add(
                    CF1[:, 2048 + og * 512:2048 + (og + 1) * 512],
                    pb[b2][:, 0:512],
                    CF1[:, (t % 2) * 1024 + og * 512:
                        (t % 2) * 1024 + (og + 1) * 512]),
                    waits=[("mm", mmid), ("dA", d_xres[t]), ("ev", ev_v1_free)])
                bank_free[bank] = ("ev", e1)
            ev_xread[t % 2] = e1
            es = EV(lambda e: e.reduce_sum(cols["sum"][:], CF1[:, 2048:3072],
                                           AX.X))
            e2 = EV(lambda e: e.tensor_scalar(
                cols["negm"][:], cols["sum"][:], -1.0 / D, None, ALU.mult),
                waits=[("ev", es)])
            w5, c5 = bank_free[5]
            a1 = ACT(lambda e: e.activation(
                pb[5][:, 0:512], CF1[:, 2048:2560], AF.Square,
                bias=cols["negm"][:], accum_out=cols["ssq"][:]),
                waits=[("ev", e2), (w5, c5), ("ac", ac_sq_prev)])
            a2 = ACT(lambda e: e.activation(
                pb[5][:, 0:512], CF1[:, 2560:3072], AF.Square,
                bias=cols["negm"][:], accum_out=cols["ssqb"][:]))
            bank_free[5] = ("ac", a2)
            ac_sq_prev = a2
            e3 = EV(lambda e: e.tensor_add(cols["ssq"][:], cols["ssq"][:],
                                           cols["ssqb"][:]),
                    waits=[("ac", a2)])
            e3b = EV(lambda e: e.tensor_scalar(
                cols["ssq"][:], cols["ssq"][:], 1.0 / D, 1e-5,
                ALU.mult, ALU.add), waits=[("ev", e3)])
            a3 = ACT(lambda e: e.activation(cols["sd"][:], cols["ssq"][:],
                                            AF.Sqrt),
                     waits=[("ev", e3b)])
            e4 = EV(lambda e: e.reciprocal(cols["rstd"][:], cols["sd"][:]),
                    waits=[("ac", a3)])
            e5 = EV(lambda e: e.tensor_scalar(
                CF1[:, 3072:4096], CF1[:, 2048:3072], cols["negm"][:],
                cols["rstd"][:], ALU.add, ALU.mult),
                waits=[("ev", e4), ("dA", d_y1q_prev)])
            ev_v1_free = e5
            d_y1_prev = S.dma("scalar", y1_dr[t * 128:(t + 1) * 128, :],
                              CF1[:, 3072:4096], waits=[("ev", e5)])
            e6 = EV(lambda e: e.reduce_max(cols["rmax"][:], CF1[:, 3072:4096],
                                           AX.X, apply_absolute_value=True))
            e7 = EV(lambda e, t=t: e.tensor_scalar(
                rsa_all[:, t:t + 1], cols["rmax"][:], 1e-5, 1.0 / 127,
                ALU.max, ALU.mult), waits=[("ev", e6)])
            e8 = EV(lambda e, t=t: e.reciprocal(cols["sa"][:],
                                                rsa_all[:, t:t + 1]),
                    waits=[("ev", e7)])
            for hh in range(2):
                EV(lambda e, hh=hh: e.tensor_scalar(
                    EI[:], CF1[:, 3072 + hh * 512:3072 + (hh + 1) * 512],
                    cols["sa"][:], None, ALU.mult),
                    waits=[("ev", e8)])
                EV(lambda e, hh=hh: e.tensor_copy(
                    scB[:, hh * 512:(hh + 1) * 512], EI[:]),
                    waits=[("dA", d_y1q_prev)])
            evd = DRAIN("vector", "ev")
            ev_y1_free = S.c("ev")
            d_y1q_prev = S.dma("scalar", y1q_dr[t * 128:(t + 1) * 128, :],
                               scB[:, 0:1024], waits=[("ev", evd)])
            d_y1q_all = d_y1q_prev
            wq2_pump(4)
        wq2_pump(10 ** 6)
        mm_proj_end = S.c("mm")

        d_st_prev = {0: 0, 1: 0}
        d_y1f_prev = 0
        for cch in range(KC):
            buf = scE[:] if cch % 2 == 0 else scC[:, 0:2048]
            wv_ = [("dA", d_y1q_all)]
            if cch % 2 == 1 and cch == 1:
                wv_.append(("mm", mm_proj_end))
            if d_st_prev[cch % 2]:
                wv_.append(("dS", d_st_prev[cch % 2]))
            dtr = S.dmat("sync", buf, y1q_dr[:, cch * 128:(cch + 1) * 128],
                         waits=wv_)
            d_st_prev[cch % 2] = S.dma(
                "sync", y1qfm_dr[cch * 128:(cch + 1) * 128, :],
                buf, waits=[("dS", dtr)])
        d_y1qfm_all = d_st_prev[1]

        wq3 = [None]

        def wq3_pump(n):
            if wq3[0] is None:
                wq3[0] = pass2_gen(wdT, "wsds", wdq_dr, p_d, wsd_box, CFD,
                                   1024, p2u_out, p2d_out)
            for _ in range(n):
                next(wq3[0], None)


        d_wq_all = S.dma("sync", S2[:].rearrange("p (k i) -> p k i", k=4),
                         wuq_dr[512:1024, :].rearrange("(k p) i -> p k i",
                                                       p=128),
                         waits=[("mm", mm_proj_end)])

        ev_dqg = EV(lambda e: e.tensor_scalar(
            rsa2_all[:], rsa_all[:], cols["wsgd"][:], None, ALU.mult))
        ev_dqu = EV(lambda e: e.tensor_scalar(
            rsa3_all[:], rsa_all[:], cols["wsud"][:], None, ALU.mult))

        f1t_d = dt("dbg_f1trace", [128, 4096], F32, "ExternalOutput") \
            if "f1trace" in dump else None
        mm_f1_tile = {0: 0, 1: 0}
        ac_f1_prev = [0]
        eh_slot = {0: 0, 1: 0, 2: 0}
        gp_hq_prev = 0
        gp_ei_last = [0]
        d_hq_last = [0]
        d_hq_all = 0
        ev_h_last = 0
        for t in range(NT):
            d_y1f = S.dma("scalar", scE[:, (t % 2) * 1024:(t % 2) * 1024 + 1024]
                          .rearrange("p (k q) -> p k q", k=KC),
                          y1qfm_dr[:, t * 128:(t + 1) * 128]
                          .rearrange("(k p) q -> p k q", p=128),
                          waits=([("dS", d_y1qfm_all)] if t == 0 else []) +
                                [("mm", mm_f1_tile[t % 2])])

            def stat(k, t=t):
                return scE[:, (t % 2) * 1024 + k * 128:
                           (t % 2) * 1024 + (k + 1) * 128]


            for ig in range(8):
                bg = (ig % 3) * 2
                bu = bg + 1
                gslot = ig % 3
                gbuf = (scB[:, gslot * 512:(gslot + 1) * 512] if gslot < 2
                        else scC[:, 0:512])
                wg0, cg0 = bank_free[bg]
                mg = 0
                for k in range(KC):
                    kw = []
                    if k == 0:
                        kw = [("dA", d_y1f), ("dS", d_wq_all), (wg0, cg0)]
                    mg = MM(pb[bg][:, 0:512], stat(k),
                            wgq(k)[:, ig * 512:(ig + 1) * 512],
                            start=(k == 0), stop=(k == KC - 1), waits=kw)
                wu0, cu0 = bank_free[bu]
                mu = 0
                for k in range(KC):
                    kw = [(wu0, cu0)] if k == 0 else []
                    mu = MM(pb[bu][:, 0:512], stat(k),
                            wuq(k)[:, ig * 512:(ig + 1) * 512],
                            start=(k == 0), stop=(k == KC - 1), waits=kw)
                a1 = ACT(lambda e, b2=bg, gb=gbuf, t=t: e.activation(
                    gb, pb[b2][:, 0:512],
                    AF.Silu, scale=rsa2_all[:, t:t + 1]),
                    waits=[("ev", ev_dqg), ("mm", mg), ("ev", eh_slot[gslot]),
                           ("dA", d_y1q_all if t == 0 else 0),
                           ("dS", d_y1qfm_all if t == 0 else 0)])
                bank_free[bg] = ("ac", a1)
                eh = EV(lambda e, b2=bu, ig=ig, gb=gbuf, t=t:
                        e.scalar_tensor_tensor(
                            CF1[:, ig * 512:(ig + 1) * 512], pb[b2][:, 0:512],
                            rsa3_all[:, t:t + 1],
                            gb, ALU.mult,
                            ALU.mult),
                        waits=[("ac", a1), ("ev", ev_dqu), ("mm", mu)])
                bank_free[bu] = ("ev", eh)
                eh_slot[gslot] = eh
            if "f1trace" in dump and t == 0:
                ehd0 = DRAIN("vector", "ev")
                d_f1t = S.dma("sync", f1t_d, CF1[:, 0:4096],
                              waits=[("ev", ehd0)])
                e_g = EV(lambda e: e.tensor_copy(azb[0][:, 0:1],
                                                 cols["dqg"][:]),
                         waits=[("dS", d_f1t)])
            ehd = DRAIN("vector", "ev")
            g1 = EV(lambda e: e.tensor_reduce(
                cols["tmp2"][:], CF1[:], AX.X, ALU.max, True),
                waits=[("ev", ehd)])
            e10 = EV(lambda e, t=t: e.tensor_scalar(
                rs2_all[:, t:t + 1], cols["tmp2"][:], 1e-5, 1.0 / 127,
                ALU.max, ALU.mult), waits=[("ev", g1)])
            e11 = EV(lambda e, t=t: e.reciprocal(cols["s2"][:],
                                                 rs2_all[:, t:t + 1]),
                     waits=[("ev", e10)])
            for q in range(8):
                EV(lambda e, q=q: e.tensor_scalar(
                    EI[:], CF1[:, q * 512:(q + 1) * 512], cols["s2"][:], None,
                    ALU.mult), waits=[("ev", e11)])
                EV(lambda e, q=q: e.tensor_copy(
                    scB[:, 1024 + (q % 2) * 512:1024 + (q % 2 + 1) * 512],
                    EI[:]),
                    waits=[("dS", d_hq_last[0])])
                if q % 2 == 1:
                    evd2 = DRAIN("vector", "ev")
                    d_hq = S.dma("sync",
                                 hq_dr[t * 128:(t + 1) * 128,
                                       (q - 1) * 512:(q + 1) * 512],
                                 scB[:, 1024:2048], waits=[("ev", evd2)])
                    d_hq_last[0] = d_hq
                    d_hq_all = d_hq
            ev_h_last = S.c("ev")
            mm_f1_tile[t % 2] = S.c("mm")
            ac_f1_prev[0] = S.c("ac")
            wq3_pump(4)
        wq3_pump(10 ** 6)
        mm_f1_end = S.c("mm")
        ac_f1_end = S.c("ac")

        ev_dqd = EV(lambda e: e.tensor_scalar(
            rs2d_all[:], rs2_all[:], cols["wsdd"][:], None, ALU.mult))
        d_wdc = []
        for ic in range(32):
            dst = (S2[:, ic * 1024:(ic + 1) * 1024] if ic < 16 else
                   S1[:, (ic - 16) * 1024:(ic - 15) * 1024])
            kw = ([("dS", p2d_out[1]), ("mm", mm_f1_end)] if ic == 0 else [])
            d_wdc.append(S.dma("gpsimd", dst,
                               wdq_dr[ic * 128:(ic + 1) * 128, :], waits=kw))

        def wdq(ic):
            if ic < 16:
                return S2[:, ic * 1024:(ic + 1) * 1024]
            return S1[:, (ic - 16) * 1024:(ic - 15) * 1024]

        ev_v2_free = 0
        d_out_prev = 0
        ac_sq2_prev = 0
        mm_tb_read = {0: mm_f1_end, 1: mm_f1_end}
        ev_y1r_free = {0: 0, 1: 0}
        for tb in range(4):
            slab = S3 if tb % 2 == 0 else S4
            d_hqf = 0
            for ic in range(ICN):
                kw = []
                if ic == 0:
                    kw = [("dS", d_hq_all), ("mm", mm_tb_read[tb % 2])]
                d_hqf = S.dmat("sync", slab[:, ic * 512:(ic + 1) * 512],
                               hq_dr[tb * 512:(tb + 1) * 512,
                                     ic * 128:(ic + 1) * 128], waits=kw)
            for t4 in range(4):
                t = tb * 4 + t4
                yw = [("ev", ev_y1r_free[t4 % 2])]
                if tb == 0 and t4 == 0:
                    yw += [("ev", ev_h_last)]
                d_y1r = S.dma("scalar",
                              CF1[:, (t4 % 2) * 1024:(t4 % 2) * 1024 + 1024],
                              y1_dr[t * 128:(t + 1) * 128, :], waits=yw)

                e1 = 0
                for og in range(2):
                    bank = t4 % 2 + og * 2
                    w0, c0 = bank_free[bank]
                    mmid = 0
                    for ic in range(ICN):
                        kw = []
                        if ic == 0:
                            kw = [("dS", d_hqf), (w0, c0)]
                        if tb == 0 and t4 == 0 and og == 0:
                            kw.append(("dG", d_wdc[ic]))
                        mmid = MM(pb[bank][:, 0:512],
                                  slab[:, ic * 512 + t4 * 128:
                                       ic * 512 + (t4 + 1) * 128],
                                  wdq(ic)[:, og * 512:(og + 1) * 512],
                                  start=(ic == 0), stop=(ic == ICN - 1),
                                  waits=kw)
                    e1 = EV(lambda e, og=og, t4=t4, b2=bank, t=t:
                            e.scalar_tensor_tensor(
                                CF1[:, 2048 + og * 512:2048 + (og + 1) * 512],
                                pb[b2][:, 0:512], rs2d_all[:, t:t + 1],
                                CF1[:, (t4 % 2) * 1024 + og * 512:
                                    (t4 % 2) * 1024 + (og + 1) * 512],
                                ALU.mult, ALU.add),
                            waits=[("mm", mmid), ("ev", ev_dqd), ("dA", d_y1r),
                                   ("ev", ev_v2_free)])
                    bank_free[bank] = ("ev", e1)
                ev_y1r_free[t4 % 2] = e1
                es = EV(lambda e: e.reduce_sum(cols["sum"][:],
                                               CF1[:, 2048:3072], AX.X))
                e2 = EV(lambda e: e.tensor_scalar(
                    cols["negm"][:], cols["sum"][:], -1.0 / D, None, ALU.mult),
                    waits=[("ev", es)])
                w5, c5 = bank_free[5]
                aw = [("ev", e2), (w5, c5), ("ac", ac_sq2_prev)]
                if tb == 0 and t4 == 0:
                    aw.append(("ac", ac_f1_end))
                a1 = ACT(lambda e: e.activation(
                    pb[5][:, 0:512], CF1[:, 2048:2560], AF.Square,
                    bias=cols["negm"][:], accum_out=cols["ssq"][:]), waits=aw)
                a2 = ACT(lambda e: e.activation(
                    pb[5][:, 0:512], CF1[:, 2560:3072], AF.Square,
                    bias=cols["negm"][:], accum_out=cols["ssqb"][:]))
                bank_free[5] = ("ac", a2)
                ac_sq2_prev = a2
                e3 = EV(lambda e: e.tensor_add(cols["ssq"][:], cols["ssq"][:],
                                               cols["ssqb"][:]),
                        waits=[("ac", a2)])
                e3b = EV(lambda e: e.tensor_scalar(
                    cols["ssq"][:], cols["ssq"][:], 1.0 / D, 1e-5,
                    ALU.mult, ALU.add), waits=[("ev", e3)])
                a3 = ACT(lambda e: e.activation(cols["sd"][:], cols["ssq"][:],
                                                AF.Sqrt),
                         waits=[("ev", e3b)])
                e4r = EV(lambda e: e.reciprocal(cols["rstd"][:],
                                                cols["sd"][:]),
                         waits=[("ac", a3)])
                e5 = EV(lambda e: e.tensor_scalar(
                    CF1[:, 3072:4096], CF1[:, 2048:3072], cols["negm"][:],
                    cols["rstd"][:], ALU.add, ALU.mult),
                    waits=[("ev", e4r), ("dA", d_out_prev)])
                ev_v2_free = e5
                d_out_prev = S.dma("scalar", out_d[t * 128:(t + 1) * 128, :],
                                   CF1[:, 3072:4096], waits=[("ev", e5)])
            mm_tb_read[tb % 2] = S.c("mm")

        if "k2probe" in dump:
            kp = dt("dbg_k2probe", [128, 132], F32, "ExternalOutput")
            wfin = [("dA", S.c("dA")), ("dS", S.c("dS")), ("dG", S.c("dG")),
                    ("ev", S.c("ev")), ("ac", S.c("ac")), ("gp", S.c("gp")),
                    ("mm", S.c("mm"))]
            e_kp = EV(lambda e: e.tensor_copy(CF1[:, 0:1], cols["rowr"][:]),
                      waits=wfin)
            EV(lambda e: e.tensor_copy(CF1[:, 1:2], cols["tmp"][:]))
            EV(lambda e: e.tensor_copy(CF1[:, 2:66], k2e[:]))
            EV(lambda e: e.tensor_copy(CF1[:, 66:130], k2s[:]))
            EV(lambda e: e.tensor_copy(CF1[:, 130:131], cols["wsgs"][:]))
            e_kp2 = EV(lambda e: e.tensor_copy(CF1[:, 131:132], cols["wsgd"][:]))
            S.dma("sync", kp[:, 0:132], CF1[:, 0:132], waits=[("ev", e_kp2)])

        dumpable = {"y1_dr": y1_dr, "y1q_dr": y1q_dr, "y1qfm_dr": y1qfm_dr,
                    "hq_dr": hq_dr, "wgq_dr": wgq_dr, "wuq_dr": wuq_dr,
                    "wdq_dr": wdq_dr, "k2_dr": k2_dr, "ag_out": ag_out,
                    "ar_in": ar_in, "ar_out": ar_out, "w2_out": w2_out}
        shapes = {"y1_dr": ([TL, D], F32), "y1q_dr": ([TL, D], BF16),
                  "y1qfm_dr": ([D, TL], BF16), "hq_dr": ([TL, DI], BF16),
                  "wgq_dr": ([D, DI], BF16), "wuq_dr": ([D, DI], BF16),
                  "wdq_dr": ([DI, D], BF16), "k2_dr": ([2048, 64], BF16),
                  "ag_out": ([128, D], BF16), "ar_in": ([2048, 132], F32),
                  "ar_out": ([2048, 132], F32), "w2_out": ([2048, 64], BF16)}
        for nmd in dump:
            if nmd in ("k2probe", "k2trace", "f1trace"):
                continue
            sh, dty = shapes[nmd]
            od = dt("dbg_" + nmd, sh, dty, "ExternalOutput")
            S.dma("sync", od, dumpable[nmd],
                  waits=[("dA", S.c("dA")), ("dS", S.c("dS")),
                         ("dG", S.c("dG")), ("ev", S.c("ev")),
                         ("ac", S.c("ac")), ("gp", S.c("gp")),
                         ("mm", S.c("mm")), ("cc2", 1), ("cc3", 1)])

        S.emit("sync", lambda e: e.nop(),
               waits=[("dS", S.c("dS")), ("dA", S.c("dA")), ("dG", S.c("dG")),
                      ("ev", S.c("ev")), ("ac", S.c("ac")), ("gp", S.c("gp")),
                      ("mm", S.c("mm"))])

        S.build(blk, sems)
    return nc


_NC_CACHE = [None]


def _consts():
    c = {}
    c["ident"] = np.eye(128).astype(ml_dtypes.bfloat16)
    id3 = np.zeros((64, 192), np.float32)
    id3[:, 0:64] = 7.0 * np.eye(64)
    id3[:, 64:128] = 15.0 * np.eye(64)
    id3[:, 128:192] = 13.0 * np.eye(64)
    c["id3"] = id3
    p1 = np.zeros((128, 2), np.float32)
    p1[0:64, 0] = 1.0
    p1[64:128, 1] = 1.0
    c["pair1"] = p1.astype(ml_dtypes.bfloat16)
    bcm = np.zeros((2, 128), np.float32)
    bcm[0, 0:64] = 1.0
    bcm[1, 64:128] = 1.0
    c["bcmask"] = bcm
    c["onesr"] = np.ones((1, 128), np.float32)
    c["o1b"] = np.ones((128, 1), np.float32).astype(ml_dtypes.bfloat16)
    c["o1f"] = np.ones((128, 1), np.float32)
    return c


def kernel(L0_attn_x, L0_attn_w_qkv, L0_attn_w_proj,
           L0_ffn_w_gate, L0_ffn_w_up, L0_ffn_w_down, _dump=(), _raw=False):
    x = np.ascontiguousarray(np.asarray(L0_attn_x, np.float32))
    wqkv = np.ascontiguousarray(np.asarray(L0_attn_w_qkv, np.float32))
    wpj = np.ascontiguousarray(np.asarray(L0_attn_w_proj, np.float32))
    wgT = np.ascontiguousarray(np.asarray(L0_ffn_w_gate, np.float32).T)
    wuT = np.ascontiguousarray(np.asarray(L0_ffn_w_up, np.float32).T)
    wdT = np.ascontiguousarray(np.asarray(L0_ffn_w_down, np.float32).T)

    if _dump:
        nc = build_program(dump=_dump)
    else:
        if _NC_CACHE[0] is None:
            _NC_CACHE[0] = build_program()
        nc = _NC_CACHE[0]

    cs = _consts()
    in_maps = []
    for c in range(NC):
        xs = np.ascontiguousarray(x[:, c * NL:(c + 1) * NL, :]).reshape(TL, D)
        gi = np.zeros((64, 4), np.int32)
        for j in range(4):
            jj, h = divmod(j, 2)
            idx = 2 * c + jj
            gi[:, j] = idx * 128 + h * 64 + np.arange(64)
        m = {
            "x_tm": xs,
            "x_fmT": np.ascontiguousarray(xs.T),
            "wqkv": wqkv, "wproj": wpj,
            "wgT": wgT, "wuT": wuT, "wdT": wdT,
            "gidx": gi,
        }
        m.update(cs)
        in_maps.append(m)

    res = run_bass_kernel_spmd(nc, in_maps, core_ids=list(range(NC)))
    if _raw:
        return res
    full = np.empty((B, N, D), np.float32)
    for c in range(NC):
        r = np.asarray(res.results[c]["out"], np.float32)
        full[:, c * NL:(c + 1) * NL, :] = r.reshape(B, NL, D)
    return full


# version 16
# speedup vs baseline: 1.0768x; 1.0429x over previous
import numpy as np
import ml_dtypes
import contextlib
import concourse.bass as bass
import concourse.mybir as mybir
from concourse.bass import IndirectOffsetOnAxis
from concourse.bass_utils import run_bass_kernel_spmd

F32 = mybir.dt.float32
BF16 = mybir.dt.bfloat16
I32 = mybir.dt.int32
AF = mybir.ActivationFunctionType
ALU = mybir.AluOpType
AX = mybir.AxisListType

NC = 8
B = 2
N = 8192
D = 1024
NL = N // NC
TL = B * NL
NT = TL // 128
DI = 4096
KC = D // 128
ICN = DI // 128

_ENGINES = ("sync", "scalar", "vector", "tensor", "gpsimd")
_DMA_SEM = {"sync": "dS", "scalar": "dA", "gpsimd": "dG"}


class Sched:
    def __init__(self):
        self.count = {}
        self.ops = {e: [] for e in _ENGINES}

    def c(self, sem):
        return self.count.get(sem, 0)

    def emit(self, engine, fn, waits=(), inc=None, inc_amt=1):
        waits = [(s, v) for (s, v) in waits if v > 0]
        for s, v in waits:
            assert v <= self.c(s), f"wait {s}>={v} before inc (have {self.c(s)})"
        self.ops[engine].append((tuple(waits), fn, inc, inc_amt))
        if inc is not None:
            self.count[inc] = self.c(inc) + inc_amt
            return self.count[inc]
        return None

    def dma(self, queue, out, in_, waits=()):
        return self.emit(queue, lambda e, o=out, i=in_: e.dma_start(out=o, in_=i),
                         waits=waits, inc=_DMA_SEM[queue], inc_amt=16)

    def dmat(self, queue, out, in_, waits=()):
        return self.emit(queue, lambda e, o=out, i=in_: e.dma_start_transpose(o, i),
                         waits=waits, inc=_DMA_SEM[queue], inc_amt=16)

    def build(self, block, sems):
        for eng_name in _ENGINES:
            ops = self.ops[eng_name]

            def mk(ops):
                def run(eng):
                    last = {}
                    for waits, fn, inc, amt in ops:
                        for s, v in waits:
                            if last.get(s, 0) < v:
                                eng.wait_ge(sems[s], v)
                                last[s] = v
                        ins = fn(eng)
                        if inc is not None:
                            ins.then_inc(sems[inc], amt)
                return run
            getattr(block, eng_name)(mk(ops))


def build_program(dump=()):
    nc = bass.Bass("TRN2", target_bir_lowering=False)
    dt = lambda n, s, d, k="Internal": nc.dram_tensor(n, s, d, kind=k).ap()

    x_tm = dt("x_tm", [TL, D], F32, "ExternalInput")
    x_fmT = dt("x_fmT", [D, TL], F32, "ExternalInput")
    wqkv = dt("wqkv", [D, 3 * D], F32, "ExternalInput")
    wproj = dt("wproj", [D, D], F32, "ExternalInput")
    wgT = dt("wgT", [D, DI], F32, "ExternalInput")
    wuT = dt("wuT", [D, DI], F32, "ExternalInput")
    wdT = dt("wdT", [DI, D], F32, "ExternalInput")
    ident_d = dt("ident", [128, 128], BF16, "ExternalInput")
    id3_d = dt("id3", [64, 192], F32, "ExternalInput")
    pair1_d = dt("pair1", [128, 2], BF16, "ExternalInput")
    bcmask_d = dt("bcmask", [2, 128], F32, "ExternalInput")
    onesr_d = dt("onesr", [1, 128], F32, "ExternalInput")
    o1b_d = dt("o1b", [128, 1], BF16, "ExternalInput")
    o1f_d = dt("o1f", [128, 1], F32, "ExternalInput")
    gidx_d = dt("gidx", [64, 4], I32, "ExternalInput")
    out_d = dt("out", [TL, D], F32, "ExternalOutput")

    y1_dr = dt("y1_dr", [TL, D], F32)
    y1q_dr = dt("y1q_dr", [TL, D], BF16)
    y1qfm_dr = dt("y1qfm_dr", [D, TL], BF16)
    hq_dr = dt("hq_dr", [TL, DI], BF16)
    wgq_dr = dt("wgq_dr", [D, DI], BF16)
    wuq_dr = dt("wuq_dr", [D, DI], BF16)
    wdq_dr = dt("wdq_dr", [DI, D], BF16)
    k2_dr = dt("k2_dr", [2048, 64], BF16)
    ag_in = dt("ag_in", [16, D], BF16)
    ag_out = dt("ag_out", [128, D], BF16)
    ar_in = dt("ar_in", [2048, 132], F32)
    ar_out = dt("ar_out", [2048, 132], F32)
    w2_in = dt("w2_in", [256, 64], BF16)
    w2_out = dt("w2_out", [2048, 64], BF16)

    S = Sched()
    semnames = ["dS", "dA", "dG", "mm", "ev", "ac", "gp", "cc1", "cc2", "cc3"]

    with contextlib.ExitStack() as st:
        ent = st.enter_context
        sems = {n: ent(nc.semaphore(n)) for n in semnames}
        sb = lambda n, s, d: ent(nc.sbuf_tensor(n, s, d))

        S1 = sb("S1", [128, 16384], BF16)
        S2 = sb("S2", [128, 16384], BF16)
        S3 = sb("S3", [128, 16384], BF16)
        S4 = sb("S4", [128, 16384], BF16)
        scC = sb("scC", [128, 8192], BF16)
        scB = sb("scB", [128, 2048], BF16)
        scD = sb("scD", [128, 2048], BF16)
        scE = sb("scE", [128, 2048], BF16)
        CF1 = sb("CF1", [128, 4096], F32)
        CFD = sb("CFD", [128, 2048], F32)
        EI = sb("EI", [128, 512], I32)
        EI2 = sb("EI2", [128, 512], I32)
        bcs = sb("bcs", [128, 512], F32)
        zc = CF1[0:2, 0:512]
        rz = CF1[0:2, 512:1024]
        c_id = sb("c_id", [128, 128], BF16)
        c_id3 = sb("c_id3", [64, 192], F32)
        c_p1 = sb("c_p1", [128, 2], BF16)
        c_bc = sb("c_bc", [2, 128], F32)
        c_or = sb("c_or", [1, 128], F32)
        c_o1b = sb("c_o1b", [128, 1], BF16)
        c_o1f = sb("c_o1f", [128, 1], F32)
        gidx = sb("gidx_s", [64, 4], I32)
        xlpf = sb("xlpf", [128, 128], F32)
        xlp = sb("xlp", [128, 128], BF16)
        xlt = sb("xlt", [128, 128], BF16)
        xl_g = sb("xl_g", [128, 1024], BF16)
        xl_fm = sb("xl_fm", [128, 1024], BF16)
        ql_fm = sb("ql_fm", [128, 1024], BF16)
        kl_fm = sb("kl_fm", [128, 1024], BF16)
        bdq = [sb(f"bdq{i}", [128, 128], BF16) for i in range(2)]
        bdk = [sb(f"bdk{i}", [128, 128], BF16) for i in range(2)]
        bdw = [sb(f"bdw{i}", [128, 128], BF16) for i in range(2)]
        bsm = [sb(f"bsm{i}", [128, 128], BF16) for i in range(2)]
        k3v = [sb(f"k3v{i}", [128, 128], BF16) for i in range(2)]
        azb = [sb(f"azb{i}", [128, 132], F32) for i in range(2)]
        k2e = sb("k2e", [128, 64], F32)
        k2s = sb("k2s", [128, 64], BF16)
        nw = [{k: sb(f"nw_{k}{j}", [64, 64], BF16) for k in
               ("k2b", "k2t", "vk", "vt", "ptb", "tt")} for j in range(4)]
        dnc = [sb(f"dnc{j}", [64, 1], F32) for j in range(4)]
        colnames = ["sum", "negm", "ssq", "ssqb", "sd", "rstd", "rmax", "sa",
                    "dqg", "dqu", "dqd", "s2", "tmp", "tmp2",
                    "wsgs", "wsgd", "wsus", "wsud", "wsds", "wsdd",
                    "accg", "accu", "accd", "rowr"]
        cols = {n: sb("col_" + n, [128, 1], F32) for n in colnames}
        rsa_all = sb("rsa_all", [128, 16], F32)
        rsa2_all = sb("rsa2_all", [128, 16], F32)
        rsa3_all = sb("rsa3_all", [128, 16], F32)
        rs2d_all = sb("rs2d_all", [128, 16], F32)
        rs2_all = sb("rs2_all", [128, 16], F32)
        t11 = sb("t11", [1, 16], F32)

        pb = [ent(nc.psum_tensor(f"pb{i}", [128, 512], F32)) for i in range(6)]
        pA = ent(nc.psum_tensor("pA", [128, 132], F32))
        pT = ent(nc.psum_tensor("pT", [128, 128], BF16))

        blk = ent(nc.Block())

        def MM(ps_ap, lhsT, rhs, start, stop, waits=()):
            return S.emit("tensor",
                          lambda e, a=ps_ap, l=lhsT, r=rhs, s1=start, s2=stop:
                          e.matmul(a, l, r, start=s1, stop=s2),
                          waits=waits, inc="mm")

        def TRP(ps_ap, in_ap, id_ap, waits=()):
            return S.emit("tensor",
                          lambda e, a=ps_ap, i=in_ap, d=id_ap:
                          e.transpose(a, i, d), waits=waits, inc="mm")

        def EV(fn, waits=()):
            return S.emit("vector", fn, waits=waits, inc="ev")

        def ACT(fn, waits=()):
            return S.emit("scalar", fn, waits=waits, inc="ac")

        def GP(fn, waits=()):
            return S.emit("gpsimd", fn, waits=waits, inc="gp")

        def DRAIN(engine, sem):
            return S.emit(engine, lambda e: e.drain(), inc=sem)

        bank_free = {i: ("ev", 0) for i in list(range(6)) + ["pA", "pT"]}

        xfm = lambda k: S1[:, k * 2048:(k + 1) * 2048]
        Qf = lambda p: S2[:, p * 2048:(p + 1) * 2048]
        Kf = lambda p: S3[:, p * 2048:(p + 1) * 2048]
        Vt = lambda t: S4[:, t * 1024:(t + 1) * 1024]
        exp1 = lambda i: S1[:, i * 1024:(i + 1) * 1024]
        SVf = lambda p: S2[:, p * 2048:(p + 1) * 2048]
        wv16 = lambda k: S3[:, k * 1024:(k + 1) * 1024]

        S.dma("sync", c_id[:], ident_d)
        S.dma("sync", c_id3[:], id3_d)
        S.dma("sync", c_p1[:], pair1_d)
        S.dma("sync", c_bc[:], bcmask_d)
        S.dma("sync", c_or[:], onesr_d)
        S.dma("sync", c_o1b[:], o1b_d)
        S.dma("sync", c_o1f[:], o1f_d)
        d_consts = S.dma("sync", gidx[:], gidx_d)

        d_xfm = S.dma("gpsimd", S1[:].rearrange("p (c t) -> p c t", c=KC),
                     x_fmT.rearrange("(c p) t -> p c t", p=128))
        d_wv = S.dma("gpsimd", S3[:, 0:8192].rearrange("p (k q) -> p k q", k=KC),
                     wqkv[:, 2048:3072].rearrange("(k p) q -> p k q", p=128))

        for k in range(KC):
            for t in range(NT):
                ACT(lambda e, k=k, t=t: e.activation(
                    scB[:, 0:128], xfm(k)[:, t * 128:(t + 1) * 128], AF.Copy,
                    scale=1.0 / 128, accum_out=xlpf[:, k * 16 + t:k * 16 + t + 1]),
                    waits=[("dG", d_xfm)] if (k == 0 and t == 0)
                    else [("ac", S.c("ac"))])
        ac_xlp = S.c("ac")
        ev_xlp = EV(lambda e: e.tensor_copy(xlp[:], xlpf[:]),
                    waits=[("ac", ac_xlp)])
        w0, c0 = bank_free["pT"]
        m_t = TRP(pT[:], xlp[:], c_id[:],
                  waits=[("ev", ev_xlp), ("dS", d_consts), (w0, c0)])
        e_t = EV(lambda e: e.tensor_copy(xlt[:], pT[:]), waits=[("mm", m_t)])
        bank_free["pT"] = ("ev", e_t)
        d_ag = S.dma("sync", ag_in.rearrange("t (k p) -> k t p", p=128),
                     xlt[:], waits=[("ev", e_t)])
        S.emit("gpsimd",
               lambda e: e.collective_compute(
                   "AllGather", ALU.bypass, replica_groups=[list(range(NC))],
                   ins=[ag_in.opt()], outs=[ag_out.opt()]),
               waits=[("dS", d_ag)], inc="cc1")
        d_xlg = S.dma("sync", xl_g[:], ag_out, waits=[("cc1", 1)])

        for k in range(KC):
            w0, c0 = bank_free["pT"]
            m1 = TRP(pT[:], xl_g[:, k * 128:(k + 1) * 128], c_id[:],
                     waits=[("dS", d_xlg), (w0, c0)])
            e1 = EV(lambda e, k=k: e.tensor_copy(
                xl_fm[:, k * 128:(k + 1) * 128]
                .rearrange("p (b r s) -> p r b s", b=2, r=8, s=8),
                pT[:].rearrange("p (r b s) -> p r b s", r=8, b=2, s=8)),
                waits=[("mm", m1)])
            bank_free["pT"] = ("ev", e1)
        ev_xlfm = S.c("ev")

        def pass1_gen(wT, accname, pieces, buf_tensor, bufw, first_ev_wait,
                      out):
            prev_ev = {}
            acc = cols[accname]
            for i, (r0, cc, wdt) in enumerate(pieces):
                half = i % 2
                buf = buf_tensor[:, half * bufw:half * bufw + wdt]
                waits = []
                if i >= 2:
                    waits.append(("ev", prev_ev[i - 2]))
                elif first_ev_wait[0]:
                    waits.append(("ev", first_ev_wait[0]))
                d1 = S.dma("sync", buf, wT[r0:r0 + 128, cc:cc + wdt], waits=waits)
                gr = EV(lambda e, b=buf: e.tensor_reduce(
                    cols["tmp2"][:], b, AX.X, ALU.add, True),
                    waits=[("dS", d1)])
                if i == 0:
                    g2 = EV(lambda e, a=acc: e.tensor_copy(a[:], cols["tmp2"][:]),
                            waits=[("ev", gr)])
                else:
                    g2 = EV(lambda e, a=acc: e.tensor_add(a[:], a[:],
                                                          cols["tmp2"][:]),
                            waits=[("ev", gr)])
                prev_ev[i] = g2
                out[0] = g2
                yield

        p_gu = [(r * 128, ch * 2048, 2048) for r in range(8) for ch in range(2)]
        p_d = [(r * 128, 0, 1024) for r in range(32)]


        def ws_finalize(accname, numel, scol, dcol, slot, evwait):
            w5, c5 = bank_free[5]
            m1 = MM(pb[5][0:1, 0:1], cols[accname][:], c_o1f[:],
                    start=True, stop=True, waits=[("ev", evwait), (w5, c5)])
            e1 = EV(lambda e, s=slot: e.tensor_scalar(
                t11[0:1, s * 2:s * 2 + 1], pb[5][0:1, 0:1],
                1.0 / numel, 1e-5, ALU.mult, ALU.max), waits=[("mm", m1)])
            e2 = EV(lambda e, s=slot: e.reciprocal(
                t11[0:1, s * 2 + 1:s * 2 + 2], t11[0:1, s * 2:s * 2 + 1]),
                waits=[("ev", e1)])
            m2 = MM(pb[5][:, 0:1], c_or[:], t11[0:1, slot * 2:slot * 2 + 1],
                    start=True, stop=True, waits=[("ev", e2)])
            m3 = MM(pb[5][:, 1:2], c_or[:], t11[0:1, slot * 2 + 1:slot * 2 + 2],
                    start=True, stop=True)
            EV(lambda e, d=dcol: e.tensor_copy(cols[d][:], pb[5][:, 0:1]),
               waits=[("mm", m3)])
            e4 = EV(lambda e, s=scol: e.tensor_copy(cols[s][:], pb[5][:, 1:2]))
            bank_free[5] = ("ev", e4)
            return e4


        def pass2_gen(wT, scol, q_dr, pieces, wswait, buf_tensor, bufw,
                      first_gp_wait, out, first_ev_wait=0):
            prev = {}
            for i, (r0, cc, wdt) in enumerate(pieces):
                half = i % 2
                buf = buf_tensor[:, half * bufw:half * bufw + wdt]
                waits = [("ev", wswait[0])]
                if i >= 2:
                    waits.append(("ev", prev[i - 2][0]))
                elif first_gp_wait[0]:
                    waits.append(("ev", first_gp_wait[0]))
                if i < 2 and first_ev_wait:
                    waits.append(("ev", first_ev_wait))
                d1 = S.dma("sync", buf, wT[r0:r0 + 128, cc:cc + wdt], waits=waits)
                glast = None
                for q in range(wdt // 512):
                    qw = [("dS", d1)]
                    if i >= 1 and q == 0:
                        qw.append(("dS", prev[i - 1][1]))
                    g1 = EV(lambda e, b=buf, q=q, s=scol: e.tensor_scalar(
                        EI2[:], b[:, q * 512:(q + 1) * 512], cols[s][:], None,
                        ALU.mult), waits=qw)
                    glast = EV(lambda e, q=q: e.tensor_scalar(
                        scD[:, q * 512:(q + 1) * 512], EI2[:], 1.0, -1.0,
                        ALU.min, ALU.max), waits=[("ev", g1)])
                glast = DRAIN("vector", "ev")
                d2 = S.dma("sync", q_dr[r0:r0 + 128, cc:cc + wdt],
                           scD[:, 0:wdt], waits=[("ev", glast)])
                prev[i] = (glast, d2)
                out[0] = glast
                out[1] = d2
                yield

        p1g_out = [0]
        p1u_out = [0]
        p1d_out = [0]
        wsg_box = [0]
        wsu_box = [0]
        wsd_box = [0]
        p2g_out = [0, 0]
        p2u_out = [0, 0]
        p2d_out = [0, 0]
        p_g2 = [(r * 128, ch * 1024, 1024) for r in range(8) for ch in range(4)]

        def wq1_fn():
            yield from pass1_gen(wgT, "accg", p_gu, CF1, 2048, [0], p1g_out)
            yield from pass1_gen(wuT, "accu", p_gu, CF1, 2048, p1g_out, p1u_out)
            yield from pass1_gen(wdT, "accd", p_d, CF1, 2048, p1u_out, p1d_out)
            wsg_box[0] = ws_finalize("accg", D * DI, "wsgs", "wsgd", 0,
                                     p1g_out[0])
            wsu_box[0] = ws_finalize("accu", D * DI, "wsus", "wsud", 1,
                                     p1u_out[0])
            wsd_box[0] = ws_finalize("accd", D * DI, "wsds", "wsdd", 2,
                                     p1d_out[0])
            yield

        wq1 = wq1_fn()

        def wq1_pump(n):
            for _ in range(n):
                next(wq1, None)

        def wq2_fn():
            yield from pass2_gen(wgT, "wsgs", wgq_dr, p_g2, wsg_box, CFD,
                                 1024, [0], p2g_out)
            yield from pass2_gen(wuT, "wsus", wuq_dr, p_g2, wsu_box, CFD,
                                 1024, [0], p2u_out)

        wq2 = [None]

        def wq2_pump(n):
            if not wsd_box[0]:
                return
            if wq2[0] is None:
                wq2[0] = wq2_fn()
            for _ in range(n):
                next(wq2[0], None)

        for t in range(NT):
            for g2 in range(2):
                bank = 4 + (t * 2 + g2) % 2
                w0, c0 = bank_free[bank]
                for k in range(KC):
                    mmid = MM(pb[bank][:, 0:512], xfm(k)[:, t * 128:(t + 1) * 128],
                              wv16(k)[:, g2 * 512:(g2 + 1) * 512],
                              start=(k == 0), stop=(k == KC - 1),
                              waits=([("dG", max(d_wv, d_xfm)), (w0, c0)]
                                     if k == 0 else ()))
                ev1 = EV(lambda e, t=t, g2=g2, b=bank: e.tensor_copy(
                    Vt(t)[:, g2 * 512:(g2 + 1) * 512], pb[b][:, 0:512]),
                    waits=[("mm", mmid)])
                bank_free[bank] = ("ev", ev1)
                wq1_pump(1)
        mm_v_end = S.c("mm")
        ev_v_end = S.c("ev")

        mm_qlkl = {}
        kt_d = dt("dbg_k2trace", [128, 128], F32, "ExternalOutput") \
            if "k2trace" in dump else None
        d_trc = [0]
        k2_iter = [0]
        mm_bdq2 = {0: 0, 1: 0}
        d_k2_prev = 0
        ev_k2_prev = 0
        last_round_mm = mm_v_end
        for rnd in range(2):
            dwq = S.dma("gpsimd", scC[:, 0:4096].rearrange("p (k q) -> p k q", k=KC),
                        wqkv[:, rnd * 512:rnd * 512 + 512]
                        .rearrange("(k p) q -> p k q", p=128),
                        waits=[("mm", last_round_mm)])
            dwk = S.dma("gpsimd", scC[:, 4096:8192].rearrange("p (k q) -> p k q", k=KC),
                        wqkv[:, 1024 + rnd * 512:1024 + rnd * 512 + 512]
                        .rearrange("(k p) q -> p k q", p=128))
            for half in range(2):
                dwait = dwq if half == 0 else dwk
                for j in range(4):
                    p = rnd * 4 + j
                    wsl = scC[:, half * 4096 + j * 1024:half * 4096 + (j + 1) * 1024]
                    for g in range(4):
                        bank = g
                        w0, c0 = bank_free[bank]
                        for k in range(KC):
                            kw = []
                            if k == 0:
                                kw = [("dG", dwait), (w0, c0)]
                            mmid = MM(pb[bank][:, 0:512],
                                      wsl[:, k * 128:(k + 1) * 128],
                                      xfm(k)[:, g * 512:(g + 1) * 512],
                                      start=(k == 0), stop=(k == KC - 1),
                                      waits=kw)
                        dst = Qf(p) if half == 0 else Kf(p)
                        scale = 0.125 if half == 0 else 1.0
                        a1 = ACT(lambda e, d=dst, g=g, b=bank, s=scale:
                                 e.activation(d[:, g * 512:(g + 1) * 512],
                                              pb[b][:, 0:512], AF.Copy, scale=s),
                                 waits=[("mm", mmid)])
                        bank_free[bank] = ("ac", a1)
                    wpa, cpa = bank_free["pA"]
                    for k in range(KC):
                        mmid = MM(pA[:, 0:128], wsl[:, k * 128:(k + 1) * 128],
                                  xl_fm[:, k * 128:(k + 1) * 128],
                                  start=(k == 0), stop=(k == KC - 1),
                                  waits=([("ev", ev_xlfm), (wpa, cpa)]
                                         if k == 0 else ()))
                    dst2 = ql_fm if half == 0 else kl_fm
                    scale = 0.125 if half == 0 else 1.0
                    e1 = EV(lambda e, d=dst2, p=p, s=scale: e.tensor_scalar(
                        d[:, p * 128:(p + 1) * 128], pA[:, 0:128], s, None,
                        ALU.mult), waits=[("mm", mmid)])
                    bank_free["pA"] = ("ev", e1)
                    mm_qlkl[(half, p)] = e1
                    wq1_pump(2)
                last_round_mm = S.c("mm")

            for j in range(4):
                p = rnd * 4 + j
                for b in range(2):
                    idx = b * 8 + p
                    jb = k2_iter[0] % 2
                    k2_iter[0] += 1
                    wql = mm_qlkl[(0, p)]
                    wkl = mm_qlkl[(1, p)]
                    evm = EV(lambda e, jb=jb: e.memset(bdq[jb][:], 0.0),
                             waits=[("ev", max(wql, wkl)),
                                    ("mm", mm_bdq2[jb])])
                    EV(lambda e, jb=jb, p=p, b=b: e.tensor_copy(
                        bdq[jb][0:64, 0:64],
                        ql_fm[0:64, p * 128 + b * 64:p * 128 + b * 64 + 64]))
                    e2 = EV(lambda e, jb=jb, p=p, b=b: e.tensor_copy(
                        bdq[jb][64:128, 64:128],
                        ql_fm[64:128, p * 128 + b * 64:p * 128 + b * 64 + 64]))
                    bank = 4
                    w0, c0 = bank_free[bank]
                    m1 = MM(pb[bank][:, 0:64], bdq[jb][:],
                            kl_fm[:, p * 128 + b * 64:p * 128 + b * 64 + 64],
                            start=True, stop=True, waits=[("ev", e2), (w0, c0)])
                    a1 = ACT(lambda e, b2=bank: e.activation(
                        k2e[:], pb[b2][:, 0:64], AF.Exp,
                        accum_out=cols["rowr"][:]),
                        waits=[("mm", m1), ("ev", ev_k2_prev)])
                    bank_free[bank] = ("ac", a1)
                    er = EV(lambda e: e.reciprocal(cols["tmp"][:],
                                                   cols["rowr"][:]),
                            waits=[("ac", a1), ("dG", d_k2_prev)])
                    e4 = EV(lambda e: e.tensor_scalar(
                        k2s[:], k2e[:], cols["tmp"][:], None, ALU.mult),
                        waits=[("ev", er)])
                    if "k2trace" in dump:
                        EV(lambda e: e.tensor_copy(
                            azb[0][:, 0:1], cols["rowr"][:]),
                            waits=[("dA", d_trc[0])])
                        EV(lambda e: e.tensor_copy(
                            azb[0][:, 1:2], cols["tmp"][:]))
                        EV(lambda e: e.tensor_copy(azb[0][:, 2:4], k2e[:, 0:2]))
                        e4 = EV(lambda e: e.tensor_copy(
                            azb[0][:, 4:6], k2s[:, 0:2]))
                        d_trc[0] = S.dma("scalar",
                                         kt_d[:, idx * 8:idx * 8 + 8],
                                         azb[0][:, 0:8], waits=[("ev", e4)])
                    e4d = DRAIN("vector", "ev")
                    ev_k2_prev = e4d
                    mm_bdq2[jb] = S.c("mm")
                    d_k2_prev = S.dma("gpsimd", k2_dr[idx * 128:(idx + 1) * 128, :],
                                      k2s[:], waits=[("ev", e4d)])
        d_k2st_all = d_k2_prev
        mm_qkv_end = S.c("mm")
        ac_qkv_end = S.c("ac")

        d_gath = []
        for j in range(4):
            d_gath.append(S.emit(
                "gpsimd",
                lambda e, j=j: e.indirect_dma_start(
                    nw[j]["k2b"][:], None, k2_dr,
                    IndirectOffsetOnAxis(gidx[:, j:j + 1], 0)),
                waits=[("dG", d_k2st_all)], inc="dG", inc_amt=16))


        d_wpj = S.dma("gpsimd", scC[:].rearrange("p (k q) -> p k q", k=KC),
                      wproj.rearrange("(k p) q -> p k q", p=128),
                      waits=[("mm", mm_qkv_end)])


        newton_done_ev = [None]

        def newton_gen():
            for j in range(4):
                w0, c0 = bank_free["pT"]
                m1 = TRP(pT[0:64, 0:64], nw[j]["k2b"][:], c_id[0:64, 0:64],
                         waits=[("dG", d_gath[j]), (w0, c0)])
                e1 = EV(lambda e, j=j: e.tensor_copy(nw[j]["k2t"][:],
                                                     pT[0:64, 0:64]),
                        waits=[("mm", m1)])
                bank_free["pT"] = ("ev", e1)
                yield
            for j in range(4):
                bankid = 4 if j < 2 else 5
                co = 256 + (j % 2) * 128
                w0, c0 = bank_free[bankid]
                m1 = MM(pb[bankid][0:1, co:co + 64], c_o1b[0:64, :],
                        nw[j]["k2b"][:], start=True, stop=True,
                        waits=[("dG", d_gath[j]), (w0, c0)])
                e1 = EV(lambda e, b=bankid, co=co, j=j: e.reduce_max(
                    t11[0:1, 8 + j:9 + j], pb[b][0:1, co:co + 64], AX.X),
                    waits=[("mm", m1)])
                e2 = EV(lambda e, j=j: e.reciprocal(
                    t11[0:1, 12 + j:13 + j], t11[0:1, 8 + j:9 + j]),
                    waits=[("ev", e1)])
                m2 = MM(pb[bankid][0:64, co + 64:co + 65], c_or[0:1, 0:64],
                        t11[0:1, 12 + j:13 + j], start=True, stop=True,
                        waits=[("ev", e2)])
                e3d = EV(lambda e, j=j, b=bankid, co=co: e.tensor_copy(
                    dnc[j][:], pb[b][0:64, co + 64:co + 65]),
                    waits=[("mm", m2)])
                EV(lambda e, j=j: e.tensor_scalar(
                    nw[j]["vk"][:], nw[j]["k2t"][:], dnc[j][:], None, ALU.mult),
                    waits=[("ev", e3d)])
                e5 = EV(lambda e, j=j: e.tensor_scalar(
                    nw[j]["vt"][:], nw[j]["k2b"][:], dnc[j][:], None, ALU.mult))
                bank_free[bankid] = ("ev", e5)
                yield
            for it in range(6):
                for pair in range(2):
                    bankid = 4 + pair
                    w0, c0 = bank_free[bankid]
                    mm1 = 0
                    for jj in range(2):
                        j = pair * 2 + jj
                        co = jj * 128
                        MM(pb[bankid][0:64, co:co + 64], nw[j]["k2t"][:],
                           nw[j]["vk"][:], start=True, stop=True,
                           waits=[(w0, c0)] if jj == 0 else ())
                        mm1 = MM(pb[bankid][0:64, co + 64:co + 128],
                                 nw[j]["vk"][:], nw[j]["k2t"][:],
                                 start=True, stop=True)
                    ee = 0
                    for jj in range(2):
                        j = pair * 2 + jj
                        co = jj * 128
                        EV(lambda e, j=j, b=bankid, co=co: e.scalar_tensor_tensor(
                            nw[j]["tt"][:], pb[b][0:64, co:co + 64], -1.0,
                            c_id3[:, 0:64], ALU.mult, ALU.add),
                            waits=[("mm", mm1)])
                        ee = EV(lambda e, j=j, b=bankid, co=co: e.tensor_copy(
                            nw[j]["ptb"][:], pb[b][0:64, co + 64:co + 128]))
                    yield
                    mm2 = 0
                    for jj in range(2):
                        j = pair * 2 + jj
                        co = jj * 128
                        mm2 = MM(pb[bankid][0:64, co:co + 64], nw[j]["ptb"][:],
                                 nw[j]["tt"][:], start=True, stop=True,
                                 waits=[("ev", ee)] if jj == 0 else ())
                    for jj in range(2):
                        j = pair * 2 + jj
                        co = jj * 128
                        ee = EV(lambda e, j=j, b=bankid, co=co:
                                e.scalar_tensor_tensor(
                                    nw[j]["tt"][:], pb[b][0:64, co:co + 64],
                                    -1.0, c_id3[:, 64:128], ALU.mult, ALU.add),
                                waits=[("mm", mm2)])
                    yield
                    for jj in range(2):
                        j = pair * 2 + jj
                        co = jj * 128
                        mm2 = MM(pb[bankid][0:64, co:co + 64], nw[j]["ptb"][:],
                                 nw[j]["tt"][:], start=True, stop=True,
                                 waits=[("ev", ee)] if jj == 0 else ())
                    for jj in range(2):
                        j = pair * 2 + jj
                        co = jj * 128
                        ee = EV(lambda e, j=j, b=bankid, co=co:
                                e.scalar_tensor_tensor(
                                    nw[j]["tt"][:], pb[b][0:64, co:co + 64],
                                    -1.0, c_id3[:, 128:192], ALU.mult, ALU.add),
                                waits=[("mm", mm2)])
                    yield
                    for jj in range(2):
                        j = pair * 2 + jj
                        co = jj * 128
                        MM(pb[bankid][0:64, co:co + 64], nw[j]["vt"][:],
                           nw[j]["tt"][:], start=True, stop=True,
                           waits=[("ev", ee)] if jj == 0 else ())
                        mm2 = MM(pb[bankid][0:64, co + 64:co + 128],
                                 nw[j]["tt"][:], nw[j]["vt"][:],
                                 start=True, stop=True)
                    for jj in range(2):
                        j = pair * 2 + jj
                        co = jj * 128
                        EV(lambda e, j=j, b=bankid, co=co: e.tensor_scalar(
                            nw[j]["vk"][:], pb[b][0:64, co:co + 64], 0.25,
                            None, ALU.mult), waits=[("mm", mm2)])
                        ee = EV(lambda e, j=j, b=bankid, co=co: e.tensor_scalar(
                            nw[j]["vt"][:], pb[b][0:64, co + 64:co + 128],
                            0.25, None, ALU.mult))
                    bank_free[bankid] = ("ev", ee)
                    yield
            newton_done_ev[0] = S.c("ev")

        ngen = newton_gen()

        def pump(n):
            for _ in range(n):
                if newton_done_ev[0] is not None:
                    return
                next(ngen, None)

        def pump_all():
            while newton_done_ev[0] is None:
                if next(ngen, "X") == "X":
                    break

        mm_bdk = {0: 0, 1: 0}
        for idx in range(16):
            b, p = divmod(idx, 8)
            jb = idx % 2
            evm4 = EV(lambda e, jb=jb: e.memset(bdk[jb][:], 0.0),
                      waits=[("mm", mm_bdk[jb]), ("ev", mm_qlkl[(1, p)])])
            EV(lambda e, jb=jb, p=p, b=b: e.tensor_copy(
                bdk[jb][0:64, 0:64],
                kl_fm[0:64, p * 128 + b * 64:p * 128 + b * 64 + 64]))
            e2 = EV(lambda e, jb=jb, p=p, b=b: e.tensor_copy(
                bdk[jb][64:128, 64:128],
                kl_fm[64:128, p * 128 + b * 64:p * 128 + b * 64 + 64]))
            for g in range(2):
                bank = (idx * 2 + g) % 2
                w0, c0 = bank_free[bank]
                m1 = MM(pb[bank][:, 0:512], bdk[jb][:],
                        Qf(p)[:, b * 1024 + g * 512:b * 1024 + (g + 1) * 512],
                        start=True, stop=True,
                        waits=[("ev", e2), (w0, c0), ("ac", ac_qkv_end)])
                a1 = ACT(lambda e, idx=idx, g=g, b2=bank: e.activation(
                    exp1(idx)[:, g * 512:(g + 1) * 512], pb[b2][:, 0:512],
                    AF.Exp), waits=[("mm", m1), ("mm", mm_qkv_end)])
                bank_free[bank] = ("ac", a1)
            mm_bdk[jb] = S.c("mm")
            pump(2)
            wq1_pump(4)
        ac_exp1_end = S.c("ac")

        mm_bdq5 = {0: 0, 1: 0}
        mm_a = {0: 0, 1: 0}
        d_az = 0
        d_az_prev = {0: 0, 1: 0}
        for idx in range(16):
            b, p = divmod(idx, 8)
            jb = idx % 2
            evm5 = EV(lambda e, jb=jb: e.memset(bdq[jb][:], 0.0),
                      waits=[("mm", mm_bdq5[jb]), ("ev", mm_qlkl[(0, p)])])
            EV(lambda e, jb=jb, p=p, b=b: e.tensor_copy(
                bdq[jb][0:64, 0:64],
                ql_fm[0:64, p * 128 + b * 64:p * 128 + b * 64 + 64]))
            e2 = EV(lambda e, jb=jb, p=p, b=b: e.tensor_copy(
                bdq[jb][64:128, 64:128],
                ql_fm[64:128, p * 128 + b * 64:p * 128 + b * 64 + 64]))
            pa_key = "pA" if idx % 2 == 0 else 2
            pa_ap = pA if idx % 2 == 0 else pb[2]
            wA, cA = bank_free[pa_key]
            m2 = 0
            for quad in range(2):
                bank = quad % 2
                half = quad % 2
                w0, c0 = bank_free[bank]
                m1 = 0
                for q in range(4):
                    t8 = quad * 4 + q
                    kw = []
                    if q == 0:
                        kw = [("ev", e2), (w0, c0), ("ac", ac_qkv_end)]
                    m1 = MM(pb[bank][:, q * 128:(q + 1) * 128],
                            Kf(p)[:, b * 1024 + t8 * 128:b * 1024 + (t8 + 1) * 128],
                            bdq[jb][:], start=True, stop=True, waits=kw)
                a1 = ACT(lambda e, b2=bank, h=half: e.activation(
                    scB[:, h * 1024:h * 1024 + 512], pb[b2][:, 0:512], AF.Exp),
                    waits=[("mm", m1), ("mm", mm_a[half])])
                bank_free[bank] = ("ac", a1)
                for q in range(4):
                    t8 = quad * 4 + q
                    tglob = b * 8 + t8
                    kw = [("ac", a1)]
                    if t8 == 0:
                        kw += [(wA, cA), ("ev", ev_v_end)]
                    MM(pa_ap[:, 0:128],
                       scB[:, half * 1024 + q * 128:half * 1024 + (q + 1) * 128],
                       Vt(tglob)[:, p * 128:(p + 1) * 128],
                       start=(t8 == 0), stop=(t8 == 7), waits=kw)
                    m2 = MM(pa_ap[:, 128:129],
                            scB[:, half * 1024 + q * 128:half * 1024 + (q + 1) * 128],
                            c_o1b[:], start=False, stop=(t8 == 7))
                mm_a[half] = S.c("mm")
            mm_bdq5[jb] = S.c("mm")
            eaz = EV(lambda e, i=idx % 2, a=pa_ap: e.tensor_copy(
                azb[i][:, 0:129], a[:, 0:129]),
                     waits=[("mm", m2), ("dA", max(d_az_prev[idx % 2],
                                                   d_trc[0]))])
            bank_free[pa_key] = ("ev", eaz)
            d_az = S.dma("scalar", ar_in[idx * 128:(idx + 1) * 128, 0:129],
                         azb[idx % 2][:, 0:129], waits=[("ev", eaz)])
            d_az_prev[idx % 2] = d_az
            pump(3)
            wq1_pump(4)
            wq2_pump(2)
        pump_all()
        wq1_pump(10 ** 6)
        mm_p5_end = S.c("mm")

        d_w2 = 0
        for j in range(4):
            d_w2 = S.dma("scalar", w2_in[j * 64:(j + 1) * 64, :], nw[j]["vt"][:],
                         waits=[("ev", newton_done_ev[0])])
        S.emit("gpsimd",
               lambda e: e.collective_compute(
                   "AllReduce", ALU.add, replica_groups=[list(range(NC))],
                   ins=[ar_in.opt()], outs=[ar_out.opt()]),
               waits=[("dA", d_az)], inc="cc2")
        S.emit("gpsimd",
               lambda e: e.collective_compute(
                   "AllGather", ALU.bypass, replica_groups=[list(range(NC))],
                   ins=[w2_in.opt()], outs=[w2_out.opt()]),
               waits=[("dA", d_w2)], inc="cc3")

        p_u2 = [(r * 128, ch * 1024, 1024) for r in range(8) for ch in range(4)]


        ev_azread = {0: 0, 1: 0}
        mm_bdwread = {0: 0, 1: 0}
        ev_sv_final = 0
        mm_rz_prev = 0
        for idx in range(16):
            b, p = divmod(idx, 8)
            i2 = idx % 2
            r0 = (idx // 2) * 256 + (idx % 2) * 128
            d1 = S.dma("scalar", azb[i2][:, 0:132],
                       ar_out[idx * 128:(idx + 1) * 128, :],
                       waits=[("cc2", 1), ("ev", ev_azread[i2])])
            d2a = S.dma("scalar", bdw[i2][0:64, 0:64], w2_out[r0:r0 + 64, :],
                        waits=[("cc3", 1), ("mm", mm_bdwread[i2])])
            d2b = S.dma("scalar", bdw[i2][64:128, 64:128],
                        w2_out[r0 + 64:r0 + 128, :])
            if idx < 2:
                EV(lambda e, i2=i2: e.memset(bdw[i2][0:64, 64:128], 0.0))
                EV(lambda e, i2=i2: e.memset(bdw[i2][64:128, 0:64], 0.0))
                EV(lambda e, i2=i2: e.memset(bsm[i2][:], 0.0))
            e1 = EV(lambda e, i2=i2: e.reciprocal(cols["tmp"][:],
                                                  azb[i2][:, 128:129]),
                    waits=[("dA", d1)])
            e1b = EV(lambda e, i2=i2: e.tensor_scalar(
                k3v[i2][:], azb[i2][:, 0:128], cols["tmp"][:], None, ALU.mult),
                waits=[("ev", e1)])
            ev_azread[i2] = e1b
            bank = 2
            w0, c0 = bank_free[bank]
            m1 = MM(pb[bank][:, 0:128], bdw[i2][:], k3v[i2][:],
                    start=True, stop=True,
                    waits=[("ev", e1b), ("dA", d2b), (w0, c0),
                           ("mm", mm_bdwread[i2])])
            e3 = EV(lambda e, b2=bank, i2=i2: e.tensor_copy(
                bsm[i2][0:64, 0:64], pb[b2][0:64, 0:64]), waits=[("mm", m1)])
            e4 = EV(lambda e, b2=bank, i2=i2: e.tensor_copy(
                bsm[i2][64:128, 64:128], pb[b2][64:128, 64:128]))
            bank_free[bank] = ("ev", e4)
            for g in range(2):
                bank = 3
                w0, c0 = bank_free[bank]
                m2b = MM(pb[bank][0:2, 0:512], c_p1[:],
                         exp1(idx)[:, g * 512:(g + 1) * 512],
                         start=True, stop=True,
                         waits=[("ac", ac_exp1_end), (w0, c0)])
                e5 = EV(lambda e, b2=bank: e.tensor_copy(zc[:],
                                                         pb[b2][0:2, 0:512]),
                        waits=[("mm", m2b)] +
                              ([("ev", p1d_out[0])] if idx == 0 and g == 0 else []))
                e6 = EV(lambda e: e.reciprocal(rz[:], zc[:]),
                        waits=[("mm", mm_rz_prev), ("ev", e5)])
                w4, c4 = bank_free[4]
                m3 = MM(pb[4][:, 0:512], c_bc[:], rz[:],
                        start=True, stop=True, waits=[("ev", e6), (w4, c4)])
                mm_rz_prev = m3
                e7 = EV(lambda e: e.tensor_copy(bcs[:], pb[4][:, 0:512]),
                        waits=[("mm", m3)])
                bank_free[4] = ("ev", e7)
                bank_free[bank] = ("ev", e5)
                bank = (idx * 2 + g) % 2
                w0, c0 = bank_free[bank]
                m4 = MM(pb[bank][:, 0:512], bsm[i2][:],
                        exp1(idx)[:, g * 512:(g + 1) * 512],
                        start=True, stop=True, waits=[("ev", e4), (w0, c0)])
                e8 = EV(lambda e, p=p, b=b, g=g, b2=bank: e.tensor_tensor(
                    SVf(p)[:, b * 1024 + g * 512:b * 1024 + (g + 1) * 512],
                    pb[b2][:, 0:512], bcs[:], ALU.mult),
                    waits=[("mm", m4), ("ev", e7)])
                bank_free[bank] = ("ev", e8)
            mm_bdwread[i2] = S.c("mm")
            ev_sv_final = S.c("ev")
            wq2_pump(4)
        mm_g_end = S.c("mm")

        d_wgq1 = S.dma("sync", S3[:].rearrange("p (k i) -> p k i", k=4),
                       wgq_dr[0:512, :].rearrange("(k p) i -> p k i", p=128),
                       waits=[("dS", p2g_out[1]), ("mm", mm_p5_end)])
        S.dma("sync", S4[:].rearrange("p (k i) -> p k i", k=4),
              wgq_dr[512:1024, :].rearrange("(k p) i -> p k i", p=128))
        S.dma("sync", S1[:].rearrange("p (k i) -> p k i", k=4),
              wuq_dr[0:512, :].rearrange("(k p) i -> p k i", p=128),
              waits=[("dS", p2u_out[1]), ("mm", mm_g_end)])

        def wgq(k):
            if k < 4:
                return S3[:, k * 4096:(k + 1) * 4096]
            return S4[:, (k - 4) * 4096:(k - 3) * 4096]

        def wuq(k):
            if k < 4:
                return S1[:, k * 4096:(k + 1) * 4096]
            return S2[:, (k - 4) * 4096:(k - 3) * 4096]

        d_xres = {}
        ev_xread = {0: 0, 1: 0}
        ev_v1_free = 0
        ev_y1_free = 0
        d_y1_prev = 0
        ac_sq_prev = 0
        gp_ei_reader = 0
        d_y1q_prev = 0
        d_y1q_all = 0
        for t in range(NT):
            bankA = (t % 2) * 2
            bankB = bankA + 1
            xw = [("ev", ev_xread[t % 2])]
            if t == 0:
                xw.append(("ev", p1d_out[0]))
                xw.append(("mm", S.c("mm")))
            d_xres[t] = S.dma("scalar",
                              CF1[:, (t % 2) * 1024:(t % 2) * 1024 + 1024],
                              x_tm[t * 128:(t + 1) * 128, :], waits=xw)
            e1 = 0
            for og in range(2):
                bank = bankA if og == 0 else bankB
                w0, c0 = bank_free[bank]
                for k in range(KC):
                    kw = []
                    if k == 0:
                        kw = [("dG", d_wpj), (w0, c0), ("ev", ev_sv_final)]
                    mmid = MM(pb[bank][:, 0:512],
                              SVf(k)[:, t * 128:(t + 1) * 128],
                              scC[:, k * 1024 + og * 512:k * 1024 + (og + 1) * 512],
                              start=(k == 0), stop=(k == KC - 1), waits=kw)
                e1 = EV(lambda e, t=t, og=og, b2=bank: e.tensor_add(
                    CF1[:, 2048 + og * 512:2048 + (og + 1) * 512],
                    pb[b2][:, 0:512],
                    CF1[:, (t % 2) * 1024 + og * 512:
                        (t % 2) * 1024 + (og + 1) * 512]),
                    waits=[("mm", mmid), ("dA", d_xres[t]), ("ev", ev_v1_free)])
                bank_free[bank] = ("ev", e1)
            ev_xread[t % 2] = e1
            es = EV(lambda e: e.reduce_sum(cols["sum"][:], CF1[:, 2048:3072],
                                           AX.X))
            e2 = EV(lambda e: e.tensor_scalar(
                cols["negm"][:], cols["sum"][:], -1.0 / D, None, ALU.mult),
                waits=[("ev", es)])
            w5, c5 = bank_free[5]
            a1 = ACT(lambda e: e.activation(
                pb[5][:, 0:512], CF1[:, 2048:2560], AF.Square,
                bias=cols["negm"][:], accum_out=cols["ssq"][:]),
                waits=[("ev", e2), (w5, c5), ("ac", ac_sq_prev)])
            a2 = ACT(lambda e: e.activation(
                pb[5][:, 0:512], CF1[:, 2560:3072], AF.Square,
                bias=cols["negm"][:], accum_out=cols["ssqb"][:]))
            bank_free[5] = ("ac", a2)
            ac_sq_prev = a2
            e3 = EV(lambda e: e.tensor_add(cols["ssq"][:], cols["ssq"][:],
                                           cols["ssqb"][:]),
                    waits=[("ac", a2)])
            e3b = EV(lambda e: e.tensor_scalar(
                cols["ssq"][:], cols["ssq"][:], 1.0 / D, 1e-5,
                ALU.mult, ALU.add), waits=[("ev", e3)])
            a3 = ACT(lambda e: e.activation(cols["sd"][:], cols["ssq"][:],
                                            AF.Sqrt),
                     waits=[("ev", e3b)])
            e4 = EV(lambda e: e.reciprocal(cols["rstd"][:], cols["sd"][:]),
                    waits=[("ac", a3)])
            e5 = EV(lambda e: e.tensor_scalar(
                CF1[:, 3072:4096], CF1[:, 2048:3072], cols["negm"][:],
                cols["rstd"][:], ALU.add, ALU.mult),
                waits=[("ev", e4), ("dA", d_y1q_prev)])
            ev_v1_free = e5
            d_y1_prev = S.dma("scalar", y1_dr[t * 128:(t + 1) * 128, :],
                              CF1[:, 3072:4096], waits=[("ev", e5)])
            e6 = EV(lambda e: e.reduce_max(cols["rmax"][:], CF1[:, 3072:4096],
                                           AX.X, apply_absolute_value=True))
            e7 = EV(lambda e, t=t: e.tensor_scalar(
                rsa_all[:, t:t + 1], cols["rmax"][:], 1e-5, 1.0 / 127,
                ALU.max, ALU.mult), waits=[("ev", e6)])
            e8 = EV(lambda e, t=t: e.reciprocal(cols["sa"][:],
                                                rsa_all[:, t:t + 1]),
                    waits=[("ev", e7)])
            for hh in range(2):
                EV(lambda e, hh=hh: e.tensor_scalar(
                    EI[:], CF1[:, 3072 + hh * 512:3072 + (hh + 1) * 512],
                    cols["sa"][:], None, ALU.mult),
                    waits=[("ev", e8)])
                EV(lambda e, hh=hh: e.tensor_copy(
                    scB[:, hh * 512:(hh + 1) * 512], EI[:]),
                    waits=[("dA", d_y1q_prev)])
            evd = DRAIN("vector", "ev")
            ev_y1_free = S.c("ev")
            d_y1q_prev = S.dma("scalar", y1q_dr[t * 128:(t + 1) * 128, :],
                               scB[:, 0:1024], waits=[("ev", evd)])
            d_y1q_all = d_y1q_prev
            wq2_pump(4)
        wq2_pump(10 ** 6)
        mm_proj_end = S.c("mm")

        d_st_prev = {0: 0, 1: 0}
        d_y1f_prev = 0
        for cch in range(KC):
            buf = scE[:] if cch % 2 == 0 else scC[:, 0:2048]
            wv_ = [("dA", d_y1q_all)]
            if cch % 2 == 1 and cch == 1:
                wv_.append(("mm", mm_proj_end))
            if d_st_prev[cch % 2]:
                wv_.append(("dS", d_st_prev[cch % 2]))
            dtr = S.dmat("sync", buf, y1q_dr[:, cch * 128:(cch + 1) * 128],
                         waits=wv_)
            d_st_prev[cch % 2] = S.dma(
                "sync", y1qfm_dr[cch * 128:(cch + 1) * 128, :],
                buf, waits=[("dS", dtr)])
        d_y1qfm_all = d_st_prev[1]

        wq3 = [None]

        def wq3_pump(n):
            if wq3[0] is None:
                wq3[0] = pass2_gen(wdT, "wsds", wdq_dr, p_d, wsd_box, CFD,
                                   1024, p2u_out, p2d_out)
            for _ in range(n):
                next(wq3[0], None)


        d_wq_all = S.dma("sync", S2[:].rearrange("p (k i) -> p k i", k=4),
                         wuq_dr[512:1024, :].rearrange("(k p) i -> p k i",
                                                       p=128),
                         waits=[("mm", mm_proj_end)])

        ev_dqg = EV(lambda e: e.tensor_scalar(
            rsa2_all[:], rsa_all[:], cols["wsgd"][:], None, ALU.mult))
        ev_dqu = EV(lambda e: e.tensor_scalar(
            rsa3_all[:], rsa_all[:], cols["wsud"][:], None, ALU.mult))

        f1t_d = dt("dbg_f1trace", [128, 4096], F32, "ExternalOutput") \
            if "f1trace" in dump else None
        mm_f1_tile = {0: 0, 1: 0}
        ac_f1_prev = [0]
        eh_slot = {0: 0, 1: 0, 2: 0}
        gp_hq_prev = 0
        gp_ei_last = [0]
        d_hq_slot = {0: 0, 1: 0}
        d_hq_all = 0
        ev_h_last = 0
        for t in range(NT):
            d_y1f = S.dma("scalar", scE[:, (t % 2) * 1024:(t % 2) * 1024 + 1024]
                          .rearrange("p (k q) -> p k q", k=KC),
                          y1qfm_dr[:, t * 128:(t + 1) * 128]
                          .rearrange("(k p) q -> p k q", p=128),
                          waits=([("dS", d_y1qfm_all)] if t == 0 else []) +
                                [("mm", mm_f1_tile[t % 2])])

            def stat(k, t=t):
                return scE[:, (t % 2) * 1024 + k * 128:
                           (t % 2) * 1024 + (k + 1) * 128]


            for ig in range(8):
                bg = (ig % 3) * 2
                bu = bg + 1
                gslot = ig % 3
                gbuf = (scB[:, gslot * 512:(gslot + 1) * 512] if gslot < 2
                        else scC[:, 0:512])
                wg0, cg0 = bank_free[bg]
                mg = 0
                for k in range(KC):
                    kw = []
                    if k == 0:
                        kw = [("dA", d_y1f), ("dS", d_wq_all), (wg0, cg0)]
                    mg = MM(pb[bg][:, 0:512], stat(k),
                            wgq(k)[:, ig * 512:(ig + 1) * 512],
                            start=(k == 0), stop=(k == KC - 1), waits=kw)
                wu0, cu0 = bank_free[bu]
                mu = 0
                for k in range(KC):
                    kw = [(wu0, cu0)] if k == 0 else []
                    mu = MM(pb[bu][:, 0:512], stat(k),
                            wuq(k)[:, ig * 512:(ig + 1) * 512],
                            start=(k == 0), stop=(k == KC - 1), waits=kw)
                a1 = ACT(lambda e, b2=bg, gb=gbuf, t=t: e.activation(
                    gb, pb[b2][:, 0:512],
                    AF.Silu, scale=rsa2_all[:, t:t + 1]),
                    waits=[("ev", ev_dqg), ("mm", mg), ("ev", eh_slot[gslot]),
                           ("dA", d_y1q_all if t == 0 else 0),
                           ("dS", d_y1qfm_all if t == 0 else 0)])
                bank_free[bg] = ("ac", a1)
                eh = EV(lambda e, b2=bu, ig=ig, gb=gbuf, t=t:
                        e.scalar_tensor_tensor(
                            CF1[:, ig * 512:(ig + 1) * 512], pb[b2][:, 0:512],
                            rsa3_all[:, t:t + 1],
                            gb, ALU.mult,
                            ALU.mult),
                        waits=[("ac", a1), ("ev", ev_dqu), ("mm", mu)])
                bank_free[bu] = ("ev", eh)
                eh_slot[gslot] = eh
            if "f1trace" in dump and t == 0:
                ehd0 = DRAIN("vector", "ev")
                d_f1t = S.dma("sync", f1t_d, CF1[:, 0:4096],
                              waits=[("ev", ehd0)])
                e_g = EV(lambda e: e.tensor_copy(azb[0][:, 0:1],
                                                 cols["dqg"][:]),
                         waits=[("dS", d_f1t)])
            ehd = DRAIN("vector", "ev")
            g1 = EV(lambda e: e.tensor_reduce(
                cols["tmp2"][:], CF1[:], AX.X, ALU.max, True),
                waits=[("ev", ehd)])
            e10 = EV(lambda e, t=t: e.tensor_scalar(
                rs2_all[:, t:t + 1], cols["tmp2"][:], 1e-5, 1.0 / 127,
                ALU.max, ALU.mult), waits=[("ev", g1)])
            e11 = EV(lambda e, t=t: e.reciprocal(cols["s2"][:],
                                                 rs2_all[:, t:t + 1]),
                     waits=[("ev", e10)])
            for q in range(8):
                slot = (q // 2) % 2
                hbuf = scB[:, 1024:2048] if slot == 0 else scC[:, 1024:2048]
                EV(lambda e, q=q: e.tensor_scalar(
                    EI[:], CF1[:, q * 512:(q + 1) * 512], cols["s2"][:], None,
                    ALU.mult), waits=[("ev", e11)])
                EV(lambda e, q=q, hb=hbuf: e.tensor_copy(
                    hb[:, (q % 2) * 512:(q % 2 + 1) * 512], EI[:]),
                    waits=[("dS", d_hq_slot[slot])] +
                          ([("dS", d_y1qfm_all)]
                           if t == 0 and q == 2 else []))
                if q % 2 == 1:
                    evd2 = DRAIN("vector", "ev")
                    d_hq = S.dma("sync",
                                 hq_dr[t * 128:(t + 1) * 128,
                                       (q - 1) * 512:(q + 1) * 512],
                                 hbuf, waits=[("ev", evd2)])
                    d_hq_slot[slot] = d_hq
                    d_hq_all = d_hq
            ev_h_last = S.c("ev")
            mm_f1_tile[t % 2] = S.c("mm")
            ac_f1_prev[0] = S.c("ac")
            wq3_pump(4)
        wq3_pump(10 ** 6)
        mm_f1_end = S.c("mm")
        ac_f1_end = S.c("ac")

        ev_dqd = EV(lambda e: e.tensor_scalar(
            rs2d_all[:], rs2_all[:], cols["wsdd"][:], None, ALU.mult))
        d_wdc = []
        for ic in range(32):
            dst = (S2[:, ic * 1024:(ic + 1) * 1024] if ic < 16 else
                   S1[:, (ic - 16) * 1024:(ic - 15) * 1024])
            kw = ([("dS", p2d_out[1]), ("mm", mm_f1_end)] if ic == 0 else [])
            d_wdc.append(S.dma("gpsimd", dst,
                               wdq_dr[ic * 128:(ic + 1) * 128, :], waits=kw))

        def wdq(ic):
            if ic < 16:
                return S2[:, ic * 1024:(ic + 1) * 1024]
            return S1[:, (ic - 16) * 1024:(ic - 15) * 1024]

        ev_v2_free = 0
        d_out_prev = 0
        ac_sq2_prev = 0
        mm_tb_read = {0: mm_f1_end, 1: mm_f1_end}
        ev_y1r_free = {0: 0, 1: 0}
        for tb in range(4):
            slab = S3 if tb % 2 == 0 else S4
            d_hqf = 0
            for ic in range(ICN):
                kw = []
                if ic == 0:
                    kw = [("dS", d_hq_all), ("mm", mm_tb_read[tb % 2])]
                d_hqf = S.dmat("sync", slab[:, ic * 512:(ic + 1) * 512],
                               hq_dr[tb * 512:(tb + 1) * 512,
                                     ic * 128:(ic + 1) * 128], waits=kw)
            for t4 in range(4):
                t = tb * 4 + t4
                yw = [("ev", ev_y1r_free[t4 % 2])]
                if tb == 0 and t4 == 0:
                    yw += [("ev", ev_h_last)]
                d_y1r = S.dma("scalar",
                              CF1[:, (t4 % 2) * 1024:(t4 % 2) * 1024 + 1024],
                              y1_dr[t * 128:(t + 1) * 128, :], waits=yw)

                e1 = 0
                for og in range(2):
                    bank = t4 % 2 + og * 2
                    w0, c0 = bank_free[bank]
                    mmid = 0
                    for ic in range(ICN):
                        kw = []
                        if ic == 0:
                            kw = [("dS", d_hqf), (w0, c0)]
                        if tb == 0 and t4 == 0 and og == 0:
                            kw.append(("dG", d_wdc[ic]))
                        mmid = MM(pb[bank][:, 0:512],
                                  slab[:, ic * 512 + t4 * 128:
                                       ic * 512 + (t4 + 1) * 128],
                                  wdq(ic)[:, og * 512:(og + 1) * 512],
                                  start=(ic == 0), stop=(ic == ICN - 1),
                                  waits=kw)
                    e1 = EV(lambda e, og=og, t4=t4, b2=bank, t=t:
                            e.scalar_tensor_tensor(
                                CF1[:, 2048 + og * 512:2048 + (og + 1) * 512],
                                pb[b2][:, 0:512], rs2d_all[:, t:t + 1],
                                CF1[:, (t4 % 2) * 1024 + og * 512:
                                    (t4 % 2) * 1024 + (og + 1) * 512],
                                ALU.mult, ALU.add),
                            waits=[("mm", mmid), ("ev", ev_dqd), ("dA", d_y1r),
                                   ("ev", ev_v2_free)])
                    bank_free[bank] = ("ev", e1)
                ev_y1r_free[t4 % 2] = e1
                es = EV(lambda e: e.reduce_sum(cols["sum"][:],
                                               CF1[:, 2048:3072], AX.X))
                e2 = EV(lambda e: e.tensor_scalar(
                    cols["negm"][:], cols["sum"][:], -1.0 / D, None, ALU.mult),
                    waits=[("ev", es)])
                w5, c5 = bank_free[5]
                aw = [("ev", e2), (w5, c5), ("ac", ac_sq2_prev)]
                if tb == 0 and t4 == 0:
                    aw.append(("ac", ac_f1_end))
                a1 = ACT(lambda e: e.activation(
                    pb[5][:, 0:512], CF1[:, 2048:2560], AF.Square,
                    bias=cols["negm"][:], accum_out=cols["ssq"][:]), waits=aw)
                a2 = ACT(lambda e: e.activation(
                    pb[5][:, 0:512], CF1[:, 2560:3072], AF.Square,
                    bias=cols["negm"][:], accum_out=cols["ssqb"][:]))
                bank_free[5] = ("ac", a2)
                ac_sq2_prev = a2
                e3 = EV(lambda e: e.tensor_add(cols["ssq"][:], cols["ssq"][:],
                                               cols["ssqb"][:]),
                        waits=[("ac", a2)])
                e3b = EV(lambda e: e.tensor_scalar(
                    cols["ssq"][:], cols["ssq"][:], 1.0 / D, 1e-5,
                    ALU.mult, ALU.add), waits=[("ev", e3)])
                a3 = ACT(lambda e: e.activation(cols["sd"][:], cols["ssq"][:],
                                                AF.Sqrt),
                         waits=[("ev", e3b)])
                e4r = EV(lambda e: e.reciprocal(cols["rstd"][:],
                                                cols["sd"][:]),
                         waits=[("ac", a3)])
                e5 = EV(lambda e: e.tensor_scalar(
                    CF1[:, 3072:4096], CF1[:, 2048:3072], cols["negm"][:],
                    cols["rstd"][:], ALU.add, ALU.mult),
                    waits=[("ev", e4r), ("dA", d_out_prev)])
                ev_v2_free = e5
                d_out_prev = S.dma("scalar", out_d[t * 128:(t + 1) * 128, :],
                                   CF1[:, 3072:4096], waits=[("ev", e5)])
            mm_tb_read[tb % 2] = S.c("mm")

        if "k2probe" in dump:
            kp = dt("dbg_k2probe", [128, 132], F32, "ExternalOutput")
            wfin = [("dA", S.c("dA")), ("dS", S.c("dS")), ("dG", S.c("dG")),
                    ("ev", S.c("ev")), ("ac", S.c("ac")), ("gp", S.c("gp")),
                    ("mm", S.c("mm"))]
            e_kp = EV(lambda e: e.tensor_copy(CF1[:, 0:1], cols["rowr"][:]),
                      waits=wfin)
            EV(lambda e: e.tensor_copy(CF1[:, 1:2], cols["tmp"][:]))
            EV(lambda e: e.tensor_copy(CF1[:, 2:66], k2e[:]))
            EV(lambda e: e.tensor_copy(CF1[:, 66:130], k2s[:]))
            EV(lambda e: e.tensor_copy(CF1[:, 130:131], cols["wsgs"][:]))
            e_kp2 = EV(lambda e: e.tensor_copy(CF1[:, 131:132], cols["wsgd"][:]))
            S.dma("sync", kp[:, 0:132], CF1[:, 0:132], waits=[("ev", e_kp2)])

        dumpable = {"y1_dr": y1_dr, "y1q_dr": y1q_dr, "y1qfm_dr": y1qfm_dr,
                    "hq_dr": hq_dr, "wgq_dr": wgq_dr, "wuq_dr": wuq_dr,
                    "wdq_dr": wdq_dr, "k2_dr": k2_dr, "ag_out": ag_out,
                    "ar_in": ar_in, "ar_out": ar_out, "w2_out": w2_out}
        shapes = {"y1_dr": ([TL, D], F32), "y1q_dr": ([TL, D], BF16),
                  "y1qfm_dr": ([D, TL], BF16), "hq_dr": ([TL, DI], BF16),
                  "wgq_dr": ([D, DI], BF16), "wuq_dr": ([D, DI], BF16),
                  "wdq_dr": ([DI, D], BF16), "k2_dr": ([2048, 64], BF16),
                  "ag_out": ([128, D], BF16), "ar_in": ([2048, 132], F32),
                  "ar_out": ([2048, 132], F32), "w2_out": ([2048, 64], BF16)}
        for nmd in dump:
            if nmd in ("k2probe", "k2trace", "f1trace"):
                continue
            sh, dty = shapes[nmd]
            od = dt("dbg_" + nmd, sh, dty, "ExternalOutput")
            S.dma("sync", od, dumpable[nmd],
                  waits=[("dA", S.c("dA")), ("dS", S.c("dS")),
                         ("dG", S.c("dG")), ("ev", S.c("ev")),
                         ("ac", S.c("ac")), ("gp", S.c("gp")),
                         ("mm", S.c("mm")), ("cc2", 1), ("cc3", 1)])

        S.emit("sync", lambda e: e.nop(),
               waits=[("dS", S.c("dS")), ("dA", S.c("dA")), ("dG", S.c("dG")),
                      ("ev", S.c("ev")), ("ac", S.c("ac")), ("gp", S.c("gp")),
                      ("mm", S.c("mm"))])

        S.build(blk, sems)
    return nc


_NC_CACHE = [None]


def _consts():
    c = {}
    c["ident"] = np.eye(128).astype(ml_dtypes.bfloat16)
    id3 = np.zeros((64, 192), np.float32)
    id3[:, 0:64] = 7.0 * np.eye(64)
    id3[:, 64:128] = 15.0 * np.eye(64)
    id3[:, 128:192] = 13.0 * np.eye(64)
    c["id3"] = id3
    p1 = np.zeros((128, 2), np.float32)
    p1[0:64, 0] = 1.0
    p1[64:128, 1] = 1.0
    c["pair1"] = p1.astype(ml_dtypes.bfloat16)
    bcm = np.zeros((2, 128), np.float32)
    bcm[0, 0:64] = 1.0
    bcm[1, 64:128] = 1.0
    c["bcmask"] = bcm
    c["onesr"] = np.ones((1, 128), np.float32)
    c["o1b"] = np.ones((128, 1), np.float32).astype(ml_dtypes.bfloat16)
    c["o1f"] = np.ones((128, 1), np.float32)
    return c


def kernel(L0_attn_x, L0_attn_w_qkv, L0_attn_w_proj,
           L0_ffn_w_gate, L0_ffn_w_up, L0_ffn_w_down, _dump=(), _raw=False):
    x = np.ascontiguousarray(np.asarray(L0_attn_x, np.float32))
    wqkv = np.ascontiguousarray(np.asarray(L0_attn_w_qkv, np.float32))
    wpj = np.ascontiguousarray(np.asarray(L0_attn_w_proj, np.float32))
    wgT = np.ascontiguousarray(np.asarray(L0_ffn_w_gate, np.float32).T)
    wuT = np.ascontiguousarray(np.asarray(L0_ffn_w_up, np.float32).T)
    wdT = np.ascontiguousarray(np.asarray(L0_ffn_w_down, np.float32).T)

    if _dump:
        nc = build_program(dump=_dump)
    else:
        if _NC_CACHE[0] is None:
            _NC_CACHE[0] = build_program()
        nc = _NC_CACHE[0]

    cs = _consts()
    in_maps = []
    for c in range(NC):
        xs = np.ascontiguousarray(x[:, c * NL:(c + 1) * NL, :]).reshape(TL, D)
        gi = np.zeros((64, 4), np.int32)
        for j in range(4):
            jj, h = divmod(j, 2)
            idx = 2 * c + jj
            gi[:, j] = idx * 128 + h * 64 + np.arange(64)
        m = {
            "x_tm": xs,
            "x_fmT": np.ascontiguousarray(xs.T),
            "wqkv": wqkv, "wproj": wpj,
            "wgT": wgT, "wuT": wuT, "wdT": wdT,
            "gidx": gi,
        }
        m.update(cs)
        in_maps.append(m)

    res = run_bass_kernel_spmd(nc, in_maps, core_ids=list(range(NC)))
    if _raw:
        return res
    full = np.empty((B, N, D), np.float32)
    for c in range(NC):
        r = np.asarray(res.results[c]["out"], np.float32)
        full[:, c * NL:(c + 1) * NL, :] = r.reshape(B, NL, D)
    return full


# version 17 (final)
# speedup vs baseline: 1.0818x; 1.0046x over previous
import numpy as np
import ml_dtypes
import contextlib
import concourse.bass as bass
import concourse.mybir as mybir
from concourse.bass import IndirectOffsetOnAxis
from concourse.bass_utils import run_bass_kernel_spmd

F32 = mybir.dt.float32
BF16 = mybir.dt.bfloat16
I32 = mybir.dt.int32
AF = mybir.ActivationFunctionType
ALU = mybir.AluOpType
AX = mybir.AxisListType

NC = 8
B = 2
N = 8192
D = 1024
NL = N // NC
TL = B * NL
NT = TL // 128
DI = 4096
KC = D // 128
ICN = DI // 128

_ENGINES = ("sync", "scalar", "vector", "tensor", "gpsimd")
_DMA_SEM = {"sync": "dS", "scalar": "dA", "gpsimd": "dG"}


class Sched:
    def __init__(self):
        self.count = {}
        self.ops = {e: [] for e in _ENGINES}

    def c(self, sem):
        return self.count.get(sem, 0)

    def emit(self, engine, fn, waits=(), inc=None, inc_amt=1):
        waits = [(s, v) for (s, v) in waits if v > 0]
        for s, v in waits:
            assert v <= self.c(s), f"wait {s}>={v} before inc (have {self.c(s)})"
        self.ops[engine].append((tuple(waits), fn, inc, inc_amt))
        if inc is not None:
            self.count[inc] = self.c(inc) + inc_amt
            return self.count[inc]
        return None

    def dma(self, queue, out, in_, waits=()):
        return self.emit(queue, lambda e, o=out, i=in_: e.dma_start(out=o, in_=i),
                         waits=waits, inc=_DMA_SEM[queue], inc_amt=16)

    def dmat(self, queue, out, in_, waits=()):
        return self.emit(queue, lambda e, o=out, i=in_: e.dma_start_transpose(o, i),
                         waits=waits, inc=_DMA_SEM[queue], inc_amt=16)

    def build(self, block, sems):
        for eng_name in _ENGINES:
            ops = self.ops[eng_name]

            def mk(ops):
                def run(eng):
                    last = {}
                    for waits, fn, inc, amt in ops:
                        for s, v in waits:
                            if last.get(s, 0) < v:
                                eng.wait_ge(sems[s], v)
                                last[s] = v
                        ins = fn(eng)
                        if inc is not None:
                            ins.then_inc(sems[inc], amt)
                return run
            getattr(block, eng_name)(mk(ops))


def build_program(dump=()):
    nc = bass.Bass("TRN2", target_bir_lowering=False)
    dt = lambda n, s, d, k="Internal": nc.dram_tensor(n, s, d, kind=k).ap()

    x_tm = dt("x_tm", [TL, D], F32, "ExternalInput")
    x_fmT = dt("x_fmT", [D, TL], F32, "ExternalInput")
    wqkv = dt("wqkv", [D, 3 * D], F32, "ExternalInput")
    wproj = dt("wproj", [D, D], F32, "ExternalInput")
    wgT = dt("wgT", [D, DI], F32, "ExternalInput")
    wuT = dt("wuT", [D, DI], F32, "ExternalInput")
    wdT = dt("wdT", [DI, D], F32, "ExternalInput")
    ident_d = dt("ident", [128, 128], BF16, "ExternalInput")
    id3_d = dt("id3", [64, 192], F32, "ExternalInput")
    pair1_d = dt("pair1", [128, 2], BF16, "ExternalInput")
    bcmask_d = dt("bcmask", [2, 128], F32, "ExternalInput")
    onesr_d = dt("onesr", [1, 128], F32, "ExternalInput")
    o1b_d = dt("o1b", [128, 1], BF16, "ExternalInput")
    o1f_d = dt("o1f", [128, 1], F32, "ExternalInput")
    gidx_d = dt("gidx", [64, 4], I32, "ExternalInput")
    out_d = dt("out", [TL, D], F32, "ExternalOutput")

    y1_dr = dt("y1_dr", [TL, D], F32)
    y1q_dr = dt("y1q_dr", [TL, D], BF16)
    y1qfm_dr = dt("y1qfm_dr", [D, TL], BF16)
    hq_dr = dt("hq_dr", [TL, DI], BF16)
    wgq_dr = dt("wgq_dr", [D, DI], BF16)
    wuq_dr = dt("wuq_dr", [D, DI], BF16)
    wdq_dr = dt("wdq_dr", [DI, D], BF16)
    k2_dr = dt("k2_dr", [2048, 64], BF16)
    ag_in = dt("ag_in", [16, D], BF16)
    ag_out = dt("ag_out", [128, D], BF16)
    ar_in = dt("ar_in", [2048, 132], F32)
    ar_out = dt("ar_out", [2048, 132], F32)
    w2_in = dt("w2_in", [256, 64], BF16)
    w2_out = dt("w2_out", [2048, 64], BF16)

    S = Sched()
    semnames = ["dS", "dA", "dG", "mm", "ev", "ac", "gp", "cc1", "cc2", "cc3"]

    with contextlib.ExitStack() as st:
        ent = st.enter_context
        sems = {n: ent(nc.semaphore(n)) for n in semnames}
        sb = lambda n, s, d: ent(nc.sbuf_tensor(n, s, d))

        S1 = sb("S1", [128, 16384], BF16)
        S2 = sb("S2", [128, 16384], BF16)
        S3 = sb("S3", [128, 16384], BF16)
        S4 = sb("S4", [128, 16384], BF16)
        scC = sb("scC", [128, 8192], BF16)
        scB = sb("scB", [128, 2048], BF16)
        scD = sb("scD", [128, 2048], BF16)
        scE = sb("scE", [128, 2048], BF16)
        CF1 = sb("CF1", [128, 4096], F32)
        CFD = sb("CFD", [128, 2048], F32)
        EI = sb("EI", [128, 512], I32)
        EI2 = sb("EI2", [128, 512], I32)
        bcs = sb("bcs", [128, 512], F32)
        zc = CF1[0:2, 0:512]
        rz = CF1[0:2, 512:1024]
        c_id = sb("c_id", [128, 128], BF16)
        c_id3 = sb("c_id3", [64, 192], F32)
        c_p1 = sb("c_p1", [128, 2], BF16)
        c_bc = sb("c_bc", [2, 128], F32)
        c_or = sb("c_or", [1, 128], F32)
        c_o1b = sb("c_o1b", [128, 1], BF16)
        c_o1f = sb("c_o1f", [128, 1], F32)
        gidx = sb("gidx_s", [64, 4], I32)
        xlpf = sb("xlpf", [128, 128], F32)
        xlp = sb("xlp", [128, 128], BF16)
        xlt = sb("xlt", [128, 128], BF16)
        xl_g = sb("xl_g", [128, 1024], BF16)
        xl_fm = sb("xl_fm", [128, 1024], BF16)
        ql_fm = sb("ql_fm", [128, 1024], BF16)
        kl_fm = sb("kl_fm", [128, 1024], BF16)
        bdq = [sb(f"bdq{i}", [128, 128], BF16) for i in range(2)]
        bdk = [sb(f"bdk{i}", [128, 128], BF16) for i in range(2)]
        bdw = [sb(f"bdw{i}", [128, 128], BF16) for i in range(2)]
        bsm = [sb(f"bsm{i}", [128, 128], BF16) for i in range(2)]
        k3v = [sb(f"k3v{i}", [128, 128], BF16) for i in range(2)]
        azb = [sb(f"azb{i}", [128, 132], F32) for i in range(2)]
        k2e = sb("k2e", [128, 64], F32)
        k2s = sb("k2s", [128, 64], BF16)
        nw = [{k: sb(f"nw_{k}{j}", [64, 64], BF16) for k in
               ("k2b", "k2t", "vk", "vt", "ptb", "tt")} for j in range(4)]
        dnc = [sb(f"dnc{j}", [64, 1], F32) for j in range(4)]
        colnames = ["sum", "negm", "ssq", "ssqb", "sd", "rstd", "rmax", "sa",
                    "dqg", "dqu", "dqd", "s2", "tmp", "tmp2",
                    "wsgs", "wsgd", "wsus", "wsud", "wsds", "wsdd",
                    "accg", "accu", "accd", "rowr"]
        cols = {n: sb("col_" + n, [128, 1], F32) for n in colnames}
        rsa_all = sb("rsa_all", [128, 16], F32)
        rsa2_all = sb("rsa2_all", [128, 16], F32)
        rsa3_all = sb("rsa3_all", [128, 16], F32)
        rs2d_all = sb("rs2d_all", [128, 16], F32)
        rs2_all = sb("rs2_all", [128, 16], F32)
        t11 = sb("t11", [1, 16], F32)

        pb = [ent(nc.psum_tensor(f"pb{i}", [128, 512], F32)) for i in range(6)]
        pA = ent(nc.psum_tensor("pA", [128, 132], F32))
        pT = ent(nc.psum_tensor("pT", [128, 128], BF16))

        blk = ent(nc.Block())

        def MM(ps_ap, lhsT, rhs, start, stop, waits=()):
            return S.emit("tensor",
                          lambda e, a=ps_ap, l=lhsT, r=rhs, s1=start, s2=stop:
                          e.matmul(a, l, r, start=s1, stop=s2),
                          waits=waits, inc="mm")

        def TRP(ps_ap, in_ap, id_ap, waits=()):
            return S.emit("tensor",
                          lambda e, a=ps_ap, i=in_ap, d=id_ap:
                          e.transpose(a, i, d), waits=waits, inc="mm")

        def EV(fn, waits=()):
            return S.emit("vector", fn, waits=waits, inc="ev")

        def ACT(fn, waits=()):
            return S.emit("scalar", fn, waits=waits, inc="ac")

        def GP(fn, waits=()):
            return S.emit("gpsimd", fn, waits=waits, inc="gp")

        def DRAIN(engine, sem):
            return S.emit(engine, lambda e: e.drain(), inc=sem)

        bank_free = {i: ("ev", 0) for i in list(range(6)) + ["pA", "pT"]}

        xfm = lambda k: S1[:, k * 2048:(k + 1) * 2048]
        Qf = lambda p: S2[:, p * 2048:(p + 1) * 2048]
        Kf = lambda p: S3[:, p * 2048:(p + 1) * 2048]
        Vt = lambda t: S4[:, t * 1024:(t + 1) * 1024]
        exp1 = lambda i: S1[:, i * 1024:(i + 1) * 1024]
        SVf = lambda p: S2[:, p * 2048:(p + 1) * 2048]
        wv16 = lambda k: S3[:, k * 1024:(k + 1) * 1024]

        S.dma("sync", c_id[:], ident_d)
        S.dma("sync", c_id3[:], id3_d)
        S.dma("sync", c_p1[:], pair1_d)
        S.dma("sync", c_bc[:], bcmask_d)
        S.dma("sync", c_or[:], onesr_d)
        S.dma("sync", c_o1b[:], o1b_d)
        S.dma("sync", c_o1f[:], o1f_d)
        d_consts = S.dma("sync", gidx[:], gidx_d)

        d_xfm = S.dma("gpsimd", S1[:].rearrange("p (c t) -> p c t", c=KC),
                     x_fmT.rearrange("(c p) t -> p c t", p=128))
        d_wv = S.dma("gpsimd", S3[:, 0:8192].rearrange("p (k q) -> p k q", k=KC),
                     wqkv[:, 2048:3072].rearrange("(k p) q -> p k q", p=128))

        for k in range(KC):
            for t in range(NT):
                ACT(lambda e, k=k, t=t: e.activation(
                    scB[:, 0:128], xfm(k)[:, t * 128:(t + 1) * 128], AF.Copy,
                    scale=1.0 / 128, accum_out=xlpf[:, k * 16 + t:k * 16 + t + 1]),
                    waits=[("dG", d_xfm)] if (k == 0 and t == 0)
                    else [("ac", S.c("ac"))])
        ac_xlp = S.c("ac")
        ev_xlp = EV(lambda e: e.tensor_copy(xlp[:], xlpf[:]),
                    waits=[("ac", ac_xlp)])
        w0, c0 = bank_free["pT"]
        m_t = TRP(pT[:], xlp[:], c_id[:],
                  waits=[("ev", ev_xlp), ("dS", d_consts), (w0, c0)])
        e_t = EV(lambda e: e.tensor_copy(xlt[:], pT[:]), waits=[("mm", m_t)])
        bank_free["pT"] = ("ev", e_t)
        d_ag = S.dma("sync", ag_in.rearrange("t (k p) -> k t p", p=128),
                     xlt[:], waits=[("ev", e_t)])
        S.emit("gpsimd",
               lambda e: e.collective_compute(
                   "AllGather", ALU.bypass, replica_groups=[list(range(NC))],
                   ins=[ag_in.opt()], outs=[ag_out.opt()]),
               waits=[("dS", d_ag)], inc="cc1")
        d_xlg = S.dma("sync", xl_g[:], ag_out, waits=[("cc1", 1)])

        for k in range(KC):
            w0, c0 = bank_free["pT"]
            m1 = TRP(pT[:], xl_g[:, k * 128:(k + 1) * 128], c_id[:],
                     waits=[("dS", d_xlg), (w0, c0)])
            e1 = EV(lambda e, k=k: e.tensor_copy(
                xl_fm[:, k * 128:(k + 1) * 128]
                .rearrange("p (b r s) -> p r b s", b=2, r=8, s=8),
                pT[:].rearrange("p (r b s) -> p r b s", r=8, b=2, s=8)),
                waits=[("mm", m1)])
            bank_free["pT"] = ("ev", e1)
        ev_xlfm = S.c("ev")

        def pass1_gen(wT, accname, pieces, buf_tensor, bufw, first_ev_wait,
                      out):
            prev_ev = {}
            acc = cols[accname]
            for i, (r0, cc, wdt) in enumerate(pieces):
                half = i % 2
                buf = buf_tensor[:, half * bufw:half * bufw + wdt]
                waits = []
                if i >= 2:
                    waits.append(("ev", prev_ev[i - 2]))
                elif first_ev_wait[0]:
                    waits.append(("ev", first_ev_wait[0]))
                d1 = S.dma("sync", buf, wT[r0:r0 + 128, cc:cc + wdt], waits=waits)
                gr = EV(lambda e, b=buf: e.tensor_reduce(
                    cols["tmp2"][:], b, AX.X, ALU.add, True),
                    waits=[("dS", d1)])
                if i == 0:
                    g2 = EV(lambda e, a=acc: e.tensor_copy(a[:], cols["tmp2"][:]),
                            waits=[("ev", gr)])
                else:
                    g2 = EV(lambda e, a=acc: e.tensor_add(a[:], a[:],
                                                          cols["tmp2"][:]),
                            waits=[("ev", gr)])
                prev_ev[i] = g2
                out[0] = g2
                yield

        p_gu = [(r * 128, ch * 2048, 2048) for r in range(8) for ch in range(2)]
        p_d = [(r * 128, 0, 1024) for r in range(32)]


        def ws_finalize(accname, numel, scol, dcol, slot, evwait):
            w5, c5 = bank_free[5]
            m1 = MM(pb[5][0:1, 0:1], cols[accname][:], c_o1f[:],
                    start=True, stop=True, waits=[("ev", evwait), (w5, c5)])
            e1 = EV(lambda e, s=slot: e.tensor_scalar(
                t11[0:1, s * 2:s * 2 + 1], pb[5][0:1, 0:1],
                1.0 / numel, 1e-5, ALU.mult, ALU.max), waits=[("mm", m1)])
            e2 = EV(lambda e, s=slot: e.reciprocal(
                t11[0:1, s * 2 + 1:s * 2 + 2], t11[0:1, s * 2:s * 2 + 1]),
                waits=[("ev", e1)])
            m2 = MM(pb[5][:, 0:1], c_or[:], t11[0:1, slot * 2:slot * 2 + 1],
                    start=True, stop=True, waits=[("ev", e2)])
            m3 = MM(pb[5][:, 1:2], c_or[:], t11[0:1, slot * 2 + 1:slot * 2 + 2],
                    start=True, stop=True)
            EV(lambda e, d=dcol: e.tensor_copy(cols[d][:], pb[5][:, 0:1]),
               waits=[("mm", m3)])
            e4 = EV(lambda e, s=scol: e.tensor_copy(cols[s][:], pb[5][:, 1:2]))
            bank_free[5] = ("ev", e4)
            return e4


        def pass2_gen(wT, scol, q_dr, pieces, wswait, buf_tensor, bufw,
                      first_gp_wait, out, first_ev_wait=0):
            prev = {}
            for i, (r0, cc, wdt) in enumerate(pieces):
                half = i % 2
                buf = buf_tensor[:, half * bufw:half * bufw + wdt]
                waits = [("ev", wswait[0])]
                if i >= 2:
                    waits.append(("ev", prev[i - 2][0]))
                elif first_gp_wait[0]:
                    waits.append(("ev", first_gp_wait[0]))
                if i < 2 and first_ev_wait:
                    waits.append(("ev", first_ev_wait))
                d1 = S.dma("sync", buf, wT[r0:r0 + 128, cc:cc + wdt], waits=waits)
                glast = None
                for q in range(wdt // 512):
                    qw = [("dS", d1)]
                    if i >= 1 and q == 0:
                        qw.append(("dS", prev[i - 1][1]))
                    g1 = EV(lambda e, b=buf, q=q, s=scol: e.tensor_scalar(
                        EI2[:], b[:, q * 512:(q + 1) * 512], cols[s][:], None,
                        ALU.mult), waits=qw)
                    glast = EV(lambda e, q=q: e.tensor_scalar(
                        scD[:, q * 512:(q + 1) * 512], EI2[:], 1.0, -1.0,
                        ALU.min, ALU.max), waits=[("ev", g1)])
                glast = DRAIN("vector", "ev")
                d2 = S.dma("sync", q_dr[r0:r0 + 128, cc:cc + wdt],
                           scD[:, 0:wdt], waits=[("ev", glast)])
                prev[i] = (glast, d2)
                out[0] = glast
                out[1] = d2
                yield

        p1g_out = [0]
        p1u_out = [0]
        p1d_out = [0]
        wsg_box = [0]
        wsu_box = [0]
        wsd_box = [0]
        p2g_out = [0, 0]
        p2u_out = [0, 0]
        p2d_out = [0, 0]
        p_g2 = [(r * 128, ch * 1024, 1024) for r in range(8) for ch in range(4)]

        def wq1_fn():
            yield from pass1_gen(wgT, "accg", p_gu, CF1, 2048, [0], p1g_out)
            yield from pass1_gen(wuT, "accu", p_gu, CF1, 2048, p1g_out, p1u_out)
            yield from pass1_gen(wdT, "accd", p_d, CF1, 2048, p1u_out, p1d_out)
            wsg_box[0] = ws_finalize("accg", D * DI, "wsgs", "wsgd", 0,
                                     p1g_out[0])
            wsu_box[0] = ws_finalize("accu", D * DI, "wsus", "wsud", 1,
                                     p1u_out[0])
            wsd_box[0] = ws_finalize("accd", D * DI, "wsds", "wsdd", 2,
                                     p1d_out[0])
            yield

        wq1 = wq1_fn()

        def wq1_pump(n):
            for _ in range(n):
                next(wq1, None)

        def wq2_fn():
            yield from pass2_gen(wgT, "wsgs", wgq_dr, p_g2, wsg_box, CFD,
                                 1024, [0], p2g_out)
            yield from pass2_gen(wuT, "wsus", wuq_dr, p_g2, wsu_box, CFD,
                                 1024, [0], p2u_out)

        wq2 = [None]

        def wq2_pump(n):
            if not wsd_box[0]:
                return
            if wq2[0] is None:
                wq2[0] = wq2_fn()
            for _ in range(n):
                next(wq2[0], None)

        for t in range(NT):
            for g2 in range(2):
                bank = 4 + (t * 2 + g2) % 2
                w0, c0 = bank_free[bank]
                for k in range(KC):
                    mmid = MM(pb[bank][:, 0:512], xfm(k)[:, t * 128:(t + 1) * 128],
                              wv16(k)[:, g2 * 512:(g2 + 1) * 512],
                              start=(k == 0), stop=(k == KC - 1),
                              waits=([("dG", max(d_wv, d_xfm)), (w0, c0)]
                                     if k == 0 else ()))
                ev1 = EV(lambda e, t=t, g2=g2, b=bank: e.tensor_copy(
                    Vt(t)[:, g2 * 512:(g2 + 1) * 512], pb[b][:, 0:512]),
                    waits=[("mm", mmid)])
                bank_free[bank] = ("ev", ev1)
                wq1_pump(1)
        mm_v_end = S.c("mm")
        ev_v_end = S.c("ev")

        mm_qlkl = {}
        kt_d = dt("dbg_k2trace", [128, 128], F32, "ExternalOutput") \
            if "k2trace" in dump else None
        d_trc = [0]
        k2_iter = [0]
        mm_bdq2 = {0: 0, 1: 0}
        d_k2_prev = 0
        ev_k2_prev = 0
        last_round_mm = mm_v_end
        for rnd in range(2):
            dwq = S.dma("gpsimd", scC[:, 0:4096].rearrange("p (k q) -> p k q", k=KC),
                        wqkv[:, rnd * 512:rnd * 512 + 512]
                        .rearrange("(k p) q -> p k q", p=128),
                        waits=[("mm", last_round_mm)])
            dwk = S.dma("gpsimd", scC[:, 4096:8192].rearrange("p (k q) -> p k q", k=KC),
                        wqkv[:, 1024 + rnd * 512:1024 + rnd * 512 + 512]
                        .rearrange("(k p) q -> p k q", p=128))
            for half in range(2):
                dwait = dwq if half == 0 else dwk
                for j in range(4):
                    p = rnd * 4 + j
                    wsl = scC[:, half * 4096 + j * 1024:half * 4096 + (j + 1) * 1024]
                    for g in range(4):
                        bank = g
                        w0, c0 = bank_free[bank]
                        for k in range(KC):
                            kw = []
                            if k == 0:
                                kw = [("dG", dwait), (w0, c0)]
                            mmid = MM(pb[bank][:, 0:512],
                                      wsl[:, k * 128:(k + 1) * 128],
                                      xfm(k)[:, g * 512:(g + 1) * 512],
                                      start=(k == 0), stop=(k == KC - 1),
                                      waits=kw)
                        dst = Qf(p) if half == 0 else Kf(p)
                        scale = 0.125 if half == 0 else 1.0
                        a1 = ACT(lambda e, d=dst, g=g, b=bank, s=scale:
                                 e.activation(d[:, g * 512:(g + 1) * 512],
                                              pb[b][:, 0:512], AF.Copy, scale=s),
                                 waits=[("mm", mmid)])
                        bank_free[bank] = ("ac", a1)
                    wpa, cpa = bank_free["pA"]
                    for k in range(KC):
                        mmid = MM(pA[:, 0:128], wsl[:, k * 128:(k + 1) * 128],
                                  xl_fm[:, k * 128:(k + 1) * 128],
                                  start=(k == 0), stop=(k == KC - 1),
                                  waits=([("ev", ev_xlfm), (wpa, cpa)]
                                         if k == 0 else ()))
                    dst2 = ql_fm if half == 0 else kl_fm
                    scale = 0.125 if half == 0 else 1.0
                    e1 = EV(lambda e, d=dst2, p=p, s=scale: e.tensor_scalar(
                        d[:, p * 128:(p + 1) * 128], pA[:, 0:128], s, None,
                        ALU.mult), waits=[("mm", mmid)])
                    bank_free["pA"] = ("ev", e1)
                    mm_qlkl[(half, p)] = e1
                    wq1_pump(2)
                last_round_mm = S.c("mm")

            for j in range(4):
                p = rnd * 4 + j
                for b in range(2):
                    idx = b * 8 + p
                    jb = k2_iter[0] % 2
                    k2_iter[0] += 1
                    wql = mm_qlkl[(0, p)]
                    wkl = mm_qlkl[(1, p)]
                    evm = EV(lambda e, jb=jb: e.memset(bdq[jb][:], 0.0),
                             waits=[("ev", max(wql, wkl)),
                                    ("mm", mm_bdq2[jb])])
                    EV(lambda e, jb=jb, p=p, b=b: e.tensor_copy(
                        bdq[jb][0:64, 0:64],
                        ql_fm[0:64, p * 128 + b * 64:p * 128 + b * 64 + 64]))
                    e2 = EV(lambda e, jb=jb, p=p, b=b: e.tensor_copy(
                        bdq[jb][64:128, 64:128],
                        ql_fm[64:128, p * 128 + b * 64:p * 128 + b * 64 + 64]))
                    bank = 4
                    w0, c0 = bank_free[bank]
                    m1 = MM(pb[bank][:, 0:64], bdq[jb][:],
                            kl_fm[:, p * 128 + b * 64:p * 128 + b * 64 + 64],
                            start=True, stop=True, waits=[("ev", e2), (w0, c0)])
                    a1 = ACT(lambda e, b2=bank: e.activation(
                        k2e[:], pb[b2][:, 0:64], AF.Exp,
                        accum_out=cols["rowr"][:]),
                        waits=[("mm", m1), ("ev", ev_k2_prev)])
                    bank_free[bank] = ("ac", a1)
                    er = EV(lambda e: e.reciprocal(cols["tmp"][:],
                                                   cols["rowr"][:]),
                            waits=[("ac", a1), ("dG", d_k2_prev)])
                    e4 = EV(lambda e: e.tensor_scalar(
                        k2s[:], k2e[:], cols["tmp"][:], None, ALU.mult),
                        waits=[("ev", er)])
                    if "k2trace" in dump:
                        EV(lambda e: e.tensor_copy(
                            azb[0][:, 0:1], cols["rowr"][:]),
                            waits=[("dA", d_trc[0])])
                        EV(lambda e: e.tensor_copy(
                            azb[0][:, 1:2], cols["tmp"][:]))
                        EV(lambda e: e.tensor_copy(azb[0][:, 2:4], k2e[:, 0:2]))
                        e4 = EV(lambda e: e.tensor_copy(
                            azb[0][:, 4:6], k2s[:, 0:2]))
                        d_trc[0] = S.dma("scalar",
                                         kt_d[:, idx * 8:idx * 8 + 8],
                                         azb[0][:, 0:8], waits=[("ev", e4)])
                    e4d = DRAIN("vector", "ev")
                    ev_k2_prev = e4d
                    mm_bdq2[jb] = S.c("mm")
                    d_k2_prev = S.dma("gpsimd", k2_dr[idx * 128:(idx + 1) * 128, :],
                                      k2s[:], waits=[("ev", e4d)])
        d_k2st_all = d_k2_prev
        mm_qkv_end = S.c("mm")
        ac_qkv_end = S.c("ac")

        d_gath = []
        for j in range(4):
            d_gath.append(S.emit(
                "gpsimd",
                lambda e, j=j: e.indirect_dma_start(
                    nw[j]["k2b"][:], None, k2_dr,
                    IndirectOffsetOnAxis(gidx[:, j:j + 1], 0)),
                waits=[("dG", d_k2st_all)], inc="dG", inc_amt=16))


        d_wpj = S.dma("gpsimd", scC[:].rearrange("p (k q) -> p k q", k=KC),
                      wproj.rearrange("(k p) q -> p k q", p=128),
                      waits=[("mm", mm_qkv_end)])


        newton_done_ev = [None]

        def newton_gen():
            for j in range(4):
                w0, c0 = bank_free["pT"]
                m1 = TRP(pT[0:64, 0:64], nw[j]["k2b"][:], c_id[0:64, 0:64],
                         waits=[("dG", d_gath[j]), (w0, c0)])
                e1 = EV(lambda e, j=j: e.tensor_copy(nw[j]["k2t"][:],
                                                     pT[0:64, 0:64]),
                        waits=[("mm", m1)])
                bank_free["pT"] = ("ev", e1)
                yield
            for j in range(4):
                bankid = 4 if j < 2 else 5
                co = 256 + (j % 2) * 128
                w0, c0 = bank_free[bankid]
                m1 = MM(pb[bankid][0:1, co:co + 64], c_o1b[0:64, :],
                        nw[j]["k2b"][:], start=True, stop=True,
                        waits=[("dG", d_gath[j]), (w0, c0)])
                e1 = EV(lambda e, b=bankid, co=co, j=j: e.reduce_max(
                    t11[0:1, 8 + j:9 + j], pb[b][0:1, co:co + 64], AX.X),
                    waits=[("mm", m1)])
                e2 = EV(lambda e, j=j: e.reciprocal(
                    t11[0:1, 12 + j:13 + j], t11[0:1, 8 + j:9 + j]),
                    waits=[("ev", e1)])
                m2 = MM(pb[bankid][0:64, co + 64:co + 65], c_or[0:1, 0:64],
                        t11[0:1, 12 + j:13 + j], start=True, stop=True,
                        waits=[("ev", e2)])
                e3d = EV(lambda e, j=j, b=bankid, co=co: e.tensor_copy(
                    dnc[j][:], pb[b][0:64, co + 64:co + 65]),
                    waits=[("mm", m2)])
                EV(lambda e, j=j: e.tensor_scalar(
                    nw[j]["vk"][:], nw[j]["k2t"][:], dnc[j][:], None, ALU.mult),
                    waits=[("ev", e3d)])
                e5 = EV(lambda e, j=j: e.tensor_scalar(
                    nw[j]["vt"][:], nw[j]["k2b"][:], dnc[j][:], None, ALU.mult))
                bank_free[bankid] = ("ev", e5)
                yield
            for it in range(6):
                for pair in range(2):
                    bankid = 4 + pair
                    w0, c0 = bank_free[bankid]
                    mm1 = 0
                    for jj in range(2):
                        j = pair * 2 + jj
                        co = jj * 128
                        MM(pb[bankid][0:64, co:co + 64], nw[j]["k2t"][:],
                           nw[j]["vk"][:], start=True, stop=True,
                           waits=[(w0, c0)] if jj == 0 else ())
                        mm1 = MM(pb[bankid][0:64, co + 64:co + 128],
                                 nw[j]["vk"][:], nw[j]["k2t"][:],
                                 start=True, stop=True)
                    ee = 0
                    for jj in range(2):
                        j = pair * 2 + jj
                        co = jj * 128
                        EV(lambda e, j=j, b=bankid, co=co: e.scalar_tensor_tensor(
                            nw[j]["tt"][:], pb[b][0:64, co:co + 64], -1.0,
                            c_id3[:, 0:64], ALU.mult, ALU.add),
                            waits=[("mm", mm1)])
                        ee = EV(lambda e, j=j, b=bankid, co=co: e.tensor_copy(
                            nw[j]["ptb"][:], pb[b][0:64, co + 64:co + 128]))
                    yield
                    mm2 = 0
                    for jj in range(2):
                        j = pair * 2 + jj
                        co = jj * 128
                        mm2 = MM(pb[bankid][0:64, co:co + 64], nw[j]["ptb"][:],
                                 nw[j]["tt"][:], start=True, stop=True,
                                 waits=[("ev", ee)] if jj == 0 else ())
                    for jj in range(2):
                        j = pair * 2 + jj
                        co = jj * 128
                        ee = EV(lambda e, j=j, b=bankid, co=co:
                                e.scalar_tensor_tensor(
                                    nw[j]["tt"][:], pb[b][0:64, co:co + 64],
                                    -1.0, c_id3[:, 64:128], ALU.mult, ALU.add),
                                waits=[("mm", mm2)])
                    yield
                    for jj in range(2):
                        j = pair * 2 + jj
                        co = jj * 128
                        mm2 = MM(pb[bankid][0:64, co:co + 64], nw[j]["ptb"][:],
                                 nw[j]["tt"][:], start=True, stop=True,
                                 waits=[("ev", ee)] if jj == 0 else ())
                    for jj in range(2):
                        j = pair * 2 + jj
                        co = jj * 128
                        ee = EV(lambda e, j=j, b=bankid, co=co:
                                e.scalar_tensor_tensor(
                                    nw[j]["tt"][:], pb[b][0:64, co:co + 64],
                                    -1.0, c_id3[:, 128:192], ALU.mult, ALU.add),
                                waits=[("mm", mm2)])
                    yield
                    for jj in range(2):
                        j = pair * 2 + jj
                        co = jj * 128
                        MM(pb[bankid][0:64, co:co + 64], nw[j]["vt"][:],
                           nw[j]["tt"][:], start=True, stop=True,
                           waits=[("ev", ee)] if jj == 0 else ())
                        mm2 = MM(pb[bankid][0:64, co + 64:co + 128],
                                 nw[j]["tt"][:], nw[j]["vt"][:],
                                 start=True, stop=True)
                    for jj in range(2):
                        j = pair * 2 + jj
                        co = jj * 128
                        EV(lambda e, j=j, b=bankid, co=co: e.tensor_scalar(
                            nw[j]["vk"][:], pb[b][0:64, co:co + 64], 0.25,
                            None, ALU.mult), waits=[("mm", mm2)])
                        ee = EV(lambda e, j=j, b=bankid, co=co: e.tensor_scalar(
                            nw[j]["vt"][:], pb[b][0:64, co + 64:co + 128],
                            0.25, None, ALU.mult))
                    bank_free[bankid] = ("ev", ee)
                    yield
            newton_done_ev[0] = S.c("ev")

        ngen = newton_gen()

        def pump(n):
            for _ in range(n):
                if newton_done_ev[0] is not None:
                    return
                next(ngen, None)

        def pump_all():
            while newton_done_ev[0] is None:
                if next(ngen, "X") == "X":
                    break

        mm_bdk = {0: 0, 1: 0}
        for idx in range(16):
            b, p = divmod(idx, 8)
            jb = idx % 2
            evm4 = EV(lambda e, jb=jb: e.memset(bdk[jb][:], 0.0),
                      waits=[("mm", mm_bdk[jb]), ("ev", mm_qlkl[(1, p)])])
            EV(lambda e, jb=jb, p=p, b=b: e.tensor_copy(
                bdk[jb][0:64, 0:64],
                kl_fm[0:64, p * 128 + b * 64:p * 128 + b * 64 + 64]))
            e2 = EV(lambda e, jb=jb, p=p, b=b: e.tensor_copy(
                bdk[jb][64:128, 64:128],
                kl_fm[64:128, p * 128 + b * 64:p * 128 + b * 64 + 64]))
            for g in range(2):
                bank = (idx * 2 + g) % 2
                w0, c0 = bank_free[bank]
                m1 = MM(pb[bank][:, 0:512], bdk[jb][:],
                        Qf(p)[:, b * 1024 + g * 512:b * 1024 + (g + 1) * 512],
                        start=True, stop=True,
                        waits=[("ev", e2), (w0, c0), ("ac", ac_qkv_end)])
                a1 = ACT(lambda e, idx=idx, g=g, b2=bank: e.activation(
                    exp1(idx)[:, g * 512:(g + 1) * 512], pb[b2][:, 0:512],
                    AF.Exp), waits=[("mm", m1), ("mm", mm_qkv_end)])
                bank_free[bank] = ("ac", a1)
            mm_bdk[jb] = S.c("mm")
            pump(2)
            wq1_pump(4)
        ac_exp1_end = S.c("ac")

        mm_bdq5 = {0: 0, 1: 0}
        mm_a = {0: 0, 1: 0}
        d_az = 0
        d_az_prev = {0: 0, 1: 0}
        for idx in range(16):
            b, p = divmod(idx, 8)
            jb = idx % 2
            evm5 = EV(lambda e, jb=jb: e.memset(bdq[jb][:], 0.0),
                      waits=[("mm", mm_bdq5[jb]), ("ev", mm_qlkl[(0, p)])])
            EV(lambda e, jb=jb, p=p, b=b: e.tensor_copy(
                bdq[jb][0:64, 0:64],
                ql_fm[0:64, p * 128 + b * 64:p * 128 + b * 64 + 64]))
            e2 = EV(lambda e, jb=jb, p=p, b=b: e.tensor_copy(
                bdq[jb][64:128, 64:128],
                ql_fm[64:128, p * 128 + b * 64:p * 128 + b * 64 + 64]))
            pa_key = "pA" if idx % 2 == 0 else 2
            pa_ap = pA if idx % 2 == 0 else pb[2]
            wA, cA = bank_free[pa_key]
            m2 = 0
            for quad in range(2):
                bank = quad % 2
                half = quad % 2
                w0, c0 = bank_free[bank]
                m1 = 0
                for q in range(4):
                    t8 = quad * 4 + q
                    kw = []
                    if q == 0:
                        kw = [("ev", e2), (w0, c0), ("ac", ac_qkv_end)]
                    m1 = MM(pb[bank][:, q * 128:(q + 1) * 128],
                            Kf(p)[:, b * 1024 + t8 * 128:b * 1024 + (t8 + 1) * 128],
                            bdq[jb][:], start=True, stop=True, waits=kw)
                a1 = ACT(lambda e, b2=bank, h=half: e.activation(
                    scB[:, h * 1024:h * 1024 + 512], pb[b2][:, 0:512], AF.Exp),
                    waits=[("mm", m1), ("mm", mm_a[half])])
                bank_free[bank] = ("ac", a1)
                for q in range(4):
                    t8 = quad * 4 + q
                    tglob = b * 8 + t8
                    kw = [("ac", a1)]
                    if t8 == 0:
                        kw += [(wA, cA), ("ev", ev_v_end)]
                    MM(pa_ap[:, 0:128],
                       scB[:, half * 1024 + q * 128:half * 1024 + (q + 1) * 128],
                       Vt(tglob)[:, p * 128:(p + 1) * 128],
                       start=(t8 == 0), stop=(t8 == 7), waits=kw)
                    m2 = MM(pa_ap[:, 128:129],
                            scB[:, half * 1024 + q * 128:half * 1024 + (q + 1) * 128],
                            c_o1b[:], start=False, stop=(t8 == 7))
                mm_a[half] = S.c("mm")
            mm_bdq5[jb] = S.c("mm")
            eaz = EV(lambda e, i=idx % 2, a=pa_ap: e.tensor_copy(
                azb[i][:, 0:129], a[:, 0:129]),
                     waits=[("mm", m2), ("dA", max(d_az_prev[idx % 2],
                                                   d_trc[0]))])
            bank_free[pa_key] = ("ev", eaz)
            d_az = S.dma("scalar", ar_in[idx * 128:(idx + 1) * 128, 0:129],
                         azb[idx % 2][:, 0:129], waits=[("ev", eaz)])
            d_az_prev[idx % 2] = d_az
            pump(3)
            wq1_pump(4)
            wq2_pump(2)
        pump_all()
        wq1_pump(10 ** 6)
        mm_p5_end = S.c("mm")

        d_w2 = 0
        for j in range(4):
            d_w2 = S.dma("scalar", w2_in[j * 64:(j + 1) * 64, :], nw[j]["vt"][:],
                         waits=[("ev", newton_done_ev[0])])
        S.emit("gpsimd",
               lambda e: e.collective_compute(
                   "AllReduce", ALU.add, replica_groups=[list(range(NC))],
                   ins=[ar_in.opt()], outs=[ar_out.opt()]),
               waits=[("dA", d_az)], inc="cc2")
        S.emit("gpsimd",
               lambda e: e.collective_compute(
                   "AllGather", ALU.bypass, replica_groups=[list(range(NC))],
                   ins=[w2_in.opt()], outs=[w2_out.opt()]),
               waits=[("dA", d_w2)], inc="cc3")

        p_u2 = [(r * 128, ch * 1024, 1024) for r in range(8) for ch in range(4)]


        ev_azread = {0: 0, 1: 0}
        mm_bdwread = {0: 0, 1: 0}
        ev_sv_final = 0
        mm_rz_prev = 0
        for idx in range(16):
            b, p = divmod(idx, 8)
            i2 = idx % 2
            r0 = (idx // 2) * 256 + (idx % 2) * 128
            d1 = S.dma("scalar", azb[i2][:, 0:132],
                       ar_out[idx * 128:(idx + 1) * 128, :],
                       waits=[("cc2", 1), ("ev", ev_azread[i2])])
            d2a = S.dma("scalar", bdw[i2][0:64, 0:64], w2_out[r0:r0 + 64, :],
                        waits=[("cc3", 1), ("mm", mm_bdwread[i2])])
            d2b = S.dma("scalar", bdw[i2][64:128, 64:128],
                        w2_out[r0 + 64:r0 + 128, :])
            if idx < 2:
                EV(lambda e, i2=i2: e.memset(bdw[i2][0:64, 64:128], 0.0))
                EV(lambda e, i2=i2: e.memset(bdw[i2][64:128, 0:64], 0.0))
                EV(lambda e, i2=i2: e.memset(bsm[i2][:], 0.0))
            e1 = EV(lambda e, i2=i2: e.reciprocal(cols["tmp"][:],
                                                  azb[i2][:, 128:129]),
                    waits=[("dA", d1)])
            e1b = EV(lambda e, i2=i2: e.tensor_scalar(
                k3v[i2][:], azb[i2][:, 0:128], cols["tmp"][:], None, ALU.mult),
                waits=[("ev", e1)])
            ev_azread[i2] = e1b
            bank = 2
            w0, c0 = bank_free[bank]
            m1 = MM(pb[bank][:, 0:128], bdw[i2][:], k3v[i2][:],
                    start=True, stop=True,
                    waits=[("ev", e1b), ("dA", d2b), (w0, c0),
                           ("mm", mm_bdwread[i2])])
            e3 = EV(lambda e, b2=bank, i2=i2: e.tensor_copy(
                bsm[i2][0:64, 0:64], pb[b2][0:64, 0:64]), waits=[("mm", m1)])
            e4 = EV(lambda e, b2=bank, i2=i2: e.tensor_copy(
                bsm[i2][64:128, 64:128], pb[b2][64:128, 64:128]))
            bank_free[bank] = ("ev", e4)
            for g in range(2):
                bank = 3
                w0, c0 = bank_free[bank]
                m2b = MM(pb[bank][0:2, 0:512], c_p1[:],
                         exp1(idx)[:, g * 512:(g + 1) * 512],
                         start=True, stop=True,
                         waits=[("ac", ac_exp1_end), (w0, c0)])
                e5 = EV(lambda e, b2=bank: e.tensor_copy(zc[:],
                                                         pb[b2][0:2, 0:512]),
                        waits=[("mm", m2b)] +
                              ([("ev", p1d_out[0])] if idx == 0 and g == 0 else []))
                e6 = EV(lambda e: e.reciprocal(rz[:], zc[:]),
                        waits=[("mm", mm_rz_prev), ("ev", e5)])
                w4, c4 = bank_free[4]
                m3 = MM(pb[4][:, 0:512], c_bc[:], rz[:],
                        start=True, stop=True, waits=[("ev", e6), (w4, c4)])
                mm_rz_prev = m3
                e7 = EV(lambda e: e.tensor_copy(bcs[:], pb[4][:, 0:512]),
                        waits=[("mm", m3)])
                bank_free[4] = ("ev", e7)
                bank_free[bank] = ("ev", e5)
                bank = (idx * 2 + g) % 2
                w0, c0 = bank_free[bank]
                m4 = MM(pb[bank][:, 0:512], bsm[i2][:],
                        exp1(idx)[:, g * 512:(g + 1) * 512],
                        start=True, stop=True, waits=[("ev", e4), (w0, c0)])
                e8 = EV(lambda e, p=p, b=b, g=g, b2=bank: e.tensor_tensor(
                    SVf(p)[:, b * 1024 + g * 512:b * 1024 + (g + 1) * 512],
                    pb[b2][:, 0:512], bcs[:], ALU.mult),
                    waits=[("mm", m4), ("ev", e7)])
                bank_free[bank] = ("ev", e8)
            mm_bdwread[i2] = S.c("mm")
            ev_sv_final = S.c("ev")
            wq2_pump(4)
        mm_g_end = S.c("mm")

        d_wgq1 = S.dma("sync", S3[:].rearrange("p (k i) -> p k i", k=4),
                       wgq_dr[0:512, :].rearrange("(k p) i -> p k i", p=128),
                       waits=[("dS", p2g_out[1]), ("mm", mm_p5_end)])
        S.dma("sync", S4[:].rearrange("p (k i) -> p k i", k=4),
              wgq_dr[512:1024, :].rearrange("(k p) i -> p k i", p=128))
        S.dma("sync", S1[:].rearrange("p (k i) -> p k i", k=4),
              wuq_dr[0:512, :].rearrange("(k p) i -> p k i", p=128),
              waits=[("dS", p2u_out[1]), ("mm", mm_g_end)])

        def wgq(k):
            if k < 4:
                return S3[:, k * 4096:(k + 1) * 4096]
            return S4[:, (k - 4) * 4096:(k - 3) * 4096]

        def wuq(k):
            if k < 4:
                return S1[:, k * 4096:(k + 1) * 4096]
            return S2[:, (k - 4) * 4096:(k - 3) * 4096]

        d_xres = {}
        ev_xread = {0: 0, 1: 0}
        ev_v1_free = 0
        ev_y1_free = 0
        d_y1_prev = 0
        ac_sq_prev = 0
        gp_ei_reader = 0
        d_y1q_prev = 0
        d_y1q_slot = {0: 0, 1: 0}
        d_y1q_all = 0
        for t in range(NT):
            bankA = (t % 2) * 2
            bankB = bankA + 1
            xw = [("ev", ev_xread[t % 2])]
            if t == 0:
                xw.append(("ev", p1d_out[0]))
                xw.append(("mm", S.c("mm")))
            d_xres[t] = S.dma("scalar",
                              CF1[:, (t % 2) * 1024:(t % 2) * 1024 + 1024],
                              x_tm[t * 128:(t + 1) * 128, :], waits=xw)
            e1 = 0
            for og in range(2):
                bank = bankA if og == 0 else bankB
                w0, c0 = bank_free[bank]
                for k in range(KC):
                    kw = []
                    if k == 0:
                        kw = [("dG", d_wpj), (w0, c0), ("ev", ev_sv_final)]
                    mmid = MM(pb[bank][:, 0:512],
                              SVf(k)[:, t * 128:(t + 1) * 128],
                              scC[:, k * 1024 + og * 512:k * 1024 + (og + 1) * 512],
                              start=(k == 0), stop=(k == KC - 1), waits=kw)
                e1 = EV(lambda e, t=t, og=og, b2=bank: e.tensor_add(
                    CF1[:, 2048 + og * 512:2048 + (og + 1) * 512],
                    pb[b2][:, 0:512],
                    CF1[:, (t % 2) * 1024 + og * 512:
                        (t % 2) * 1024 + (og + 1) * 512]),
                    waits=[("mm", mmid), ("dA", d_xres[t]), ("ev", ev_v1_free)])
                bank_free[bank] = ("ev", e1)
            ev_xread[t % 2] = e1
            es = EV(lambda e: e.reduce_sum(cols["sum"][:], CF1[:, 2048:3072],
                                           AX.X))
            e2 = EV(lambda e: e.tensor_scalar(
                cols["negm"][:], cols["sum"][:], -1.0 / D, None, ALU.mult),
                waits=[("ev", es)])
            w5, c5 = bank_free[5]
            a1 = ACT(lambda e: e.activation(
                pb[5][:, 0:512], CF1[:, 2048:2560], AF.Square,
                bias=cols["negm"][:], accum_out=cols["ssq"][:]),
                waits=[("ev", e2), (w5, c5), ("ac", ac_sq_prev)])
            a2 = ACT(lambda e: e.activation(
                pb[5][:, 0:512], CF1[:, 2560:3072], AF.Square,
                bias=cols["negm"][:], accum_out=cols["ssqb"][:]))
            bank_free[5] = ("ac", a2)
            ac_sq_prev = a2
            e3 = EV(lambda e: e.tensor_add(cols["ssq"][:], cols["ssq"][:],
                                           cols["ssqb"][:]),
                    waits=[("ac", a2)])
            e3b = EV(lambda e: e.tensor_scalar(
                cols["ssq"][:], cols["ssq"][:], 1.0 / D, 1e-5,
                ALU.mult, ALU.add), waits=[("ev", e3)])
            a3 = ACT(lambda e: e.activation(cols["sd"][:], cols["ssq"][:],
                                            AF.Sqrt),
                     waits=[("ev", e3b)])
            e4 = EV(lambda e: e.reciprocal(cols["rstd"][:], cols["sd"][:]),
                    waits=[("ac", a3)])
            e5 = EV(lambda e: e.tensor_scalar(
                CF1[:, 3072:4096], CF1[:, 2048:3072], cols["negm"][:],
                cols["rstd"][:], ALU.add, ALU.mult),
                waits=[("ev", e4), ("dA", d_y1q_prev)])
            ev_v1_free = e5
            d_y1_prev = S.dma("scalar", y1_dr[t * 128:(t + 1) * 128, :],
                              CF1[:, 3072:4096], waits=[("ev", e5)])
            e6 = EV(lambda e: e.reduce_max(cols["rmax"][:], CF1[:, 3072:4096],
                                           AX.X, apply_absolute_value=True))
            e7 = EV(lambda e, t=t: e.tensor_scalar(
                rsa_all[:, t:t + 1], cols["rmax"][:], 1e-5, 1.0 / 127,
                ALU.max, ALU.mult), waits=[("ev", e6)])
            e8 = EV(lambda e, t=t: e.reciprocal(cols["sa"][:],
                                                rsa_all[:, t:t + 1]),
                    waits=[("ev", e7)])
            ysl = (t % 2) * 1024
            for hh in range(2):
                EV(lambda e, hh=hh: e.tensor_scalar(
                    EI[:], CF1[:, 3072 + hh * 512:3072 + (hh + 1) * 512],
                    cols["sa"][:], None, ALU.mult),
                    waits=[("ev", e8)])
                EV(lambda e, hh=hh, ysl=ysl: e.tensor_copy(
                    scB[:, ysl + hh * 512:ysl + (hh + 1) * 512], EI[:]),
                    waits=[("dA", d_y1q_slot[t % 2])])
            evd = DRAIN("vector", "ev")
            ev_y1_free = S.c("ev")
            d_y1q_prev = S.dma("scalar", y1q_dr[t * 128:(t + 1) * 128, :],
                               scB[:, ysl:ysl + 1024], waits=[("ev", evd)])
            d_y1q_slot[t % 2] = d_y1q_prev
            d_y1q_all = d_y1q_prev
            wq2_pump(4)
        wq2_pump(10 ** 6)
        mm_proj_end = S.c("mm")

        d_st_prev = {0: 0, 1: 0}
        d_y1f_prev = 0
        for cch in range(KC):
            buf = scE[:] if cch % 2 == 0 else scC[:, 0:2048]
            wv_ = [("dA", d_y1q_all)]
            if cch % 2 == 1 and cch == 1:
                wv_.append(("mm", mm_proj_end))
            if d_st_prev[cch % 2]:
                wv_.append(("dS", d_st_prev[cch % 2]))
            dtr = S.dmat("sync", buf, y1q_dr[:, cch * 128:(cch + 1) * 128],
                         waits=wv_)
            d_st_prev[cch % 2] = S.dma(
                "sync", y1qfm_dr[cch * 128:(cch + 1) * 128, :],
                buf, waits=[("dS", dtr)])
        d_y1qfm_all = d_st_prev[1]

        wq3 = [None]

        def wq3_pump(n):
            if wq3[0] is None:
                wq3[0] = pass2_gen(wdT, "wsds", wdq_dr, p_d, wsd_box, CFD,
                                   1024, p2u_out, p2d_out)
            for _ in range(n):
                next(wq3[0], None)


        d_wq_all = S.dma("sync", S2[:].rearrange("p (k i) -> p k i", k=4),
                         wuq_dr[512:1024, :].rearrange("(k p) i -> p k i",
                                                       p=128),
                         waits=[("mm", mm_proj_end)])

        ev_dqg = EV(lambda e: e.tensor_scalar(
            rsa2_all[:], rsa_all[:], cols["wsgd"][:], None, ALU.mult))
        ev_dqu = EV(lambda e: e.tensor_scalar(
            rsa3_all[:], rsa_all[:], cols["wsud"][:], None, ALU.mult))

        f1t_d = dt("dbg_f1trace", [128, 4096], F32, "ExternalOutput") \
            if "f1trace" in dump else None
        mm_f1_tile = {0: 0, 1: 0}
        ac_f1_prev = [0]
        eh_slot = {0: 0, 1: 0, 2: 0}
        gp_hq_prev = 0
        gp_ei_last = [0]
        d_hq_slot = {0: 0, 1: 0}
        d_hq_all = 0
        ev_h_last = 0
        for t in range(NT):
            d_y1f = S.dma("scalar", scE[:, (t % 2) * 1024:(t % 2) * 1024 + 1024]
                          .rearrange("p (k q) -> p k q", k=KC),
                          y1qfm_dr[:, t * 128:(t + 1) * 128]
                          .rearrange("(k p) q -> p k q", p=128),
                          waits=([("dS", d_y1qfm_all)] if t == 0 else []) +
                                [("mm", mm_f1_tile[t % 2])])

            def stat(k, t=t):
                return scE[:, (t % 2) * 1024 + k * 128:
                           (t % 2) * 1024 + (k + 1) * 128]


            for ig in range(8):
                bg = (ig % 3) * 2
                bu = bg + 1
                gslot = ig % 3
                gbuf = (scB[:, gslot * 512:(gslot + 1) * 512] if gslot < 2
                        else scC[:, 0:512])
                wg0, cg0 = bank_free[bg]
                mg = 0
                for k in range(KC):
                    kw = []
                    if k == 0:
                        kw = [("dA", d_y1f), ("dS", d_wq_all), (wg0, cg0)]
                    mg = MM(pb[bg][:, 0:512], stat(k),
                            wgq(k)[:, ig * 512:(ig + 1) * 512],
                            start=(k == 0), stop=(k == KC - 1), waits=kw)
                wu0, cu0 = bank_free[bu]
                mu = 0
                for k in range(KC):
                    kw = [(wu0, cu0)] if k == 0 else []
                    mu = MM(pb[bu][:, 0:512], stat(k),
                            wuq(k)[:, ig * 512:(ig + 1) * 512],
                            start=(k == 0), stop=(k == KC - 1), waits=kw)
                a1 = ACT(lambda e, b2=bg, gb=gbuf, t=t: e.activation(
                    gb, pb[b2][:, 0:512],
                    AF.Silu, scale=rsa2_all[:, t:t + 1]),
                    waits=[("ev", ev_dqg), ("mm", mg), ("ev", eh_slot[gslot]),
                           ("dA", d_y1q_all if t == 0 else 0),
                           ("dS", d_y1qfm_all if t == 0 else 0)])
                bank_free[bg] = ("ac", a1)
                eh = EV(lambda e, b2=bu, ig=ig, gb=gbuf, t=t:
                        e.scalar_tensor_tensor(
                            CF1[:, ig * 512:(ig + 1) * 512], pb[b2][:, 0:512],
                            rsa3_all[:, t:t + 1],
                            gb, ALU.mult,
                            ALU.mult),
                        waits=[("ac", a1), ("ev", ev_dqu), ("mm", mu)])
                bank_free[bu] = ("ev", eh)
                eh_slot[gslot] = eh
            if "f1trace" in dump and t == 0:
                ehd0 = DRAIN("vector", "ev")
                d_f1t = S.dma("sync", f1t_d, CF1[:, 0:4096],
                              waits=[("ev", ehd0)])
                e_g = EV(lambda e: e.tensor_copy(azb[0][:, 0:1],
                                                 cols["dqg"][:]),
                         waits=[("dS", d_f1t)])
            ehd = DRAIN("vector", "ev")
            g1 = EV(lambda e: e.tensor_reduce(
                cols["tmp2"][:], CF1[:], AX.X, ALU.max, True),
                waits=[("ev", ehd)])
            e10 = EV(lambda e, t=t: e.tensor_scalar(
                rs2_all[:, t:t + 1], cols["tmp2"][:], 1e-5, 1.0 / 127,
                ALU.max, ALU.mult), waits=[("ev", g1)])
            e11 = EV(lambda e, t=t: e.reciprocal(cols["s2"][:],
                                                 rs2_all[:, t:t + 1]),
                     waits=[("ev", e10)])
            for q in range(8):
                slot = (q // 2) % 2
                hbuf = scB[:, 1024:2048] if slot == 0 else scC[:, 1024:2048]
                EV(lambda e, q=q: e.tensor_scalar(
                    EI[:], CF1[:, q * 512:(q + 1) * 512], cols["s2"][:], None,
                    ALU.mult), waits=[("ev", e11)])
                EV(lambda e, q=q, hb=hbuf: e.tensor_copy(
                    hb[:, (q % 2) * 512:(q % 2 + 1) * 512], EI[:]),
                    waits=[("dS", d_hq_slot[slot])] +
                          ([("dS", d_y1qfm_all)]
                           if t == 0 and q == 2 else []) +
                          ([("dA", d_y1q_all)]
                           if t == 0 and q == 0 else []))
                if q % 2 == 1:
                    evd2 = DRAIN("vector", "ev")
                    d_hq = S.dma("sync",
                                 hq_dr[t * 128:(t + 1) * 128,
                                       (q - 1) * 512:(q + 1) * 512],
                                 hbuf, waits=[("ev", evd2)])
                    d_hq_slot[slot] = d_hq
                    d_hq_all = d_hq
            ev_h_last = S.c("ev")
            mm_f1_tile[t % 2] = S.c("mm")
            ac_f1_prev[0] = S.c("ac")
            wq3_pump(4)
        wq3_pump(10 ** 6)
        mm_f1_end = S.c("mm")
        ac_f1_end = S.c("ac")

        ev_dqd = EV(lambda e: e.tensor_scalar(
            rs2d_all[:], rs2_all[:], cols["wsdd"][:], None, ALU.mult))
        d_wdc = []
        for ic in range(32):
            dst = (S2[:, ic * 1024:(ic + 1) * 1024] if ic < 16 else
                   S1[:, (ic - 16) * 1024:(ic - 15) * 1024])
            kw = ([("dS", p2d_out[1]), ("mm", mm_f1_end)] if ic == 0 else [])
            d_wdc.append(S.dma("gpsimd", dst,
                               wdq_dr[ic * 128:(ic + 1) * 128, :], waits=kw))

        def wdq(ic):
            if ic < 16:
                return S2[:, ic * 1024:(ic + 1) * 1024]
            return S1[:, (ic - 16) * 1024:(ic - 15) * 1024]

        ev_v2_free = 0
        d_out_prev = 0
        ac_sq2_prev = 0
        mm_tb_read = {0: mm_f1_end, 1: mm_f1_end}
        ev_y1r_free = {0: 0, 1: 0}
        for tb in range(4):
            slab = S3 if tb % 2 == 0 else S4
            d_hqf = 0
            for ic in range(ICN):
                kw = []
                if ic == 0:
                    kw = [("dS", d_hq_all), ("mm", mm_tb_read[tb % 2])]
                d_hqf = S.dmat("sync", slab[:, ic * 512:(ic + 1) * 512],
                               hq_dr[tb * 512:(tb + 1) * 512,
                                     ic * 128:(ic + 1) * 128], waits=kw)
            for t4 in range(4):
                t = tb * 4 + t4
                yw = [("ev", ev_y1r_free[t4 % 2])]
                if tb == 0 and t4 == 0:
                    yw += [("ev", ev_h_last)]
                d_y1r = S.dma("scalar",
                              CF1[:, (t4 % 2) * 1024:(t4 % 2) * 1024 + 1024],
                              y1_dr[t * 128:(t + 1) * 128, :], waits=yw)

                e1 = 0
                for og in range(2):
                    bank = t4 % 2 + og * 2
                    w0, c0 = bank_free[bank]
                    mmid = 0
                    for ic in range(ICN):
                        kw = []
                        if ic == 0:
                            kw = [("dS", d_hqf), (w0, c0)]
                        if tb == 0 and t4 == 0 and og == 0:
                            kw.append(("dG", d_wdc[ic]))
                        mmid = MM(pb[bank][:, 0:512],
                                  slab[:, ic * 512 + t4 * 128:
                                       ic * 512 + (t4 + 1) * 128],
                                  wdq(ic)[:, og * 512:(og + 1) * 512],
                                  start=(ic == 0), stop=(ic == ICN - 1),
                                  waits=kw)
                    e1 = EV(lambda e, og=og, t4=t4, b2=bank, t=t:
                            e.scalar_tensor_tensor(
                                CF1[:, 2048 + og * 512:2048 + (og + 1) * 512],
                                pb[b2][:, 0:512], rs2d_all[:, t:t + 1],
                                CF1[:, (t4 % 2) * 1024 + og * 512:
                                    (t4 % 2) * 1024 + (og + 1) * 512],
                                ALU.mult, ALU.add),
                            waits=[("mm", mmid), ("ev", ev_dqd), ("dA", d_y1r),
                                   ("ev", ev_v2_free)])
                    bank_free[bank] = ("ev", e1)
                ev_y1r_free[t4 % 2] = e1
                es = EV(lambda e: e.reduce_sum(cols["sum"][:],
                                               CF1[:, 2048:3072], AX.X))
                e2 = EV(lambda e: e.tensor_scalar(
                    cols["negm"][:], cols["sum"][:], -1.0 / D, None, ALU.mult),
                    waits=[("ev", es)])
                w5, c5 = bank_free[5]
                aw = [("ev", e2), (w5, c5), ("ac", ac_sq2_prev)]
                if tb == 0 and t4 == 0:
                    aw.append(("ac", ac_f1_end))
                a1 = ACT(lambda e: e.activation(
                    pb[5][:, 0:512], CF1[:, 2048:2560], AF.Square,
                    bias=cols["negm"][:], accum_out=cols["ssq"][:]), waits=aw)
                a2 = ACT(lambda e: e.activation(
                    pb[5][:, 0:512], CF1[:, 2560:3072], AF.Square,
                    bias=cols["negm"][:], accum_out=cols["ssqb"][:]))
                bank_free[5] = ("ac", a2)
                ac_sq2_prev = a2
                e3 = EV(lambda e: e.tensor_add(cols["ssq"][:], cols["ssq"][:],
                                               cols["ssqb"][:]),
                        waits=[("ac", a2)])
                e3b = EV(lambda e: e.tensor_scalar(
                    cols["ssq"][:], cols["ssq"][:], 1.0 / D, 1e-5,
                    ALU.mult, ALU.add), waits=[("ev", e3)])
                a3 = ACT(lambda e: e.activation(cols["sd"][:], cols["ssq"][:],
                                                AF.Sqrt),
                         waits=[("ev", e3b)])
                e4r = EV(lambda e: e.reciprocal(cols["rstd"][:],
                                                cols["sd"][:]),
                         waits=[("ac", a3)])
                e5 = EV(lambda e: e.tensor_scalar(
                    CF1[:, 3072:4096], CF1[:, 2048:3072], cols["negm"][:],
                    cols["rstd"][:], ALU.add, ALU.mult),
                    waits=[("ev", e4r), ("dA", d_out_prev)])
                ev_v2_free = e5
                d_out_prev = S.dma("scalar", out_d[t * 128:(t + 1) * 128, :],
                                   CF1[:, 3072:4096], waits=[("ev", e5)])
            mm_tb_read[tb % 2] = S.c("mm")

        if "k2probe" in dump:
            kp = dt("dbg_k2probe", [128, 132], F32, "ExternalOutput")
            wfin = [("dA", S.c("dA")), ("dS", S.c("dS")), ("dG", S.c("dG")),
                    ("ev", S.c("ev")), ("ac", S.c("ac")), ("gp", S.c("gp")),
                    ("mm", S.c("mm"))]
            e_kp = EV(lambda e: e.tensor_copy(CF1[:, 0:1], cols["rowr"][:]),
                      waits=wfin)
            EV(lambda e: e.tensor_copy(CF1[:, 1:2], cols["tmp"][:]))
            EV(lambda e: e.tensor_copy(CF1[:, 2:66], k2e[:]))
            EV(lambda e: e.tensor_copy(CF1[:, 66:130], k2s[:]))
            EV(lambda e: e.tensor_copy(CF1[:, 130:131], cols["wsgs"][:]))
            e_kp2 = EV(lambda e: e.tensor_copy(CF1[:, 131:132], cols["wsgd"][:]))
            S.dma("sync", kp[:, 0:132], CF1[:, 0:132], waits=[("ev", e_kp2)])

        dumpable = {"y1_dr": y1_dr, "y1q_dr": y1q_dr, "y1qfm_dr": y1qfm_dr,
                    "hq_dr": hq_dr, "wgq_dr": wgq_dr, "wuq_dr": wuq_dr,
                    "wdq_dr": wdq_dr, "k2_dr": k2_dr, "ag_out": ag_out,
                    "ar_in": ar_in, "ar_out": ar_out, "w2_out": w2_out}
        shapes = {"y1_dr": ([TL, D], F32), "y1q_dr": ([TL, D], BF16),
                  "y1qfm_dr": ([D, TL], BF16), "hq_dr": ([TL, DI], BF16),
                  "wgq_dr": ([D, DI], BF16), "wuq_dr": ([D, DI], BF16),
                  "wdq_dr": ([DI, D], BF16), "k2_dr": ([2048, 64], BF16),
                  "ag_out": ([128, D], BF16), "ar_in": ([2048, 132], F32),
                  "ar_out": ([2048, 132], F32), "w2_out": ([2048, 64], BF16)}
        for nmd in dump:
            if nmd in ("k2probe", "k2trace", "f1trace"):
                continue
            sh, dty = shapes[nmd]
            od = dt("dbg_" + nmd, sh, dty, "ExternalOutput")
            S.dma("sync", od, dumpable[nmd],
                  waits=[("dA", S.c("dA")), ("dS", S.c("dS")),
                         ("dG", S.c("dG")), ("ev", S.c("ev")),
                         ("ac", S.c("ac")), ("gp", S.c("gp")),
                         ("mm", S.c("mm")), ("cc2", 1), ("cc3", 1)])

        S.emit("sync", lambda e: e.nop(),
               waits=[("dS", S.c("dS")), ("dA", S.c("dA")), ("dG", S.c("dG")),
                      ("ev", S.c("ev")), ("ac", S.c("ac")), ("gp", S.c("gp")),
                      ("mm", S.c("mm"))])

        S.build(blk, sems)
    return nc


_NC_CACHE = [None]


def _consts():
    c = {}
    c["ident"] = np.eye(128).astype(ml_dtypes.bfloat16)
    id3 = np.zeros((64, 192), np.float32)
    id3[:, 0:64] = 7.0 * np.eye(64)
    id3[:, 64:128] = 15.0 * np.eye(64)
    id3[:, 128:192] = 13.0 * np.eye(64)
    c["id3"] = id3
    p1 = np.zeros((128, 2), np.float32)
    p1[0:64, 0] = 1.0
    p1[64:128, 1] = 1.0
    c["pair1"] = p1.astype(ml_dtypes.bfloat16)
    bcm = np.zeros((2, 128), np.float32)
    bcm[0, 0:64] = 1.0
    bcm[1, 64:128] = 1.0
    c["bcmask"] = bcm
    c["onesr"] = np.ones((1, 128), np.float32)
    c["o1b"] = np.ones((128, 1), np.float32).astype(ml_dtypes.bfloat16)
    c["o1f"] = np.ones((128, 1), np.float32)
    return c


def kernel(L0_attn_x, L0_attn_w_qkv, L0_attn_w_proj,
           L0_ffn_w_gate, L0_ffn_w_up, L0_ffn_w_down, _dump=(), _raw=False):
    x = np.ascontiguousarray(np.asarray(L0_attn_x, np.float32))
    wqkv = np.ascontiguousarray(np.asarray(L0_attn_w_qkv, np.float32))
    wpj = np.ascontiguousarray(np.asarray(L0_attn_w_proj, np.float32))
    wgT = np.ascontiguousarray(np.asarray(L0_ffn_w_gate, np.float32).T)
    wuT = np.ascontiguousarray(np.asarray(L0_ffn_w_up, np.float32).T)
    wdT = np.ascontiguousarray(np.asarray(L0_ffn_w_down, np.float32).T)

    if _dump:
        nc = build_program(dump=_dump)
    else:
        if _NC_CACHE[0] is None:
            _NC_CACHE[0] = build_program()
        nc = _NC_CACHE[0]

    cs = _consts()
    in_maps = []
    for c in range(NC):
        xs = np.ascontiguousarray(x[:, c * NL:(c + 1) * NL, :]).reshape(TL, D)
        gi = np.zeros((64, 4), np.int32)
        for j in range(4):
            jj, h = divmod(j, 2)
            idx = 2 * c + jj
            gi[:, j] = idx * 128 + h * 64 + np.arange(64)
        m = {
            "x_tm": xs,
            "x_fmT": np.ascontiguousarray(xs.T),
            "wqkv": wqkv, "wproj": wpj,
            "wgT": wgT, "wuT": wuT, "wdT": wdT,
            "gidx": gi,
        }
        m.update(cs)
        in_maps.append(m)

    res = run_bass_kernel_spmd(nc, in_maps, core_ids=list(range(NC)))
    if _raw:
        return res
    full = np.empty((B, N, D), np.float32)
    for c in range(NC):
        r = np.asarray(res.results[c]["out"], np.float32)
        full[:, c * NL:(c + 1) * NL, :] = r.reshape(B, NL, D)
    return full
